# version 29
# baseline (speedup 1.0000x reference)
import numpy as np
import concourse.bass as bass
import concourse.bacc as bacc
import concourse.mybir as mybir
from concourse.bass import IndirectOffsetOnAxis, ds
from concourse.bass_utils import run_bass_kernel_spmd
from contextlib import ExitStack

F32 = mybir.dt.float32
BF16 = mybir.dt.bfloat16
I32 = mybir.dt.int32
U32 = mybir.dt.uint32
AF = mybir.ActivationFunctionType
ALU = mybir.AluOpType
AX = mybir.AxisListType

B, T, C = 4, 4096, 1024
NH, HD = 4, 256
T_OWN, HALO = 2048, 512
T_KV = T_OWN + HALO
NT = T_OWN // 128
NKV = T_KV // 128
NCH = C // 128
E, E_LOC, CAP = 8, 4, 512
H_FFN = 4096
NEG = -1.0e30
ENGS = ("sync", "vector", "scalar", "tensor", "gpsimd")


class Prog:
    def __init__(self, nc):
        self.nc = nc
        self.ops = []
        self.phases = []
        self._cur = []
        self.dma_counts = {}
        self._pending_fence = {}

    def add(self, engine, fn, reads=(), writes=(), dma=None):
        if dma is not None:
            dma = f"{dma}_{engine}"
        writes = tuple(writes) + tuple(self._pending_fence.pop(engine, ()))
        op = dict(engine=engine, fn=fn, reads=tuple(reads), writes=tuple(writes),
                  dma=dma, idx=len(self.ops))
        if dma is not None:
            self.dma_counts[dma] = self.dma_counts.get(dma, 0) + 1
            op["dma_count"] = self.dma_counts[dma]
        self.ops.append(op)
        self._cur.append(op)
        return op

    def dma(self, engine, out, in_, tag, reads=(), writes=()):
        self.add(engine, lambda e, o=out, i=in_: e.dma_start(out=o, in_=i),
                 reads=reads, writes=writes, dma=tag)

    def fence(self, engine, regions):
        self._pending_fence[engine] = tuple(self._pending_fence.get(engine, ())) + tuple(regions)

    def barrier(self):
        if self._cur:
            self.phases.append(("ops", self._cur))
            self._cur = []
        self.phases.append(("barrier", None))

    def emit(self):
        nc = self.nc
        if self._cur:
            self.phases.append(("ops", self._cur))
            self._cur = []
        last_writer, readers = {}, {}
        for op in self.ops:
            deps = set()
            for r in op["reads"]:
                w = last_writer.get(r)
                if w is not None:
                    deps.add(w["idx"])
                readers.setdefault(r, []).append(op)
            for wr in op["writes"]:
                w = last_writer.get(wr)
                if w is not None:
                    deps.add(w["idx"])
                for rd in readers.get(wr, ()):
                    deps.add(rd["idx"])
                readers[wr] = []
                last_writer[wr] = op
            deps.discard(op["idx"])
            pruned = []
            for d in deps:
                pd = self.ops[d]
                if (pd["dma"] is None and op["dma"] is None
                        and pd["engine"] == "tensor" and op["engine"] == "tensor"):
                    continue
                pruned.append(d)
            best = {}
            for d in pruned:
                pd = self.ops[d]
                key = ("d", pd["dma"]) if pd["dma"] is not None else ("e", pd["engine"])
                if key not in best or d > best[key]:
                    best[key] = d
            op["deps"] = sorted(best.values())
        need_inc = set()
        for op in self.ops:
            for d in op["deps"]:
                if self.ops[d]["dma"] is None:
                    need_inc.add(d)
        cnt = {e: 0 for e in ENGS}
        for op in self.ops:
            if op["dma"] is None:
                op["do_inc"] = op["idx"] in need_inc
                if op["do_inc"]:
                    cnt[op["engine"]] += 1
                op["inc_val"] = cnt[op["engine"]]
        esem = {e: nc.alloc_semaphore(f"es_{e}") for e in ENGS}
        dsem = {t: nc.alloc_semaphore(f"ds_{t}") for t in self.dma_counts}
        self.esem, self.dsem = esem, dsem
        for op in self.ops:
            waits = {}
            for d in op["deps"]:
                pd = self.ops[d]
                if pd["dma"] is not None:
                    sem, val = dsem[pd["dma"]], 16 * pd["dma_count"]
                else:
                    sem, val = esem[pd["engine"]], pd["inc_val"]
                k = id(sem)
                if k not in waits or waits[k][1] < val:
                    waits[k] = (sem, val)
            op["waits"] = list(waits.values())
        watermark = {}
        for kind, payload in self.phases:
            if kind == "barrier":
                nc.all_core_barrier()
                continue
            by_eng = {}
            for op in payload:
                by_eng.setdefault(op["engine"], []).append(op)
            with nc.Block() as blk:
                for ename, ops_ in by_eng.items():
                    def mk(ename, ops_):
                        def body(eng):
                            for op in ops_:
                                for sem, val in op["waits"]:
                                    if watermark.get((ename, id(sem)), 0) < val:
                                        eng.wait_ge(sem, val)
                                        watermark[(ename, id(sem))] = val
                                inst = op["fn"](eng)
                                if op["dma"] is not None:
                                    inst.then_inc(self.dsem[op["dma"]], 16)
                                elif op["do_inc"]:
                                    inst.then_inc(self.esem[ename], 1)
                        return body
                    getattr(blk, ename)(mk(ename, ops_))

    def final_waits(self, engine, tags):
        full = [t2 for t2 in self.dma_counts for t in tags if t2.startswith(t)]
        def fw(eng):
            for t in full:
                eng.wait_ge(self.dsem[t], 16 * self.dma_counts[t])
        with self.nc.Block() as blk:
            getattr(blk, engine)(fw)


def build(debug=False, num_devices=8):
    nc = bacc.Bacc("TRN2", target_bir_lowering=False, debug=False, num_devices=num_devices)

    xin = nc.dram_tensor("x", [T_KV, C], F32, kind="ExternalInput").ap()
    wq = nc.dram_tensor("wq", [C, C], F32, kind="ExternalInput").ap()
    wk = nc.dram_tensor("wk", [HD, C], F32, kind="ExternalInput").ap()
    wv = nc.dram_tensor("wv", [HD, C], F32, kind="ExternalInput").ap()
    wo = nc.dram_tensor("wo", [C, C], F32, kind="ExternalInput").ap()
    wgi = nc.dram_tensor("wg", [C, E], F32, kind="ExternalInput").ap()
    w1 = nc.dram_tensor("w1", [E_LOC, C, H_FFN], F32, kind="ExternalInput").ap()
    w2 = nc.dram_tensor("w2", [E_LOC, H_FFN, C], F32, kind="ExternalInput").ap()
    gq_in = nc.dram_tensor("gq", [128, C], F32, kind="ExternalInput").ap()
    gk_in = nc.dram_tensor("gk", [128, HD], F32, kind="ExternalInput").ap()
    cos_in = nc.dram_tensor("cosb", [T_KV, 128], F32, kind="ExternalInput").ap()
    sin_in = nc.dram_tensor("sinb", [T_KV, 128], F32, kind="ExternalInput").ap()
    kb_in = nc.dram_tensor("kbias", [1, T_KV], F32, kind="ExternalInput").ap()
    sel_in = nc.dram_tensor("sel4", [128, E_LOC * E], F32, kind="ExternalInput").ap()
    hsel_in = nc.dram_tensor("hsel", [128, 2], F32, kind="ExternalInput").ap()
    idf_in = nc.dram_tensor("idf", [128, 128], F32, kind="ExternalInput").ap()
    i512_in = nc.dram_tensor("i512", [128, 512], F32, kind="ExternalInput").ap()
    yout = nc.dram_tensor("y", [T_OWN, C], F32, kind="ExternalOutput").ap()

    xprime = nc.dram_tensor("xprime", [T_OWN, C], BF16,
                            kind=("ExternalOutput" if debug else "Internal")).ap()
    xs = nc.dram_tensor("xs_sh", [T, C], BF16, kind="Internal", addr_space="Shared").ap()
    msh = nc.dram_tensor("ms_sh", [E, T], BF16, kind="Internal", addr_space="Shared").ap()
    so = nc.dram_tensor("so_sh", [T, C], BF16, kind="Internal", addr_space="Shared").ap()
    if debug:
        dbg_ms = nc.dram_tensor("dbg_ms", [E, T], F32, kind="ExternalOutput").ap()
        dbg_rt = nc.dram_tensor("dbg_rt", [128, 64], F32, kind="ExternalOutput").ap()
        dbg_di = nc.dram_tensor("dbg_di", [128, 16], F32, kind="ExternalOutput").ap()
        dbg_si = nc.dram_tensor("dbg_si", [E_LOC * CAP, C], F32, kind="ExternalOutput").ap()

    p = Prog(nc)
    hof = {}

    def snap_hof(e):
        hof[0] = e.snap((e.partition_id() % 2) * T_OWN, min_val=0, max_val=T_OWN)

    def snap_hofg(e):
        hof[1] = e.snap((e.partition_id() % 2) * T_OWN, min_val=0, max_val=T_OWN)
        return e.nop()
    stack = ExitStack()
    sbp = lambda n, s, d: nc.alloc_sbuf_tensor(n, s, d)
    sbs = lambda n, s, d: stack.enter_context(nc.sbuf_tensor(n, s, d))

    id_f = sbp("id_f", [128, 128], F32)
    id_b = sbp("id_b", [128, 128], BF16)
    ones1b = sbp("ones1b", [1, 128], BF16)
    ones128b = sbp("ones128b", [128, 1], BF16)
    iotadesc = sbp("iotadesc", [128, 128], F32)
    sel4 = sbp("sel4_sb", [128, E_LOC * E], F32)
    hsel = sbp("hsel_sb", [128, 2], F32)
    eps6 = sbp("eps6", [128, 1], F32)
    eps5 = sbp("eps5", [128, 1], F32)
    mask_tm = sbp("mask_tm", [128, NT * E], BF16)
    lgall = sbp("lgall", [128, NT * E], F32)
    gate_t = sbp("gate_t", [128, NT], F32)
    scale_t = sbp("scale_t", [128, NT], F32)
    gidx = sbp("gidx", [128, NT], U32)
    didx = sbp("didx", [128, 16], U32)
    lsum = sbp("lsum", [128, 2], F32)
    lmean = sbp("lmean", [128, 2], F32)
    lvar = sbp("lvar", [128, 2], F32)
    lm2 = sbp("lm2", [128, 2], F32)
    lrstd = sbp("lrstd", [128, 2], F32)
    tmpa = sbp("tmpa", [128, E], F32)
    tmpb2 = sbp("tmpb2", [128, E], F32)

    ptrb = nc.alloc_psum_tensor("ps_trb", [128, 1024], BF16)
    ptrb2 = nc.alloc_psum_tensor("ps_trb2", [128, 1024], BF16)

    p.add("scalar", snap_hof)
    p.add("gpsimd", snap_hofg)
    p.dma("sync", id_f[:], idf_in, "l_idf", writes=["id_f"])
    p.dma("gpsimd", id_b[:], idf_in, "l_idb", writes=["id_b"])
    p.dma("sync", sel4[:], sel_in, "l_sel", writes=["sel4"])
    p.dma("sync", hsel[:], hsel_in, "l_hsel", writes=["hsel"])
    for tile, val in ((ones1b, 1.0), (ones128b, 1.0),
                      (eps6, 1e-6), (eps5, 1e-5)):
        p.add("vector", lambda e, tl=tile, v=val: e.memset(tl[:], v),
              writes=[tile.name])
    p.add("gpsimd", lambda e: e.iota(iotadesci[:], pattern=[[0, 16], [-1, 8]], base=8,
                                     channel_multiplier=0), writes=["iotadesci"])
    p.add("vector", lambda e: e.tensor_copy(iotadesc[:], iotadesci[:]),
          reads=["iotadesci"], writes=["iotadesc"])

    attn_outer = ExitStack()
    so_ = attn_outer.enter_context
    kbias_b = so_(nc.sbuf_tensor("kbias_b", [1, T_KV], BF16))
    umask = so_(nc.sbuf_tensor("umask", [128, 128], F32))
    lmask = so_(nc.sbuf_tensor("lmask", [128, 128], F32))
    mTst = so_(nc.sbuf_tensor("mTst", [8, 128], BF16))
    iotadesci = so_(nc.sbuf_tensor("iotadesci", [128, 128], I32))
    gq_sb = so_(nc.sbuf_tensor("gq_sb", [128, C], BF16))
    gk_sb = so_(nc.sbuf_tensor("gk_sb", [128, HD], BF16))
    cosb = so_(nc.sbuf_tensor("cosb_sb", [128, T_KV], BF16))
    sinb = so_(nc.sbuf_tensor("sinb_sb", [128, T_KV], BF16))
    qT = so_(nc.sbuf_tensor("qT", [128, NH * 2 * T_OWN], BF16))
    kT = so_(nc.sbuf_tensor("kT", [128, 2 * T_KV], BF16))
    vsb = so_(nc.sbuf_tensor("vsb", [128, NKV * HD], BF16))
    wstg = so_(nc.sbuf_tensor("wstg", [128, 2 * C], BF16))
    rowmax = so_(nc.sbuf_tensor("rowmax", [128, 4], F32))
    tstg = so_(nc.sbuf_tensor("tstg", [128, 3 * 128], F32))
    negmax = so_(nc.sbuf_tensor("negmax", [128, 2], F32))
    denom = so_(nc.sbuf_tensor("denom", [128, 2], F32))
    rden = so_(nc.sbuf_tensor("rden", [128, 2], F32))
    xres = so_(nc.sbuf_tensor("xres", [128, 2 * C], F32))
    lnr = so_(nc.sbuf_tensor("lnr", [128, 2 * C], F32))
    xpb = so_(nc.sbuf_tensor("xpb", [128, 2 * C], BF16))
    xpTs = so_(nc.sbuf_tensor("xpTs", [128, C], BF16))
    lnsq = so_(nc.sbuf_tensor("lnsq", [128, C], F32))
    m8 = so_(nc.sbuf_tensor("m8", [128, 1], F32))
    dnm = so_(nc.sbuf_tensor("dnm", [128, 1], F32))
    pr_t = so_(nc.sbuf_tensor("pr_t", [128, E], F32))
    pmax = so_(nc.sbuf_tensor("pmax", [128, 1], F32))

    psA = ExitStack()
    qps = [psA.enter_context(nc.psum_tensor(f"ps_s{i}", [128, 512], F32)) for i in range(3)]
    qtail = psA.enter_context(nc.psum_tensor("ps_qt", [128, 128], F32))
    ps_av = psA.enter_context(nc.psum_tensor("ps_av", [128, 512], F32))
    ps_x = psA.enter_context(nc.psum_tensor("ps_x", [128, 512], F32))
    lgps = ps_av[:, 128:136]

    p.dma("gpsimd", gq_sb[:], gq_in, "l_gq", writes=["gq_sb"])
    p.dma("gpsimd", gk_sb[:], gk_in, "l_gk", writes=["gk_sb"])
    p.dma("gpsimd", kbias_b[:], kb_in, "l_kb", writes=["kbias_b"])
    for tile, val in ((umask, 0.0), (lmask, 0.0)):
        p.add("vector", lambda e, tl=tile, v=val: e.memset(tl[:], v), writes=[tile.name])
    p.add("gpsimd", lambda e: e.affine_select(umask[:], umask[:], pattern=[[1, 128]],
                                              compare_op=ALU.is_ge, fill=NEG, base=0,
                                              channel_multiplier=-1),
          reads=["umask"], writes=["umask"])
    p.add("gpsimd", lambda e: e.affine_select(lmask[:], lmask[:], pattern=[[-1, 128]],
                                              compare_op=ALU.is_ge, fill=NEG, base=0,
                                              channel_multiplier=1),
          reads=["lmask"], writes=["lmask"])
    p.dma("gpsimd", cosb[:].rearrange("p (t i) -> p t i", i=128),
          cos_in.rearrange("(t p) i -> p t i", p=128), "l_cos", writes=["cosb"])
    p.dma("gpsimd", sinb[:].rearrange("p (t i) -> p t i", i=128),
          sin_in.rearrange("(t p) i -> p t i", p=128), "l_sin", writes=["sinb"])

    attn_inner = ExitStack()
    si_ = attn_inner.enter_context
    wqT = si_(nc.sbuf_tensor("wqT", [128, NCH * C], BF16))
    wkT = si_(nc.sbuf_tensor("wkT", [128, NCH * HD], BF16))
    wvT = si_(nc.sbuf_tensor("wvT", [128, NCH * HD], BF16))
    xT = si_(nc.sbuf_tensor("xT", [128, NCH * 1024], BF16))
    qf = si_(nc.sbuf_tensor("qf", [128, 2 * C], F32))
    qsq = si_(nc.sbuf_tensor("qsq", [128, HD], F32))
    qrt = si_(nc.sbuf_tensor("qrt", [128, C], BF16))
    qb = si_(nc.sbuf_tensor("qb", [128, 4 * C], BF16))
    ms4 = si_(nc.sbuf_tensor("ms4", [128, 2 * NH], F32))
    rstd4 = si_(nc.sbuf_tensor("rstd4", [128, 2 * NH], F32))


    def load_transposed(wsrc, wdst, dname, nrow_chunks, ncol_chunks, nout):
        for rc in range(nrow_chunks):
            sl = rc % 2
            bk_, bkk = (ptrb, "ps_trb") if rc % 2 == 0 else (ptrb2, "ps_trb2")
            p.dma("gpsimd", wstg[:, sl * C: sl * C + C], wsrc[rc * 128:(rc + 1) * 128, :],
                  f"wld{sl}", writes=[("wstg", sl)])
            for cc in range(ncol_chunks):
                p.add("tensor", lambda e, cc=cc, sl=sl, bk_=bk_: e.transpose(
                    bk_[:, cc * 128:(cc + 1) * 128],
                    wstg[:, sl * C + cc * 128: sl * C + (cc + 1) * 128], id_b[:]),
                    reads=[("wstg", sl), "id_b"], writes=[bkk])
            dst = wdst[:].rearrange("p (c i) -> p c i", i=nout)[:, :, rc * 128:(rc + 1) * 128]
            p.add("vector", lambda e, dst=dst, ncol_chunks=ncol_chunks, bk_=bk_: e.tensor_copy(
                dst, bk_[:, 0:ncol_chunks * 128].rearrange("p (c i) -> p c i", i=128)),
                reads=[bkk], writes=[dname])

    load_transposed(wq, wqT, "wqT", NCH, NCH, C)
    load_transposed(wk, wkT, "wkT", HD // 128, NCH, HD)
    load_transposed(wv, wvT, "wvT", HD // 128, NCH, HD)

    xbf2 = si_(nc.sbuf_tensor("xbf2", [128, 2 * C], BF16))

    def dma_x(t):
        sl = t % 2
        p.dma("gpsimd", xbf2[:, sl * C: sl * C + C], xin[t * 128:(t + 1) * 128, :], f"xl{sl}",
              writes=[("xbf2", sl)])

    def trx(t):
        sl = t % 2
        xsl = t % 8
        for cc in range(NCH):
            p.add("tensor", lambda e, cc=cc, sl=sl: e.transpose(
                ptrb2[:, cc * 128:(cc + 1) * 128],
                xbf2[:, sl * C + cc * 128: sl * C + (cc + 1) * 128], id_b[:]),
                reads=[("xbf2", sl), "id_b"], writes=["ps_trb2"])
        dst = xT[:].rearrange("p (c i) -> p c i", i=1024)[:, :, xsl * 128:(xsl + 1) * 128]
        p.add("scalar", lambda e, dst=dst: e.copy(dst, ptrb2[:].rearrange("p (c i) -> p c i", i=128)),
              reads=["ps_trb2"], writes=[("xT", xsl)])

    def qkv_mm(t, kv, par):
        if kv:
            wname, wT_, nout = "wkT", wkT, HD
            banks, keys = [ps_av], ["ps_av"]
        else:
            wname, wT_, nout = "wqT", wqT, C
            banks, keys = [qps[0], qps[1]], [("ps_s", 0), ("ps_s", 1)]
        nb = (nout + 511) // 512
        xsl = (t if kv else t + 4) % 8
        for b_ in range(nb):
            n0 = b_ * 512
            nw = min(512, nout - n0)
            for cc in range(NCH):
                p.add("tensor", lambda e, bk=banks[b_], cc=cc, n0=n0, nw=nw, xsl=xsl, wT_=wT_, nout=nout: e.matmul(
                    bk[:, 0:nw],
                    xT[:, cc * 1024 + xsl * 128: cc * 1024 + (xsl + 1) * 128],
                    wT_[:, cc * nout + n0: cc * nout + n0 + nw],
                    start=(cc == 0), stop=(cc == NCH - 1)),
                    reads=[("xT", xsl), wname], writes=[keys[b_]])
        for b_ in range(nb):
            n0 = b_ * 512
            nw = min(512, nout - n0)
            p.add("scalar", lambda e, bk=banks[b_], n0=n0, nw=nw, par=par: e.copy(
                qf[:, par * C + n0: par * C + n0 + nw], bk[:, 0:nw]),
                reads=[keys[b_]], writes=[("qf", par)])

    def v_mm(t):
        xsl = t % 8
        for cc in range(NCH):
            p.add("tensor", lambda e, cc=cc, xsl=xsl: e.matmul(
                ps_x[:, 0:HD], xT[:, cc * 1024 + xsl * 128: cc * 1024 + (xsl + 1) * 128],
                wvT[:, cc * HD:(cc + 1) * HD], start=(cc == 0), stop=(cc == NCH - 1)),
                reads=[("xT", xsl), "wvT"], writes=["ps_x"])
        p.add("scalar", lambda e, t=t: e.copy(vsb[:, t * HD:(t + 1) * HD], ps_x[:, 0:HD]),
              reads=["ps_x"], writes=["vsb"])

    def qkv_nr(t, kv, par, par4):
        nout = HD if kv else C
        nh_loc = 1 if kv else NH
        tok0 = t * 128 + (0 if kv else HALO)
        o = par * C
        qfv = qf[:, o:o + nout]
        qbv = qb[:, par4 * C: par4 * C + nout]
        msv = ms4[:, par * NH: par * NH + nh_loc]
        rsv = rstd4[:, par * NH: par * NH + nh_loc]
        for hh in range(nh_loc):
            p.add("scalar", lambda e, hh=hh: e.activation(
                qsq[:, 0:HD], qfv[:, hh * HD:(hh + 1) * HD], AF.Square,
                accum_out=msv[:, hh:hh + 1]),
                reads=[("qf", par)], writes=["qsq_dump", ("ms4", par)])
        p.add("scalar", lambda e: e.activation(rsv, msv, AF.Sqrt, scale=1.0 / HD, bias=eps6[:]),
              reads=[("ms4", par), "eps6"], writes=[("rstd4", par)])
        p.add("vector", lambda e: e.reciprocal(rsv, rsv),
              reads=[("rstd4", par)], writes=[("rstd4", par)])
        for hh in range(nh_loc):
            gam = gk_sb[:, 0:HD] if kv else gq_sb[:, hh * HD:(hh + 1) * HD]
            p.add("vector", lambda e, hh=hh, gam=gam: e.scalar_tensor_tensor(
                qbv[:, hh * HD:(hh + 1) * HD], qfv[:, hh * HD:(hh + 1) * HD],
                rsv[:, hh:hh + 1], gam, ALU.mult, ALU.mult),
                reads=[("qf", par), ("rstd4", par), "gk_sb" if kv else "gq_sb"],
                writes=[("qb", par4)])
        ct = cosb[:, tok0: tok0 + 128]
        st_ = sinb[:, tok0: tok0 + 128]
        qv = qbv.rearrange("p (h two i) -> p h two i", two=2, i=128)
        Av = qv[:, :, 0, :]
        Bv = qv[:, :, 1, :]
        ctb = ct.unsqueeze(1).broadcast_to([128, nh_loc, 128])
        stb = st_.unsqueeze(1).broadcast_to([128, nh_loc, 128])
        u3 = qrt[:, 0:nh_loc * 128].rearrange("p (h i) -> p h i", i=128)
        v3 = qrt[:, 512:512 + nh_loc * 128].rearrange("p (h i) -> p h i", i=128)
        p.add("vector", lambda e: e.tensor_tensor(u3, Av, stb, ALU.mult),
              reads=[("qb", par4), "sinb"], writes=["qrt"])
        p.add("vector", lambda e: e.tensor_tensor(Av, Av, ctb, ALU.mult),
              reads=[("qb", par4), "cosb"], writes=[("qb", par4)])
        p.add("vector", lambda e: e.tensor_tensor(v3, Bv, stb, ALU.mult),
              reads=[("qb", par4), "sinb"], writes=["qrt"])
        p.add("vector", lambda e: e.tensor_tensor(Bv, Bv, ctb, ALU.mult),
              reads=[("qb", par4), "cosb"], writes=[("qb", par4)])
        p.add("vector", lambda e: e.tensor_tensor(Av, Av, v3, ALU.subtract),
              reads=[("qb", par4), "qrt"], writes=[("qb", par4)])
        p.add("vector", lambda e: e.tensor_tensor(Bv, Bv, u3, ALU.add),
              reads=[("qb", par4), "qrt"], writes=[("qb", par4)])

    def qkv_tr(t, kv, par4):
        o = par4 * C
        n = 2 if kv else 8
        for u in range(n):
            p.add("tensor", lambda e, u=u: e.transpose(
                ptrb[:, u * 128:(u + 1) * 128], qb[:, o + u * 128: o + (u + 1) * 128],
                id_b[:]),
                reads=[("qb", par4), "id_b"], writes=["ps_trb"])
        if kv:
            dst = kT[:].rearrange("p (u i) -> p u i", i=T_KV)[:, :, t * 128:(t + 1) * 128]
            p.add("scalar", lambda e, dst=dst: e.copy(
                dst, ptrb[:, 0:256].rearrange("p (u i) -> p u i", i=128)),
                reads=["ps_trb"], writes=["kT"])
        else:
            dst = qT[:].rearrange("p (u i) -> p u i", i=T_OWN)[:, :, t * 128:(t + 1) * 128]
            p.add("scalar", lambda e, dst=dst: e.copy(
                dst, ptrb[:].rearrange("p (u i) -> p u i", i=128)),
                reads=["ps_trb"], writes=["qT"])

    units = []
    for t in range(5):
        units.append((t, True))
    for t in range(5, NKV):
        units.append((t - 5, False))
        units.append((t, True))
    units.append((15, False))
    dma_x(0)
    dma_x(1)
    trx(0)
    pend = []
    for i, (t, kv) in enumerate(units):
        par = i % 2
        par4 = i % 4
        qkv_mm(t, kv, par)
        if kv:
            v_mm(t)
        if len(pend) >= 2:
            qkv_tr(*pend.pop(0))
        nxt = units[i + 1] if i + 1 < len(units) else None
        if nxt is not None and nxt[1]:
            tn = nxt[0]
            if tn + 1 < NKV:
                dma_x(tn + 1)
            trx(tn)
        qkv_nr(t, kv, par, par4)
        pend.append((t, kv, par4))
    for pp in pend:
        qkv_tr(*pp)

    attn_inner.close()
    aoT = so_(nc.sbuf_tensor("aoT", [128, NCH * T_OWN], BF16))
    probs_f = so_(nc.sbuf_tensor("probs_f", [128, 2 * 640], F32))
    probsN = so_(nc.sbuf_tensor("probsN", [128, 4 * 640], BF16))
    probsTT = so_(nc.sbuf_tensor("probsTT", [128, 2 * 640], BF16))

    woT = so_(nc.sbuf_tensor("woT", [128, NCH * C], BF16))
    wg_sb = so_(nc.sbuf_tensor("wg_sb", [128, NCH * E], BF16))
    FREED = ["xT", "wqT", "wkT", "wvT", ("xbf2", 0), ("xbf2", 1), "qsq_dump", "qrt"] + [
        (n, par) for n in ("qf", "ms4", "rstd4") for par in (0, 1)] + [
        ("qb", s) for s in range(4)]
    for eng in ("scalar", "vector", "gpsimd", "tensor"):
        p.fence(eng, FREED)

    load_transposed(wo, woT, "woT", NCH, NCH, C)
    p.dma("gpsimd", wg_sb[:].rearrange("p (c e) -> p c e", e=E),
          wgi.rearrange("(c p) e -> p c e", p=128), "l_wg", writes=["wg_sb"])

    def att_sc(t, hh, pi, si):
        sp = qps[si]
        tl = qtail[:]
        ts_ = tstg[:, si * 128:(si + 1) * 128]
        spk = ("ps_s", si)
        tlk = "ps_qt"
        k0 = t * 128
        for u in range(2):
            p.add("tensor", lambda e, u=u, k0=k0, t=t, hh=hh, sp=sp: e.matmul(
                sp[:, 0:512],
                qT[:, (hh * 2 + u) * T_OWN + t * 128:(hh * 2 + u) * T_OWN + (t + 1) * 128],
                kT[:, u * T_KV + k0: u * T_KV + k0 + 512],
                start=(u == 0), stop=False),
                reads=["qT", "kT"], writes=[spk])
            p.add("tensor", lambda e, u=u, k0=k0, t=t, hh=hh, tl=tl: e.matmul(
                tl,
                qT[:, (hh * 2 + u) * T_OWN + t * 128:(hh * 2 + u) * T_OWN + (t + 1) * 128],
                kT[:, u * T_KV + k0 + 512: u * T_KV + k0 + 640],
                start=(u == 0), stop=False),
                reads=["qT", "kT"], writes=[tlk])
        p.add("tensor", lambda e, k0=k0, sp=sp: e.matmul(
            sp[:, 0:512], ones1b[:], kbias_b[:, k0:k0 + 512], start=False, stop=True),
            reads=["ones1b", "kbias_b"], writes=[spk])
        p.add("tensor", lambda e, k0=k0, tl=tl: e.matmul(
            tl, ones1b[:], kbias_b[:, k0 + 512:k0 + 640],
            start=False, stop=True),
            reads=["ones1b", "kbias_b"], writes=[tlk])
        p.add("scalar", lambda e, tl=tl, ts_=ts_: e.copy(ts_, tl),
              reads=[tlk], writes=[("tstg", si)])

    def att_sm(t, hh, pi, pi4, si):
        sp = qps[si]
        tl = tstg[:, si * 128:(si + 1) * 128]
        spk = ("ps_s", si)
        tlk = ("tstg", si)
        pf = probs_f[:, pi * 640:(pi + 1) * 640]
        pn = probsN[:, pi4 * 640:(pi4 + 1) * 640]
        rmx = rowmax[:, pi:pi + 1]
        rm2 = rowmax[:, 2 + pi:3 + pi]
        nmx = negmax[:, pi:pi + 1]
        dnm_ = denom[:, pi:pi + 1]
        rdn = rden[:, pi:pi + 1]
        p.add("vector", lambda e, sp=sp: e.tensor_add(sp[:, 0:128], sp[:, 0:128], umask[:]),
              reads=[spk, "umask"], writes=[spk])
        p.add("vector", lambda e, tl=tl: e.tensor_add(tl, tl, lmask[:]),
              reads=[tlk, "lmask"], writes=[tlk])
        p.add("vector", lambda e, sp=sp, rmx=rmx: e.reduce_max(rmx, sp[:], axis=AX.X),
              reads=[spk], writes=[("rowmax", pi)])
        p.add("vector", lambda e, tl=tl, rm2=rm2: e.reduce_max(rm2, tl, axis=AX.X),
              reads=[tlk], writes=[("rowmax2", pi)])
        p.add("vector", lambda e, rmx=rmx, rm2=rm2: e.tensor_tensor(rmx, rmx, rm2, ALU.max),
              reads=[("rowmax", pi), ("rowmax2", pi)], writes=[("rowmax", pi)])
        p.add("vector", lambda e, rmx=rmx, nmx=nmx: e.tensor_scalar_mul(nmx, rmx, -0.0625),
              reads=[("rowmax", pi)], writes=[("negmax", pi)])
        p.add("scalar", lambda e, sp=sp, pf=pf, nmx=nmx: e.activation(
            pf[:, 0:512], sp[:], AF.Exp, scale=0.0625, bias=nmx),
            reads=[spk, ("negmax", pi)], writes=[("probs_f", pi)])
        p.add("scalar", lambda e, tl=tl, pf=pf, nmx=nmx: e.activation(
            pf[:, 512:640], tl, AF.Exp, scale=0.0625, bias=nmx),
            reads=[tlk, ("negmax", pi)], writes=[("probs_f", pi)])

    def att_sm2(t, hh, pi, pi4):
        pf = probs_f[:, pi * 640:(pi + 1) * 640]
        pn = probsN[:, pi4 * 640:(pi4 + 1) * 640]
        dnm_ = denom[:, pi:pi + 1]
        rdn = rden[:, pi:pi + 1]
        p.add("vector", lambda e, pf=pf, dnm_=dnm_: e.reduce_sum(dnm_, pf, axis=AX.X),
              reads=[("probs_f", pi)], writes=[("denom", pi)])
        p.add("vector", lambda e, dnm_=dnm_, rdn=rdn: e.reciprocal(rdn, dnm_),
              reads=[("denom", pi)], writes=[("rden", pi)])
        p.add("vector", lambda e, pf=pf, pn=pn, rdn=rdn: e.tensor_scalar(
            pn, pf, rdn, None, ALU.mult),
            reads=[("probs_f", pi), ("rden", pi)], writes=[("probsN", pi4)])

    def att_trav(t, hh, pi, pi4):
        pn = probsN[:, pi4 * 640:(pi4 + 1) * 640]
        ptt = probsTT[:, pi * 640:(pi + 1) * 640]
        avt = ps_av if pi == 0 else ps_x
        avk = "ps_av" if pi == 0 else "ps_x"
        bk_, bkk = (ptrb, "ps_trb") if pi == 0 else (ptrb2, "ps_trb2")
        for kt in range(5):
            p.add("tensor", lambda e, kt=kt, pn=pn, bk_=bk_: e.transpose(
                bk_[:, kt * 128:(kt + 1) * 128], pn[:, kt * 128:(kt + 1) * 128], id_b[:]),
                reads=[("probsN", pi4), "id_b"], writes=[bkk])
        p.add("scalar", lambda e, ptt=ptt, bk_=bk_: e.copy(ptt[:, 0:640], bk_[:, 0:640]),
              reads=[bkk], writes=[("probsTT", pi)])
        for u in range(2):
            for kt in range(5):
                p.add("tensor", lambda e, u=u, kt=kt, t=t, ptt=ptt, avt=avt: e.matmul(
                    avt[:, u * 128:(u + 1) * 128],
                    vsb[:, (t + kt) * HD + u * 128: (t + kt) * HD + (u + 1) * 128],
                    ptt[:, kt * 128:(kt + 1) * 128],
                    start=(u == 0 and kt == 0), stop=(u == 1 and kt == 4)),
                    reads=["vsb", ("probsTT", pi)], writes=[avk])
        for u in range(2):
            p.add("scalar", lambda e, t=t, hh=hh, u=u, avt=avt: e.copy(
                aoT[:, (hh * 2 + u) * T_OWN + t * 128:
                    (hh * 2 + u) * T_OWN + (t + 1) * 128],
                avt[:, u * 128:(u + 1) * 128]),
                reads=[avk], writes=["aoT"])

    aunits = [(t, hh) for t in range(NT) for hh in range(NH)]
    pend_a = []
    for i, (t, hh) in enumerate(aunits):
        pi = i % 2
        pi4 = i % 4
        si = i % 3
        att_sc(t, hh, pi, si)
        att_sm(t, hh, pi, pi4, si)
        if pend_a:
            att_sm2(*pend_a[-1])
        if len(pend_a) >= 2:
            att_trav(*pend_a.pop(0))
        pend_a.append((t, hh, pi, pi4))
    att_sm2(*pend_a[-1])
    for pa in pend_a:
        att_trav(*pa)

    def ph3_wo(t, pr):
        for b_ in range(2):
            for cc in range(NCH):
                p.add("tensor", lambda e, b_=b_, cc=cc, t=t: e.matmul(
                    qps[b_][:, 0:512], aoT[:, cc * T_OWN + t * 128: cc * T_OWN + (t + 1) * 128],
                    woT[:, cc * C + b_ * 512: cc * C + b_ * 512 + 512],
                    start=(cc == 0), stop=(cc == NCH - 1)),
                    reads=["aoT", "woT"], writes=[("ps_s", b_)])

    def ph3_ln(t, pr):
        o = pr * C
        xrv = xres[:, o:o + C]
        lnv = lnr[:, o:o + C]
        lsq = lnsq[:, 0:C]
        xpv = xpb[:, o:o + C]
        ls_ = lsum[:, pr:pr + 1]
        lm_ = lmean[:, pr:pr + 1]
        lv_ = lvar[:, pr:pr + 1]
        l2_ = lm2[:, pr:pr + 1]
        lr_ = lrstd[:, pr:pr + 1]
        p.dma("sync", xrv, xin[HALO + t * 128: HALO + (t + 1) * 128, :], f"xr{pr}",
              writes=[("xres", pr)])
        ls2 = lm2[:, pr:pr + 1]
        for b_ in range(2):
            acc = ls_ if b_ == 0 else ls2
            p.add("vector", lambda e, b_=b_, xrv=xrv, lnv=lnv, acc=acc: e.scalar_tensor_tensor(
                lnv[:, b_ * 512:(b_ + 1) * 512], xrv[:, b_ * 512:(b_ + 1) * 512], 1.0,
                qps[b_][:, 0:512], ALU.mult, ALU.add, accum_out=acc),
                reads=[("xres", pr), ("ps_s", b_)], writes=[("lnr", pr), ("lsum", pr)])
        p.add("vector", lambda e, ls_=ls_, ls2=ls2: e.tensor_add(ls_, ls_, ls2),
              reads=[("lsum", pr)], writes=[("lsum", pr)])
        p.add("vector", lambda e, ls_=ls_, lm_=lm_: e.tensor_scalar_mul(lm_, ls_, 1.0 / C),
              reads=[("lsum", pr)], writes=[("lmean", pr)])
        p.add("scalar", lambda e, lnv=lnv, lsq=lsq, lv_=lv_: e.activation(
            lsq, lnv, AF.Square, accum_out=lv_),
            reads=[("lnr", pr)], writes=[("lnsq", pr), ("lvar", pr)])
        p.add("vector", lambda e, lm_=lm_, l2_=l2_: e.tensor_mul(l2_, lm_, lm_),
              reads=[("lmean", pr)], writes=[("lm2", pr)])
        p.add("vector", lambda e, lv_=lv_: e.tensor_scalar(lv_, lv_, 1.0 / C, None, ALU.mult),
              reads=[("lvar", pr)], writes=[("lvar", pr)])
        p.add("vector", lambda e, lv_=lv_, l2_=l2_: e.tensor_sub(lv_, lv_, l2_),
              reads=[("lvar", pr), ("lm2", pr)], writes=[("lvar", pr)])
        p.add("scalar", lambda e, lv_=lv_, lr_=lr_: e.activation(lr_, lv_, AF.Sqrt, bias=eps5[:]),
              reads=[("lvar", pr), "eps5"], writes=[("lrstd", pr)])
        p.add("vector", lambda e, lr_=lr_: e.reciprocal(lr_, lr_),
              reads=[("lrstd", pr)], writes=[("lrstd", pr)])
        p.add("vector", lambda e, lnv=lnv, xpv=xpv, lm_=lm_, lr_=lr_: e.tensor_scalar(
            xpv, lnv, lm_, lr_, ALU.subtract, ALU.mult),
            reads=[("lnr", pr), ("lmean", pr), ("lrstd", pr)], writes=[("xpb", pr)])
        p.dma("sync", xprime[t * 128:(t + 1) * 128, :], xpv, "xpst",
              reads=[("xpb", pr)], writes=["xprime_dram"])
        def xs_store(e, t=t, xpv=xpv):
            return e.dma_start(out=xs[ds(hof[1] + t * 128, 128), :], in_=xpv)
        p.add("gpsimd", xs_store, reads=[("xpb", pr)], writes=["xs_dram"], dma=f"xsst{pr}")

    def ph3_rt(t, pr):
        o = pr * C
        xpv = xpb[:, o:o + C]
        xTs = xpTs[:, 0:C]
        bk_, bkk = (ptrb, "ps_trb") if pr == 0 else (ptrb2, "ps_trb2")
        for cc in range(NCH):
            p.add("tensor", lambda e, cc=cc, xpv=xpv, bk_=bk_: e.transpose(
                bk_[:, cc * 128:(cc + 1) * 128], xpv[:, cc * 128:(cc + 1) * 128], id_b[:]),
                reads=[("xpb", pr), "id_b"], writes=[bkk])
        p.add("scalar", lambda e, xTs=xTs, bk_=bk_: e.copy(xTs[:, 0:C], bk_[:]),
              reads=[bkk], writes=[("xpTs", pr)])
        for cc in range(NCH):
            p.add("tensor", lambda e, cc=cc, xTs=xTs: e.matmul(
                lgps, xTs[:, cc * 128:(cc + 1) * 128], wg_sb[:, cc * E:(cc + 1) * E],
                start=(cc == 0), stop=(cc == NCH - 1)),
                reads=[("xpTs", pr), "wg_sb"], writes=["ps_av"])
        lgv = lgall[:, t * E:(t + 1) * E]
        p.add("vector", lambda e, lgv=lgv: e.tensor_copy(lgv, lgps), reads=["ps_av"],
              writes=[("lgall", t)])
        p.add("vector", lambda e, lgv=lgv: e.reduce_max(m8[:], lgv, axis=AX.X),
              reads=[("lgall", t)], writes=["m8"])
        p.add("vector", lambda e, lgv=lgv: e.tensor_scalar(pr_t[:], lgv, m8[:], None,
                                                           ALU.is_ge),
              reads=[("lgall", t), "m8"], writes=["pr_t"])
        p.add("vector", lambda e: e.tensor_tensor(pr_t[:], pr_t[:], iotadesc[:, 0:E], ALU.mult),
              reads=["pr_t", "iotadesc"], writes=["pr_t"])
        p.add("vector", lambda e: e.reduce_max(pmax[:], pr_t[:], axis=AX.X),
              reads=["pr_t"], writes=["pmax"])
        p.add("vector", lambda e, t=t: e.tensor_scalar(
            mask_tm[:, t * E:(t + 1) * E], pr_t[:], pmax[:], None, ALU.is_ge),
            reads=["pr_t", "pmax"], writes=["mask_tm"])
        p.add("tensor", lambda e, t=t, bk_=bk_: e.transpose(
            bk_[0:8, 0:128], mask_tm[:, t * E:(t + 1) * E], id_b[:]),
            reads=["mask_tm", "id_b"], writes=[bkk])
        p.add("vector", lambda e, bk_=bk_: e.tensor_copy(mTst[:], bk_[0:8, 0:128]),
              reads=[bkk], writes=["mTst"])
        def ms_store(e, t=t):
            return e.dma_start(out=msh[:, ds(hof[1] + t * 128, 128)], in_=mTst[:])
        p.add("gpsimd", ms_store, reads=["mTst"], writes=["ms_dram"], dma="msst")

    prev3 = None
    for t in range(NT):
        pr = t % 2
        ph3_wo(t, pr)
        if prev3 is not None:
            ph3_rt(*prev3)
        ph3_ln(t, pr)
        prev3 = (t, pr)
    ph3_rt(*prev3)

    p.add("sync", lambda e: e.nop(),
          reads=["xs_dram", "ms_dram", "xprime_dram"])

    p.barrier()

    attn_outer.close()
    rtB = ExitStack()
    sb_ = rtB.enter_context
    iota512 = sb_(nc.sbuf_tensor("iota512", [128, 512], F32))
    cumtm = sb_(nc.sbuf_tensor("cumtm", [128, 32 * E], F32))
    mycum = sb_(nc.sbuf_tensor("mycum", [128, 32], F32))
    cmpb = sb_(nc.sbuf_tensor("cmpb", [128, 2 * 512], BF16))
    cntf = sb_(nc.sbuf_tensor("cntf", [1, 512], F32))
    nslot = sb_(nc.sbuf_tensor("nslot", [128, 1], F32))
    tmp256 = sb_(nc.sbuf_tensor("tmp256", [128, 256], F32))
    rt_scope = ExitStack()
    sr_ = rt_scope.enter_context
    ms_sb = sr_(nc.sbuf_tensor("ms_sb", [8, T], BF16))
    ones8 = sr_(nc.sbuf_tensor("ones8", [8, T], BF16))
    cums8 = sr_(nc.sbuf_tensor("cums8", [8, T], F32))
    lgm = sr_(nc.sbuf_tensor("lgm", [128, NT], F32))
    lgex = sr_(nc.sbuf_tensor("lgex", [128, NT * E], F32))
    p_t = sr_(nc.sbuf_tensor("p_t", [128, NT], F32))
    keep_t = sr_(nc.sbuf_tensor("keep_t", [128, NT], F32))
    en_t = sr_(nc.sbuf_tensor("en_t", [128, NT], F32))
    gidxf = sr_(nc.sbuf_tensor("gidxf", [128, NT], F32))
    pB = sr_(nc.sbuf_tensor("pB", [128, NT], F32))
    tmp128 = sr_(nc.sbuf_tensor("tmp128", [128, 128], F32))


    p.dma("sync", ms_sb[:], msh, "msld", writes=["ms_sb"])
    p.add("vector", lambda e: e.reduce_max(
        lgm[:], lgall[:].rearrange("p (t e) -> p t e", e=E), axis=AX.X),
        reads=[("lgall", t) for t in range(NT)] if False else ["lgall_all"], writes=["lgm"])
    p.add("vector", lambda e: e.tensor_tensor(
        lgex[:].rearrange("p (t e) -> p t e", e=E),
        lgall[:].rearrange("p (t e) -> p t e", e=E),
        lgm[:].unsqueeze(2).broadcast_to([128, NT, E]), ALU.subtract),
        reads=["lgall_all", "lgm"], writes=["lgex"])
    p.add("scalar", lambda e: e.activation(lgex[:], lgex[:], AF.Exp),
          reads=["lgex"], writes=["lgex"])
    p.add("vector", lambda e: e.reduce_sum(
        gate_t[:], lgex[:].rearrange("p (t e) -> p t e", e=E), axis=AX.X),
        reads=["lgex"], writes=["gate_t"])
    p.add("vector", lambda e: e.reciprocal(gate_t[:], gate_t[:]),
          reads=["gate_t"], writes=["gate_t"])
    p.dma("sync", iota512[:], i512_in, "l_i512", writes=["iota512"])
    p.add("vector", lambda e: e.memset(ones8[:], 1.0), writes=["ones8"])
    p.add("vector", lambda e: e.tensor_tensor_scan(cums8[:], ones8[:], ms_sb[:], 0.0,
                                                   ALU.mult, ALU.add),
          reads=["ones8", "ms_sb"], writes=["cums8"])
    for ch in range(32):
        cb, cbk = (ps_av, "ps_av") if ch % 2 == 0 else (ps_x, "ps_x")
        p.add("tensor", lambda e, ch=ch, cb=cb: e.transpose(
            cb[:, 0:8], cums8[:, ch * 128:(ch + 1) * 128], id_f[0:8, 0:8]),
            reads=["cums8", "id_f"], writes=[cbk])
        p.add("vector", lambda e, ch=ch, cb=cb: e.tensor_copy(
            cumtm[:, ch * E:(ch + 1) * E], cb[:, 0:8]),
            reads=[cbk], writes=["cumtm"])

    m3 = lambda ap: ap.rearrange("p (t e) -> p t e", e=E)
    p.add("vector", lambda e: e.tensor_tensor(tmp128[:], cumtm[:, 0:128], mask_tm[:], ALU.mult),
          reads=["cumtm", "mask_tm"], writes=["tmp128"])
    p.add("vector", lambda e: e.reduce_sum(p_t[:], m3(tmp128[:]), axis=AX.X),
          reads=["tmp128"], writes=["p_t"])
    p.add("vector", lambda e: e.tensor_tensor(tmp128[:], cumtm[:, 128:256], mask_tm[:], ALU.mult),
          reads=["cumtm", "mask_tm"], writes=["tmp128"])
    p.add("vector", lambda e: e.reduce_sum(pB[:], m3(tmp128[:]), axis=AX.X),
          reads=["tmp128"], writes=["pB"])
    p.add("vector", lambda e: e.tensor_scalar(p_t[:], p_t[:], hsel[:, 0:1], None, ALU.mult),
          reads=["p_t", "hsel"], writes=["p_t"])
    p.add("vector", lambda e: e.scalar_tensor_tensor(p_t[:], pB[:], hsel[:, 1:2], p_t[:],
                                                     ALU.mult, ALU.add),
          reads=["pB", "hsel", "p_t"], writes=["p_t"])
    p.add("vector", lambda e: e.tensor_tensor(tmp128[:], iotadesc[:], mask_tm[:], ALU.mult),
          reads=["iotadesc", "mask_tm"], writes=["tmp128"])
    p.add("vector", lambda e: e.reduce_sum(en_t[:], m3(tmp128[:]), axis=AX.X),
          reads=["tmp128"], writes=["en_t"])
    p.add("vector", lambda e: e.tensor_scalar_mul(en_t[:], en_t[:], -1.0),
          reads=["en_t"], writes=["en_t"])
    p.add("vector", lambda e: e.tensor_scalar_add(p_t[:], p_t[:], -1.0),
          reads=["p_t"], writes=["p_t"])
    p.add("vector", lambda e: e.tensor_scalar(keep_t[:], p_t[:], -1.0, 511.5,
                                              ALU.mult, ALU.add),
          reads=["p_t"], writes=["keep_t"])
    p.add("vector", lambda e: e.tensor_scalar(keep_t[:], keep_t[:], 0.0, None, ALU.is_ge),
          reads=["keep_t"], writes=["keep_t"])
    p.add("vector", lambda e: e.tensor_tensor(scale_t[:], gate_t[:], keep_t[:], ALU.mult),
          reads=["gate_t", "keep_t"], writes=["scale_t"])
    p.add("vector", lambda e: e.tensor_scalar(gidxf[:], en_t[:], 512.0, 4096.0,
                                              ALU.mult, ALU.add),
          reads=["en_t"], writes=["gidxf"])
    p.add("vector", lambda e: e.tensor_add(gidxf[:], gidxf[:], p_t[:]),
          reads=["gidxf", "p_t"], writes=["gidxf"])
    p.add("vector", lambda e: e.tensor_copy(gidx[:], gidxf[:]),
          reads=["gidxf"], writes=["gidx"])

    rt_scope.close()
    ffn_scope = ExitStack()
    sf_ = ffn_scope.enter_context
    w1sb = sf_(nc.sbuf_tensor("w1sb", [128, 2 * NCH * 2048], BF16))
    w2sb = sf_(nc.sbuf_tensor("w2sb", [128, 32 * C], BF16))
    hT = sf_(nc.sbuf_tensor("hT", [128, 32 * 512], BF16))
    slin = sf_(nc.sbuf_tensor("slin", [128, 2 * C], BF16))
    slinT = sf_(nc.sbuf_tensor("slinT", [128, 2 * NCH * 512], BF16))
    orow = sf_(nc.sbuf_tensor("orow", [128, C], BF16))
    for eng in ("vector", "scalar", "gpsimd", "sync", "tensor"):
        p.fence(eng, ["ms_sb", "ones8", "cums8", "lgm", "lgex",
                      ("wstg", 0), ("wstg", 1)])

    def dispatch(j):
        selj = sel4[:, j * E:(j + 1) * E].unsqueeze(1).broadcast_to([128, 32, E])
        p.add("vector", lambda e, selj=selj: e.tensor_tensor(
            tmp256[:].rearrange("p (c e) -> p c e", e=E),
            cumtm[:].rearrange("p (c e) -> p c e", e=E), selj, ALU.mult),
            reads=["cumtm", "sel4"], writes=["tmp256"])
        p.add("vector", lambda e: e.reduce_sum(
            mycum[:], tmp256[:].rearrange("p (c e) -> p c e", e=E), axis=AX.X),
            reads=["tmp256"], writes=["mycum"])
        bk = j % 2
        for ch in range(32):
            csl = ch % 2
            p.add("vector", lambda e, ch=ch, csl=csl: e.tensor_scalar(
                cmpb[:, csl * 512:(csl + 1) * 512], iota512[:], mycum[:, ch:ch + 1],
                None, ALU.is_ge),
                reads=["iota512", "mycum"], writes=[("cmpb", csl)])
            p.add("tensor", lambda e, ch=ch, bk=bk, csl=csl: e.matmul(
                qps[bk][0:1, 0:512], ones128b[:], cmpb[:, csl * 512:(csl + 1) * 512],
                start=(ch == 0), stop=(ch == 31)),
                reads=["ones128b", ("cmpb", csl)], writes=[("ps_s", bk)])
        p.add("vector", lambda e, bk=bk: e.tensor_copy(cntf[:], qps[bk][0:1, 0:512]),
              reads=[("ps_s", bk)], writes=["cntf"])
        for q in range(4):
            p.add("tensor", lambda e, q=q: e.transpose(
                qtail[:, 0:1], cntf[0:1, q * 128:(q + 1) * 128],
                id_f[0:1, 0:1]),
                reads=["cntf", "id_f"], writes=["ps_qt"])
            p.add("vector", lambda e: e.tensor_copy(
                nslot[:], qtail[:, 0:1]),
                reads=["ps_qt"], writes=["nslot"])
            p.add("vector", lambda e, j=j, q=q: e.tensor_copy(
                didx[:, j * 4 + q: j * 4 + q + 1], nslot[:]),
                reads=["nslot"], writes=["didx"])
        for q in range(4):
            ssl = q % 2
            sv = slin[:, ssl * C:(ssl + 1) * C]
            p.add("vector", lambda e, sv=sv: e.memset(sv, 0.0), writes=[("slin", ssl)])
            p.add("gpsimd", lambda e, j=j, q=q, sv=sv: e.indirect_dma_start(
                out=sv, out_offset=None, in_=xs,
                in_offset=IndirectOffsetOnAxis(didx[:, j * 4 + q: j * 4 + q + 1], 0),
                bounds_check=T - 1, oob_is_err=False),
                reads=["didx", ("slin", ssl), "xs_dram"], writes=[("slin", ssl)],
                dma=f"disp{ssl}")
            db, dbk = (ptrb, "ps_trb") if q % 2 == 0 else (ptrb2, "ps_trb2")
            for cc in range(NCH):
                p.add("tensor", lambda e, cc=cc, db=db, sv=sv: e.transpose(
                    db[:, cc * 128:(cc + 1) * 128],
                    sv[:, cc * 128: (cc + 1) * 128], id_b[:]),
                    reads=[("slin", ssl), "id_b"], writes=[dbk])
            dst = slinT[:, (j % 2) * NCH * 512:].rearrange(
                "p (c i) -> p c i", i=512)[:, 0:NCH, q * 128:(q + 1) * 128]
            p.add("scalar", lambda e, dst=dst, db=db: e.copy(
                dst, db[:].rearrange("p (c i) -> p c i", i=128)),
                reads=[dbk], writes=[("slinT", j % 2)])

    if debug:
        p.dma("scalar", dbg_ms, ms_sb[:], "dbg", reads=["ms_sb"])
        p.add("vector", lambda e: e.tensor_copy(nslot[:], gidx[:, 0:1]),
              reads=["gidx"], writes=["nslot"])
        def dbg_rt_store(e):
            return e.dma_start(out=dbg_rt, in_=cumtm[:, 0:64])
        p.add("scalar", dbg_rt_store, reads=["cumtm"], dma="dbg")
        p.add("vector", lambda e: e.tensor_copy(gidxf[:], gidxf[:]), reads=["gidxf"])
        def dbg_di_store(e):
            return e.dma_start(out=dbg_di.bitcast(U32) if hasattr(dbg_di, "bitcast") else dbg_di,
                               in_=didx[:])
        p.add("scalar", dbg_di_store, reads=["didx"], dma="dbg")

    hps = [qps[0][:, 0:512], qps[1][:, 0:512]]

    def w1_load(j, hh):
        sl = (2 * j + hh) % 2
        p.add("gpsimd", lambda e, j=j, hh=hh, sl=sl: e.dma_start(
            out=w1sb[:, sl * 16384:(sl + 1) * 16384].rearrange("p (c h) -> p c h", h=2048),
            in_=w1[j, :, hh * 2048:(hh + 1) * 2048].rearrange("(c p) h -> p c h", p=128)),
            writes=[("w1sb", sl)], dma=f"w1l{sl}")

    def w2_load(j, qt):
        sl = (4 * j + qt) % 4
        p.add("gpsimd", lambda e, j=j, qt=qt, sl=sl: e.dma_start(
            out=w2sb[:, sl * 8192:(sl + 1) * 8192].rearrange("p (m c) -> p m c", c=C),
            in_=w2[j, qt * 1024:(qt + 1) * 1024, :].rearrange("(m p) c -> p m c", p=128)),
            writes=[("w2sb", sl)], dma=f"w2l{sl}")

    dispatch(0)
    for hh in range(2):
        w1_load(0, hh)
    for qt in range(4):
        w2_load(0, qt)

    for j in range(E_LOC):
        for hhalf in range(2):
            sl = (2 * j + hhalf) % 2
            for m in range(16):
                mg = hhalf * 16 + m
                bk = m % 2
                for cc in range(NCH):
                    p.add("tensor", lambda e, m=m, cc=cc, bk=bk, sl=sl, j=j: e.matmul(
                        hps[bk],
                        w1sb[:, sl * 16384 + cc * 2048 + m * 128:
                             sl * 16384 + cc * 2048 + (m + 1) * 128],
                        slinT[:, ((j % 2) * NCH + cc) * 512:((j % 2) * NCH + cc + 1) * 512],
                        start=(cc == 0), stop=(cc == NCH - 1)),
                        reads=[("w1sb", sl), ("slinT", j % 2)], writes=[("ps_s", bk)])
                p.add("scalar", lambda e, mg=mg, bk=bk: e.activation(
                    hT[:, mg * 512:(mg + 1) * 512], hps[bk], AF.Gelu_apprx_tanh),
                    reads=[("ps_s", bk)], writes=["hT"])
        if j + 1 < E_LOC:
            dispatch(j + 1)
            w1_load(j + 1, 0)
            w1_load(j + 1, 1)
        op2 = [ps_x[:], ps_av[:]]
        op2k = ["ps_x", "ps_av"]
        for q in range(4):
            for half in range(2):
                for m in range(32):
                    sl = (4 * j + m // 8) % 4
                    p.add("tensor", lambda e, q=q, half=half, m=m, sl=sl: e.matmul(
                        op2[half], hT[:, m * 512 + q * 128: m * 512 + (q + 1) * 128],
                        w2sb[:, sl * 8192 + (m % 8) * C + half * 512:
                             sl * 8192 + (m % 8) * C + half * 512 + 512],
                        start=(m == 0), stop=(m == 31)),
                        reads=["hT", ("w2sb", sl)], writes=[op2k[half]])
            for half in range(2):
                p.add("vector", lambda e, half=half: e.tensor_copy(
                    orow[:, half * 512:(half + 1) * 512], op2[half]),
                    reads=[op2k[half]], writes=["orow"])
            def so_store(e, j=j, q=q):
                return e.dma_start(out=so[ds(hof[0] + j * 512 + q * 128, 128), :],
                                   in_=orow[:])
            p.add("scalar", so_store, reads=["orow"], writes=["so_dram"], dma="sost")

        if j + 1 < E_LOC:
            for qt in range(4):
                w2_load(j + 1, qt)

    if debug:
        for r in range(16):
            p.dma("scalar", dbg_si[r * 128:(r + 1) * 128, :],
                  hT[:, 0:C] if False else slinT[:, 0:512], "dbg", reads=["slinT"])
            break
    p.add("sync", lambda e: e.nop(), reads=["so_dram"])

    p.barrier()

    ffn_scope.close()
    lnsq2 = sbp("lnsq2", [128, C], F32)
    compb = sbp("compb", [128, 16 * C], BF16)
    xres2 = sbp("xres2", [128, 16 * C], BF16)
    yrow = sbp("yrow", [128, 2 * C], F32)
    for eng in ("vector", "scalar", "gpsimd", "sync"):
        p.fence(eng, ["w1sb", "w2sb", "hT", ("slin", 0), ("slin", 1)])
    p.add("vector", lambda e: e.memset(compb[:], 0.0),
          writes=[("compb", t_) for t_ in range(NT)])
    for t in range(NT):
        cpv = compb[:, t * C:(t + 1) * C]
        xrv = xres2[:, t * C:(t + 1) * C]
        p.add("gpsimd", lambda e, t=t, cpv=cpv: e.indirect_dma_start(
            out=cpv, out_offset=None, in_=so,
            in_offset=IndirectOffsetOnAxis(gidx[:, t:t + 1], 0),
            bounds_check=T - 1, oob_is_err=False),
            reads=["gidx", "so_dram"], writes=[("compb", t)],
            dma=f"comb{t}")
        p.dma("sync", xrv, xprime[t * 128:(t + 1) * 128, :], f"xpl{t}",
              reads=["xprime_dram"], writes=[("xres2", t)])
    for t in range(NT):
        pr = t % 2
        cpv = compb[:, t * C:(t + 1) * C]
        xrv = xres2[:, t * C:(t + 1) * C]
        yv = yrow[:, pr * C:(pr + 1) * C]
        ls_ = lsum[:, pr:pr + 1]
        lm_ = lmean[:, pr:pr + 1]
        lv_ = lvar[:, pr:pr + 1]
        l2_ = lm2[:, pr:pr + 1]
        lr_ = lrstd[:, pr:pr + 1]
        p.add("vector", lambda e, t=t, cpv=cpv, xrv=xrv, yv=yv, ls_=ls_: e.scalar_tensor_tensor(
            yv, cpv, scale_t[:, t:t + 1], xrv, ALU.mult, ALU.add,
            accum_out=ls_),
            reads=[("compb", t), "scale_t", ("xres2", t)],
            writes=[("yrow", pr), ("lsum", pr)])
        p.add("vector", lambda e, ls_=ls_, lm_=lm_: e.tensor_scalar_mul(lm_, ls_, 1.0 / C),
              reads=[("lsum", pr)], writes=[("lmean", pr)])
        p.add("scalar", lambda e, yv=yv, lv_=lv_: e.activation(lnsq2[:], yv, AF.Square,
                                                               accum_out=lv_),
              reads=[("yrow", pr)], writes=["lnsq2", ("lvar", pr)])
        p.add("vector", lambda e, lm_=lm_, l2_=l2_: e.tensor_mul(l2_, lm_, lm_),
              reads=[("lmean", pr)], writes=[("lm2", pr)])
        p.add("vector", lambda e, lv_=lv_: e.tensor_scalar(lv_, lv_, 1.0 / C,
                                                           None, ALU.mult),
              reads=[("lvar", pr)], writes=[("lvar", pr)])
        p.add("vector", lambda e, lv_=lv_, l2_=l2_: e.tensor_sub(lv_, lv_, l2_),
              reads=[("lvar", pr), ("lm2", pr)], writes=[("lvar", pr)])
        p.add("scalar", lambda e, lv_=lv_, lr_=lr_: e.activation(lr_, lv_, AF.Sqrt,
                                                                 bias=eps5[:]),
              reads=[("lvar", pr), "eps5"], writes=[("lrstd", pr)])
        p.add("vector", lambda e, lr_=lr_: e.reciprocal(lr_, lr_),
              reads=[("lrstd", pr)], writes=[("lrstd", pr)])
        p.add("vector", lambda e, yv=yv, lm_=lm_, lr_=lr_: e.tensor_scalar(
            yv, yv, lm_, lr_, ALU.subtract, ALU.mult),
            reads=[("yrow", pr), ("lmean", pr), ("lrstd", pr)], writes=[("yrow", pr)])
        p.dma("sync", yout[t * 128:(t + 1) * 128, :], yv, f"yst{pr}", reads=[("yrow", pr)])

    p.emit()
    p.final_waits("sync", ["yst0", "yst1"] + (["dbg"] if debug else []))
    psA.close()
    stack.close()
    nc.finalize()
    return nc


def _perm(n):
    out = []
    for h0 in range(0, n, HD):
        out.extend(range(h0, h0 + HD, 2))
        out.extend(range(h0 + 1, h0 + HD, 2))
    return np.array(out)


def make_in_maps(inputs):
    x = np.asarray(inputs["L0_attn_x"])
    Wq = np.asarray(inputs["L0_attn_Wq"])
    Wk = np.asarray(inputs["L0_attn_Wk"])
    Wv = np.asarray(inputs["L0_attn_Wv"])
    Wo = np.asarray(inputs["L0_attn_Wo"])
    qg = np.asarray(inputs["L0_attn_q_gamma"])
    kg = np.asarray(inputs["L0_attn_k_gamma"])
    Wg = np.asarray(inputs["L0_ffn_Wg"])
    W1 = np.asarray(inputs["L0_ffn_W1"])
    W2 = np.asarray(inputs["L0_ffn_W2"])

    pq = _perm(C)
    pk = _perm(HD)
    Wq_p = np.ascontiguousarray(Wq[pq, :])
    Wk_p = np.ascontiguousarray(Wk[pk, :])
    gq_t = np.tile(qg[pk][None, :], (128, NH)).astype(np.float32)
    gk_t = np.tile(kg[pk][None, :], (128, 1)).astype(np.float32)
    inv = (1.0 / (1.0e6 ** (np.arange(0, HD, 2, dtype=np.float32) / HD))).astype(np.float32)
    ident = np.eye(128, dtype=np.float32)

    ins = []
    for c in range(8):
        b, h = c >> 1, c & 1
        lo = h * T_OWN - HALO
        xs_ = np.zeros((T_KV, C), dtype=np.float32)
        s0 = max(0, lo)
        xs_[s0 - lo:, :] = x[b, s0: lo + T_KV, :]
        pos = (lo + np.arange(T_KV)).astype(np.float32)
        freqs = np.outer(pos, inv).astype(np.float32)
        kb = np.zeros((1, T_KV), dtype=np.float32)
        if h == 0:
            kb[0, :HALO] = NEG
        sel = np.zeros((128, E_LOC * E), dtype=np.float32)
        for j in range(E_LOC):
            sel[:, j * E + (E_LOC * h + j)] = 1.0
        hs = np.zeros((128, 2), dtype=np.float32)
        hs[:, h] = 1.0
        ins.append({
            "x": xs_,
            "wq": Wq_p, "wk": Wk_p, "wv": np.ascontiguousarray(Wv),
            "wo": np.ascontiguousarray(Wo), "wg": np.ascontiguousarray(Wg),
            "w1": np.ascontiguousarray(W1[E_LOC * h: E_LOC * h + E_LOC]),
            "w2": np.ascontiguousarray(W2[E_LOC * h: E_LOC * h + E_LOC]),
            "gq": gq_t, "gk": gk_t,
            "cosb": np.cos(freqs).astype(np.float32),
            "sinb": np.sin(freqs).astype(np.float32),
            "kbias": kb, "sel4": sel, "hsel": hs, "idf": ident,
            "i512": np.tile(np.arange(512, dtype=np.float32)[None, :], (128, 1)),
        })
    return ins


def kernel(**inputs):
    ins = make_in_maps(inputs)
    nc = build()
    res = run_bass_kernel_spmd(nc, ins, core_ids=list(range(8)))
    y = np.stack([res.results[c]["y"] for c in range(8)]).reshape(B, T, C)
    return y


# version 30
# speedup vs baseline: 1.0261x; 1.0261x over previous
import numpy as np
import concourse.bass as bass
import concourse.bacc as bacc
import concourse.mybir as mybir
from concourse.bass import IndirectOffsetOnAxis, ds
from concourse.bass_utils import run_bass_kernel_spmd
from contextlib import ExitStack

F32 = mybir.dt.float32
BF16 = mybir.dt.bfloat16
I32 = mybir.dt.int32
U32 = mybir.dt.uint32
AF = mybir.ActivationFunctionType
ALU = mybir.AluOpType
AX = mybir.AxisListType

B, T, C = 4, 4096, 1024
NH, HD = 4, 256
T_OWN, HALO = 2048, 512
T_KV = T_OWN + HALO
NT = T_OWN // 128
NKV = T_KV // 128
NCH = C // 128
E, E_LOC, CAP = 8, 4, 512
H_FFN = 4096
NEG = -1.0e30
ENGS = ("sync", "vector", "scalar", "tensor", "gpsimd")


class Prog:
    def __init__(self, nc):
        self.nc = nc
        self.ops = []
        self.phases = []
        self._cur = []
        self.dma_counts = {}
        self._pending_fence = {}

    def add(self, engine, fn, reads=(), writes=(), dma=None):
        if dma is not None:
            dma = f"{dma}_{engine}"
        writes = tuple(writes) + tuple(self._pending_fence.pop(engine, ()))
        op = dict(engine=engine, fn=fn, reads=tuple(reads), writes=tuple(writes),
                  dma=dma, idx=len(self.ops))
        if dma is not None:
            self.dma_counts[dma] = self.dma_counts.get(dma, 0) + 1
            op["dma_count"] = self.dma_counts[dma]
        self.ops.append(op)
        self._cur.append(op)
        return op

    def dma(self, engine, out, in_, tag, reads=(), writes=()):
        self.add(engine, lambda e, o=out, i=in_: e.dma_start(out=o, in_=i),
                 reads=reads, writes=writes, dma=tag)

    def fence(self, engine, regions):
        self._pending_fence[engine] = tuple(self._pending_fence.get(engine, ())) + tuple(regions)

    def barrier(self):
        if self._cur:
            self.phases.append(("ops", self._cur))
            self._cur = []
        self.phases.append(("barrier", None))

    def emit(self):
        nc = self.nc
        if self._cur:
            self.phases.append(("ops", self._cur))
            self._cur = []
        last_writer, readers = {}, {}
        for op in self.ops:
            deps = set()
            for r in op["reads"]:
                w = last_writer.get(r)
                if w is not None:
                    deps.add(w["idx"])
                readers.setdefault(r, []).append(op)
            for wr in op["writes"]:
                w = last_writer.get(wr)
                if w is not None:
                    deps.add(w["idx"])
                for rd in readers.get(wr, ()):
                    deps.add(rd["idx"])
                readers[wr] = []
                last_writer[wr] = op
            deps.discard(op["idx"])
            pruned = []
            for d in deps:
                pd = self.ops[d]
                if (pd["dma"] is None and op["dma"] is None
                        and pd["engine"] == "tensor" and op["engine"] == "tensor"):
                    continue
                pruned.append(d)
            best = {}
            for d in pruned:
                pd = self.ops[d]
                key = ("d", pd["dma"]) if pd["dma"] is not None else ("e", pd["engine"])
                if key not in best or d > best[key]:
                    best[key] = d
            op["deps"] = sorted(best.values())
        need_inc = set()
        for op in self.ops:
            for d in op["deps"]:
                if self.ops[d]["dma"] is None:
                    need_inc.add(d)
        cnt = {e: 0 for e in ENGS}
        for op in self.ops:
            if op["dma"] is None:
                op["do_inc"] = op["idx"] in need_inc
                if op["do_inc"]:
                    cnt[op["engine"]] += 1
                op["inc_val"] = cnt[op["engine"]]
        esem = {e: nc.alloc_semaphore(f"es_{e}") for e in ENGS}
        dsem = {t: nc.alloc_semaphore(f"ds_{t}") for t in self.dma_counts}
        self.esem, self.dsem = esem, dsem
        for op in self.ops:
            waits = {}
            for d in op["deps"]:
                pd = self.ops[d]
                if pd["dma"] is not None:
                    sem, val = dsem[pd["dma"]], 16 * pd["dma_count"]
                else:
                    sem, val = esem[pd["engine"]], pd["inc_val"]
                k = id(sem)
                if k not in waits or waits[k][1] < val:
                    waits[k] = (sem, val)
            op["waits"] = list(waits.values())
        watermark = {}
        for kind, payload in self.phases:
            if kind == "barrier":
                nc.all_core_barrier()
                continue
            by_eng = {}
            for op in payload:
                by_eng.setdefault(op["engine"], []).append(op)
            with nc.Block() as blk:
                for ename, ops_ in by_eng.items():
                    def mk(ename, ops_):
                        def body(eng):
                            for op in ops_:
                                for sem, val in op["waits"]:
                                    if watermark.get((ename, id(sem)), 0) < val:
                                        eng.wait_ge(sem, val)
                                        watermark[(ename, id(sem))] = val
                                inst = op["fn"](eng)
                                if op["dma"] is not None:
                                    inst.then_inc(self.dsem[op["dma"]], 16)
                                elif op["do_inc"]:
                                    inst.then_inc(self.esem[ename], 1)
                        return body
                    getattr(blk, ename)(mk(ename, ops_))

    def final_waits(self, engine, tags):
        full = [t2 for t2 in self.dma_counts for t in tags if t2.startswith(t)]
        def fw(eng):
            for t in full:
                eng.wait_ge(self.dsem[t], 16 * self.dma_counts[t])
        with self.nc.Block() as blk:
            getattr(blk, engine)(fw)


def build(debug=False, num_devices=8):
    nc = bacc.Bacc("TRN2", target_bir_lowering=False, debug=False, num_devices=num_devices)

    xin = nc.dram_tensor("x", [T_KV, C], F32, kind="ExternalInput").ap()
    wq = nc.dram_tensor("wq", [C, C], F32, kind="ExternalInput").ap()
    wk = nc.dram_tensor("wk", [HD, C], F32, kind="ExternalInput").ap()
    wv = nc.dram_tensor("wv", [HD, C], F32, kind="ExternalInput").ap()
    wo = nc.dram_tensor("wo", [C, C], F32, kind="ExternalInput").ap()
    wgi = nc.dram_tensor("wg", [C, E], F32, kind="ExternalInput").ap()
    w1 = nc.dram_tensor("w1", [E_LOC, C, H_FFN], F32, kind="ExternalInput").ap()
    w2 = nc.dram_tensor("w2", [E_LOC, H_FFN, C], F32, kind="ExternalInput").ap()
    gq_in = nc.dram_tensor("gq", [128, C], F32, kind="ExternalInput").ap()
    gk_in = nc.dram_tensor("gk", [128, HD], F32, kind="ExternalInput").ap()
    cos_in = nc.dram_tensor("cosb", [T_KV, 128], F32, kind="ExternalInput").ap()
    sin_in = nc.dram_tensor("sinb", [T_KV, 128], F32, kind="ExternalInput").ap()
    kb_in = nc.dram_tensor("kbias", [1, T_KV], F32, kind="ExternalInput").ap()
    sel_in = nc.dram_tensor("sel4", [128, E_LOC * E], F32, kind="ExternalInput").ap()
    hsel_in = nc.dram_tensor("hsel", [128, 2], F32, kind="ExternalInput").ap()
    idf_in = nc.dram_tensor("idf", [128, 128], F32, kind="ExternalInput").ap()
    i512_in = nc.dram_tensor("i512", [128, 512], F32, kind="ExternalInput").ap()
    yout = nc.dram_tensor("y", [T_OWN, C], F32, kind="ExternalOutput").ap()

    xprime = nc.dram_tensor("xprime", [T_OWN, C], BF16,
                            kind=("ExternalOutput" if debug else "Internal")).ap()
    xs = nc.dram_tensor("xs_sh", [T, C], BF16, kind="Internal", addr_space="Shared").ap()
    msh = nc.dram_tensor("ms_sh", [E, T], BF16, kind="Internal", addr_space="Shared").ap()
    so = nc.dram_tensor("so_sh", [T, C], BF16, kind="Internal", addr_space="Shared").ap()
    if debug:
        dbg_ms = nc.dram_tensor("dbg_ms", [E, T], F32, kind="ExternalOutput").ap()
        dbg_rt = nc.dram_tensor("dbg_rt", [128, 64], F32, kind="ExternalOutput").ap()
        dbg_di = nc.dram_tensor("dbg_di", [128, 16], F32, kind="ExternalOutput").ap()
        dbg_si = nc.dram_tensor("dbg_si", [E_LOC * CAP, C], F32, kind="ExternalOutput").ap()

    p = Prog(nc)
    hof = {}

    def snap_hof(e):
        hof[0] = e.snap((e.partition_id() % 2) * T_OWN, min_val=0, max_val=T_OWN)

    def snap_hofg(e):
        hof[1] = e.snap((e.partition_id() % 2) * T_OWN, min_val=0, max_val=T_OWN)
        return e.nop()
    stack = ExitStack()
    sbp = lambda n, s, d: nc.alloc_sbuf_tensor(n, s, d)
    sbs = lambda n, s, d: stack.enter_context(nc.sbuf_tensor(n, s, d))

    id_f = sbp("id_f", [128, 128], F32)
    id_b = sbp("id_b", [128, 128], BF16)
    ones1b = sbp("ones1b", [1, 128], BF16)
    ones128b = sbp("ones128b", [128, 1], BF16)
    iotadesc = sbp("iotadesc", [128, 128], F32)
    sel4 = sbp("sel4_sb", [128, E_LOC * E], F32)
    hsel = sbp("hsel_sb", [128, 2], F32)
    eps6 = sbp("eps6", [128, 1], F32)
    eps5 = sbp("eps5", [128, 1], F32)
    mask_tm = sbp("mask_tm", [128, NT * E], BF16)
    lgall = sbp("lgall", [128, NT * E], F32)
    gate_t = sbp("gate_t", [128, NT], F32)
    scale_t = sbp("scale_t", [128, NT], F32)
    gidx = sbp("gidx", [128, NT], U32)
    didx = sbp("didx", [128, 16], U32)
    lsum = sbp("lsum", [128, 2], F32)
    lmean = sbp("lmean", [128, 2], F32)
    lvar = sbp("lvar", [128, 2], F32)
    lm2 = sbp("lm2", [128, 2], F32)
    lrstd = sbp("lrstd", [128, 2], F32)
    tmpa = sbp("tmpa", [128, E], F32)
    tmpb2 = sbp("tmpb2", [128, E], F32)

    ptrb = nc.alloc_psum_tensor("ps_trb", [128, 1024], BF16)
    ptrb2 = nc.alloc_psum_tensor("ps_trb2", [128, 1024], BF16)

    p.add("scalar", snap_hof)
    p.add("gpsimd", snap_hofg)
    p.dma("sync", id_f[:], idf_in, "l_idf", writes=["id_f"])
    p.dma("gpsimd", id_b[:], idf_in, "l_idb", writes=["id_b"])
    p.dma("sync", sel4[:], sel_in, "l_sel", writes=["sel4"])
    p.dma("sync", hsel[:], hsel_in, "l_hsel", writes=["hsel"])
    for tile, val in ((ones1b, 1.0), (ones128b, 1.0),
                      (eps6, 1e-6), (eps5, 1e-5)):
        p.add("vector", lambda e, tl=tile, v=val: e.memset(tl[:], v),
              writes=[tile.name])
    p.add("gpsimd", lambda e: e.iota(iotadesci[:], pattern=[[0, 16], [-1, 8]], base=8,
                                     channel_multiplier=0), writes=["iotadesci"])
    p.add("vector", lambda e: e.tensor_copy(iotadesc[:], iotadesci[:]),
          reads=["iotadesci"], writes=["iotadesc"])

    attn_outer = ExitStack()
    so_ = attn_outer.enter_context
    kbias_b = so_(nc.sbuf_tensor("kbias_b", [1, T_KV], BF16))
    umask = so_(nc.sbuf_tensor("umask", [128, 128], F32))
    lmask = so_(nc.sbuf_tensor("lmask", [128, 128], F32))
    mTst = so_(nc.sbuf_tensor("mTst", [8, 128], BF16))
    iotadesci = so_(nc.sbuf_tensor("iotadesci", [128, 128], I32))
    gq_sb = so_(nc.sbuf_tensor("gq_sb", [128, C], BF16))
    gk_sb = so_(nc.sbuf_tensor("gk_sb", [128, HD], BF16))
    cosb = so_(nc.sbuf_tensor("cosb_sb", [128, T_KV], BF16))
    sinb = so_(nc.sbuf_tensor("sinb_sb", [128, T_KV], BF16))
    qT = so_(nc.sbuf_tensor("qT", [128, NH * 2 * T_OWN], BF16))
    kT = so_(nc.sbuf_tensor("kT", [128, 2 * T_KV], BF16))
    vsb = so_(nc.sbuf_tensor("vsb", [128, NKV * HD], BF16))
    wstg = so_(nc.sbuf_tensor("wstg", [128, 2 * C], BF16))
    rowmax = so_(nc.sbuf_tensor("rowmax", [128, 2], F32))
    negmax = so_(nc.sbuf_tensor("negmax", [128, 2], F32))
    denom = so_(nc.sbuf_tensor("denom", [128, 2], F32))
    rden = so_(nc.sbuf_tensor("rden", [128, 2], F32))
    xres = so_(nc.sbuf_tensor("xres", [128, 2 * C], F32))
    lnr = so_(nc.sbuf_tensor("lnr", [128, 2 * C], F32))
    xpb = so_(nc.sbuf_tensor("xpb", [128, 2 * C], BF16))
    xpTs = so_(nc.sbuf_tensor("xpTs", [128, C], BF16))
    lnsq = so_(nc.sbuf_tensor("lnsq", [128, C], F32))
    m8 = so_(nc.sbuf_tensor("m8", [128, 1], F32))
    dnm = so_(nc.sbuf_tensor("dnm", [128, 1], F32))
    pr_t = so_(nc.sbuf_tensor("pr_t", [128, E], F32))
    pmax = so_(nc.sbuf_tensor("pmax", [128, 1], F32))

    psA = ExitStack()
    qps = [psA.enter_context(nc.psum_tensor(f"ps_s{i}", [128, 640], F32)) for i in range(2)]
    ps_av = psA.enter_context(nc.psum_tensor("ps_av", [128, 512], F32))
    ps_x = psA.enter_context(nc.psum_tensor("ps_x", [128, 512], F32))
    lgps = ps_av[:, 128:136]

    p.dma("gpsimd", gq_sb[:], gq_in, "l_gq", writes=["gq_sb"])
    p.dma("gpsimd", gk_sb[:], gk_in, "l_gk", writes=["gk_sb"])
    p.dma("gpsimd", kbias_b[:], kb_in, "l_kb", writes=["kbias_b"])
    for tile, val in ((umask, 0.0), (lmask, 0.0)):
        p.add("vector", lambda e, tl=tile, v=val: e.memset(tl[:], v), writes=[tile.name])
    p.add("gpsimd", lambda e: e.affine_select(umask[:], umask[:], pattern=[[1, 128]],
                                              compare_op=ALU.is_ge, fill=NEG, base=0,
                                              channel_multiplier=-1),
          reads=["umask"], writes=["umask"])
    p.add("gpsimd", lambda e: e.affine_select(lmask[:], lmask[:], pattern=[[-1, 128]],
                                              compare_op=ALU.is_ge, fill=NEG, base=0,
                                              channel_multiplier=1),
          reads=["lmask"], writes=["lmask"])
    p.dma("gpsimd", cosb[:].rearrange("p (t i) -> p t i", i=128),
          cos_in.rearrange("(t p) i -> p t i", p=128), "l_cos", writes=["cosb"])
    p.dma("gpsimd", sinb[:].rearrange("p (t i) -> p t i", i=128),
          sin_in.rearrange("(t p) i -> p t i", p=128), "l_sin", writes=["sinb"])

    attn_inner = ExitStack()
    si_ = attn_inner.enter_context
    wqT = si_(nc.sbuf_tensor("wqT", [128, NCH * C], BF16))
    wkT = si_(nc.sbuf_tensor("wkT", [128, NCH * HD], BF16))
    wvT = si_(nc.sbuf_tensor("wvT", [128, NCH * HD], BF16))
    xT = si_(nc.sbuf_tensor("xT", [128, NCH * 1024], BF16))
    qf = si_(nc.sbuf_tensor("qf", [128, 2 * C], F32))
    qsq = si_(nc.sbuf_tensor("qsq", [128, HD], F32))
    qrt = si_(nc.sbuf_tensor("qrt", [128, C], BF16))
    qb = si_(nc.sbuf_tensor("qb", [128, 4 * C], BF16))
    ms4 = si_(nc.sbuf_tensor("ms4", [128, 2 * NH], F32))
    rstd4 = si_(nc.sbuf_tensor("rstd4", [128, 2 * NH], F32))


    def load_transposed(wsrc, wdst, dname, nrow_chunks, ncol_chunks, nout):
        for rc in range(nrow_chunks):
            sl = rc % 2
            bk_, bkk = (ptrb, "ps_trb") if rc % 2 == 0 else (ptrb2, "ps_trb2")
            p.dma("gpsimd", wstg[:, sl * C: sl * C + C], wsrc[rc * 128:(rc + 1) * 128, :],
                  f"wld{sl}", writes=[("wstg", sl)])
            for cc in range(ncol_chunks):
                p.add("tensor", lambda e, cc=cc, sl=sl, bk_=bk_: e.transpose(
                    bk_[:, cc * 128:(cc + 1) * 128],
                    wstg[:, sl * C + cc * 128: sl * C + (cc + 1) * 128], id_b[:]),
                    reads=[("wstg", sl), "id_b"], writes=[bkk])
            dst = wdst[:].rearrange("p (c i) -> p c i", i=nout)[:, :, rc * 128:(rc + 1) * 128]
            p.add("vector", lambda e, dst=dst, ncol_chunks=ncol_chunks, bk_=bk_: e.tensor_copy(
                dst, bk_[:, 0:ncol_chunks * 128].rearrange("p (c i) -> p c i", i=128)),
                reads=[bkk], writes=[dname])

    load_transposed(wq, wqT, "wqT", NCH, NCH, C)
    load_transposed(wk, wkT, "wkT", HD // 128, NCH, HD)
    load_transposed(wv, wvT, "wvT", HD // 128, NCH, HD)

    xbf2 = si_(nc.sbuf_tensor("xbf2", [128, 2 * C], BF16))

    def dma_x(t):
        sl = t % 2
        p.dma("gpsimd", xbf2[:, sl * C: sl * C + C], xin[t * 128:(t + 1) * 128, :], f"xl{sl}",
              writes=[("xbf2", sl)])

    def trx(t):
        sl = t % 2
        xsl = t % 8
        for cc in range(NCH):
            p.add("tensor", lambda e, cc=cc, sl=sl: e.transpose(
                ptrb2[:, cc * 128:(cc + 1) * 128],
                xbf2[:, sl * C + cc * 128: sl * C + (cc + 1) * 128], id_b[:]),
                reads=[("xbf2", sl), "id_b"], writes=["ps_trb2"])
        dst = xT[:].rearrange("p (c i) -> p c i", i=1024)[:, :, xsl * 128:(xsl + 1) * 128]
        p.add("scalar", lambda e, dst=dst: e.copy(dst, ptrb2[:].rearrange("p (c i) -> p c i", i=128)),
              reads=["ps_trb2"], writes=[("xT", xsl)])

    def qkv_mm(t, kv, par):
        if kv:
            wname, wT_, nout = "wkT", wkT, HD
            banks, keys = [ps_av], ["ps_av"]
        else:
            wname, wT_, nout = "wqT", wqT, C
            banks, keys = [qps[0], qps[1]], [("ps_s", 0), ("ps_s", 1)]
        nb = (nout + 511) // 512
        xsl = (t if kv else t + 4) % 8
        for b_ in range(nb):
            n0 = b_ * 512
            nw = min(512, nout - n0)
            for cc in range(NCH):
                p.add("tensor", lambda e, bk=banks[b_], cc=cc, n0=n0, nw=nw, xsl=xsl, wT_=wT_, nout=nout: e.matmul(
                    bk[:, 0:nw],
                    xT[:, cc * 1024 + xsl * 128: cc * 1024 + (xsl + 1) * 128],
                    wT_[:, cc * nout + n0: cc * nout + n0 + nw],
                    start=(cc == 0), stop=(cc == NCH - 1)),
                    reads=[("xT", xsl), wname], writes=[keys[b_]])
        for b_ in range(nb):
            n0 = b_ * 512
            nw = min(512, nout - n0)
            p.add("scalar", lambda e, bk=banks[b_], n0=n0, nw=nw, par=par: e.copy(
                qf[:, par * C + n0: par * C + n0 + nw], bk[:, 0:nw]),
                reads=[keys[b_]], writes=[("qf", par)])

    def v_mm(t):
        xsl = t % 8
        for cc in range(NCH):
            p.add("tensor", lambda e, cc=cc, xsl=xsl: e.matmul(
                ps_x[:, 0:HD], xT[:, cc * 1024 + xsl * 128: cc * 1024 + (xsl + 1) * 128],
                wvT[:, cc * HD:(cc + 1) * HD], start=(cc == 0), stop=(cc == NCH - 1)),
                reads=[("xT", xsl), "wvT"], writes=["ps_x"])
        p.add("scalar", lambda e, t=t: e.copy(vsb[:, t * HD:(t + 1) * HD], ps_x[:, 0:HD]),
              reads=["ps_x"], writes=["vsb"])

    def qkv_nr(t, kv, par, par4):
        nout = HD if kv else C
        nh_loc = 1 if kv else NH
        tok0 = t * 128 + (0 if kv else HALO)
        o = par * C
        qfv = qf[:, o:o + nout]
        qbv = qb[:, par4 * C: par4 * C + nout]
        msv = ms4[:, par * NH: par * NH + nh_loc]
        rsv = rstd4[:, par * NH: par * NH + nh_loc]
        for hh in range(nh_loc):
            p.add("scalar", lambda e, hh=hh: e.activation(
                qsq[:, 0:HD], qfv[:, hh * HD:(hh + 1) * HD], AF.Square,
                accum_out=msv[:, hh:hh + 1]),
                reads=[("qf", par)], writes=["qsq_dump", ("ms4", par)])
        p.add("scalar", lambda e: e.activation(rsv, msv, AF.Sqrt, scale=1.0 / HD, bias=eps6[:]),
              reads=[("ms4", par), "eps6"], writes=[("rstd4", par)])
        p.add("vector", lambda e: e.reciprocal(rsv, rsv),
              reads=[("rstd4", par)], writes=[("rstd4", par)])
        for hh in range(nh_loc):
            gam = gk_sb[:, 0:HD] if kv else gq_sb[:, hh * HD:(hh + 1) * HD]
            p.add("vector", lambda e, hh=hh, gam=gam: e.scalar_tensor_tensor(
                qbv[:, hh * HD:(hh + 1) * HD], qfv[:, hh * HD:(hh + 1) * HD],
                rsv[:, hh:hh + 1], gam, ALU.mult, ALU.mult),
                reads=[("qf", par), ("rstd4", par), "gk_sb" if kv else "gq_sb"],
                writes=[("qb", par4)])
        ct = cosb[:, tok0: tok0 + 128]
        st_ = sinb[:, tok0: tok0 + 128]
        qv = qbv.rearrange("p (h two i) -> p h two i", two=2, i=128)
        Av = qv[:, :, 0, :]
        Bv = qv[:, :, 1, :]
        ctb = ct.unsqueeze(1).broadcast_to([128, nh_loc, 128])
        stb = st_.unsqueeze(1).broadcast_to([128, nh_loc, 128])
        u3 = qrt[:, 0:nh_loc * 128].rearrange("p (h i) -> p h i", i=128)
        v3 = qrt[:, 512:512 + nh_loc * 128].rearrange("p (h i) -> p h i", i=128)
        p.add("vector", lambda e: e.tensor_tensor(u3, Av, stb, ALU.mult),
              reads=[("qb", par4), "sinb"], writes=["qrt"])
        p.add("vector", lambda e: e.tensor_tensor(Av, Av, ctb, ALU.mult),
              reads=[("qb", par4), "cosb"], writes=[("qb", par4)])
        p.add("vector", lambda e: e.tensor_tensor(v3, Bv, stb, ALU.mult),
              reads=[("qb", par4), "sinb"], writes=["qrt"])
        p.add("vector", lambda e: e.tensor_tensor(Bv, Bv, ctb, ALU.mult),
              reads=[("qb", par4), "cosb"], writes=[("qb", par4)])
        p.add("vector", lambda e: e.tensor_tensor(Av, Av, v3, ALU.subtract),
              reads=[("qb", par4), "qrt"], writes=[("qb", par4)])
        p.add("vector", lambda e: e.tensor_tensor(Bv, Bv, u3, ALU.add),
              reads=[("qb", par4), "qrt"], writes=[("qb", par4)])

    def qkv_tr(t, kv, par4):
        o = par4 * C
        n = 2 if kv else 8
        for u in range(n):
            p.add("tensor", lambda e, u=u: e.transpose(
                ptrb[:, u * 128:(u + 1) * 128], qb[:, o + u * 128: o + (u + 1) * 128],
                id_b[:]),
                reads=[("qb", par4), "id_b"], writes=["ps_trb"])
        if kv:
            dst = kT[:].rearrange("p (u i) -> p u i", i=T_KV)[:, :, t * 128:(t + 1) * 128]
            p.add("scalar", lambda e, dst=dst: e.copy(
                dst, ptrb[:, 0:256].rearrange("p (u i) -> p u i", i=128)),
                reads=["ps_trb"], writes=["kT"])
        else:
            dst = qT[:].rearrange("p (u i) -> p u i", i=T_OWN)[:, :, t * 128:(t + 1) * 128]
            p.add("scalar", lambda e, dst=dst: e.copy(
                dst, ptrb[:].rearrange("p (u i) -> p u i", i=128)),
                reads=["ps_trb"], writes=["qT"])

    units = []
    for t in range(5):
        units.append((t, True))
    for t in range(5, NKV):
        units.append((t - 5, False))
        units.append((t, True))
    units.append((15, False))
    dma_x(0)
    dma_x(1)
    trx(0)
    pend = []
    for i, (t, kv) in enumerate(units):
        par = i % 2
        par4 = i % 4
        qkv_mm(t, kv, par)
        if kv:
            v_mm(t)
        if len(pend) >= 2:
            qkv_tr(*pend.pop(0))
        nxt = units[i + 1] if i + 1 < len(units) else None
        if nxt is not None and nxt[1]:
            tn = nxt[0]
            if tn + 1 < NKV:
                dma_x(tn + 1)
            trx(tn)
        qkv_nr(t, kv, par, par4)
        pend.append((t, kv, par4))
    for pp in pend:
        qkv_tr(*pp)

    attn_inner.close()
    aoT = so_(nc.sbuf_tensor("aoT", [128, NCH * T_OWN], BF16))
    probs_f = so_(nc.sbuf_tensor("probs_f", [128, 2 * 640], F32))
    probsN = so_(nc.sbuf_tensor("probsN", [128, 4 * 640], BF16))
    probsTT = so_(nc.sbuf_tensor("probsTT", [128, 2 * 640], BF16))

    woT = so_(nc.sbuf_tensor("woT", [128, NCH * C], BF16))
    wg_sb = so_(nc.sbuf_tensor("wg_sb", [128, NCH * E], BF16))
    FREED = ["xT", "wqT", "wkT", "wvT", ("xbf2", 0), ("xbf2", 1), "qsq_dump", "qrt"] + [
        (n, par) for n in ("qf", "ms4", "rstd4") for par in (0, 1)] + [
        ("qb", s) for s in range(4)]
    for eng in ("scalar", "vector", "gpsimd", "tensor"):
        p.fence(eng, FREED)

    load_transposed(wo, woT, "woT", NCH, NCH, C)
    p.dma("gpsimd", wg_sb[:].rearrange("p (c e) -> p c e", e=E),
          wgi.rearrange("(c p) e -> p c e", p=128), "l_wg", writes=["wg_sb"])

    def att_sc(t, hh, pi):
        sp = qps[pi]
        spk = ("ps_s", pi)
        k0 = t * 128
        for u in range(2):
            p.add("tensor", lambda e, u=u, k0=k0, t=t, hh=hh, sp=sp: e.matmul(
                sp[:, 0:512],
                qT[:, (hh * 2 + u) * T_OWN + t * 128:(hh * 2 + u) * T_OWN + (t + 1) * 128],
                kT[:, u * T_KV + k0: u * T_KV + k0 + 512],
                start=(u == 0), stop=False),
                reads=["qT", "kT"], writes=[spk])
            p.add("tensor", lambda e, u=u, k0=k0, t=t, hh=hh, sp=sp: e.matmul(
                sp[:, 512:640],
                qT[:, (hh * 2 + u) * T_OWN + t * 128:(hh * 2 + u) * T_OWN + (t + 1) * 128],
                kT[:, u * T_KV + k0 + 512: u * T_KV + k0 + 640],
                start=(u == 0), stop=False),
                reads=["qT", "kT"], writes=[spk])
        p.add("tensor", lambda e, k0=k0, sp=sp: e.matmul(
            sp[:, 0:512], ones1b[:], kbias_b[:, k0:k0 + 512], start=False, stop=True),
            reads=["ones1b", "kbias_b"], writes=[spk])
        p.add("tensor", lambda e, k0=k0, sp=sp: e.matmul(
            sp[:, 512:640], ones1b[:], kbias_b[:, k0 + 512:k0 + 640],
            start=False, stop=True),
            reads=["ones1b", "kbias_b"], writes=[spk])

    def att_sm(t, hh, pi, pi4):
        sp = qps[pi]
        spk = ("ps_s", pi)
        pf = probs_f[:, pi * 640:(pi + 1) * 640]
        pn = probsN[:, pi4 * 640:(pi4 + 1) * 640]
        rmx = rowmax[:, pi:pi + 1]
        nmx = negmax[:, pi:pi + 1]
        dnm_ = denom[:, pi:pi + 1]
        rdn = rden[:, pi:pi + 1]
        p.add("vector", lambda e, sp=sp: e.tensor_add(sp[:, 0:128], sp[:, 0:128], umask[:]),
              reads=[spk, "umask"], writes=[spk])
        p.add("vector", lambda e, sp=sp: e.tensor_add(sp[:, 512:640], sp[:, 512:640], lmask[:]),
              reads=[spk, "lmask"], writes=[spk])
        p.add("vector", lambda e, sp=sp, rmx=rmx: e.reduce_max(rmx, sp[:], axis=AX.X),
              reads=[spk], writes=[("rowmax", pi)])
        p.add("vector", lambda e, rmx=rmx, nmx=nmx: e.tensor_scalar_mul(nmx, rmx, -0.0625),
              reads=[("rowmax", pi)], writes=[("negmax", pi)])
        p.add("scalar", lambda e, sp=sp, pf=pf, nmx=nmx: e.activation(
            pf, sp[:], AF.Exp, scale=0.0625, bias=nmx),
            reads=[spk, ("negmax", pi)], writes=[("probs_f", pi)])

    def att_sm2(t, hh, pi, pi4):
        pf = probs_f[:, pi * 640:(pi + 1) * 640]
        pn = probsN[:, pi4 * 640:(pi4 + 1) * 640]
        dnm_ = denom[:, pi:pi + 1]
        rdn = rden[:, pi:pi + 1]
        p.add("vector", lambda e, pf=pf, dnm_=dnm_: e.reduce_sum(dnm_, pf, axis=AX.X),
              reads=[("probs_f", pi)], writes=[("denom", pi)])
        p.add("vector", lambda e, dnm_=dnm_, rdn=rdn: e.reciprocal(rdn, dnm_),
              reads=[("denom", pi)], writes=[("rden", pi)])
        p.add("vector", lambda e, pf=pf, pn=pn, rdn=rdn: e.tensor_scalar(
            pn, pf, rdn, None, ALU.mult),
            reads=[("probs_f", pi), ("rden", pi)], writes=[("probsN", pi4)])

    def att_trav(t, hh, pi, pi4):
        pn = probsN[:, pi4 * 640:(pi4 + 1) * 640]
        ptt = probsTT[:, pi * 640:(pi + 1) * 640]
        avt = ps_av if pi == 0 else ps_x
        avk = "ps_av" if pi == 0 else "ps_x"
        bk_, bkk = (ptrb, "ps_trb") if pi == 0 else (ptrb2, "ps_trb2")
        for kt in range(5):
            p.add("tensor", lambda e, kt=kt, pn=pn, bk_=bk_: e.transpose(
                bk_[:, kt * 128:(kt + 1) * 128], pn[:, kt * 128:(kt + 1) * 128], id_b[:]),
                reads=[("probsN", pi4), "id_b"], writes=[bkk])
        p.add("scalar", lambda e, ptt=ptt, bk_=bk_: e.copy(ptt[:, 0:640], bk_[:, 0:640]),
              reads=[bkk], writes=[("probsTT", pi)])
        for u in range(2):
            for kt in range(5):
                p.add("tensor", lambda e, u=u, kt=kt, t=t, ptt=ptt, avt=avt: e.matmul(
                    avt[:, u * 128:(u + 1) * 128],
                    vsb[:, (t + kt) * HD + u * 128: (t + kt) * HD + (u + 1) * 128],
                    ptt[:, kt * 128:(kt + 1) * 128],
                    start=(u == 0 and kt == 0), stop=(u == 1 and kt == 4)),
                    reads=["vsb", ("probsTT", pi)], writes=[avk])
        for u in range(2):
            p.add("scalar", lambda e, t=t, hh=hh, u=u, avt=avt: e.copy(
                aoT[:, (hh * 2 + u) * T_OWN + t * 128:
                    (hh * 2 + u) * T_OWN + (t + 1) * 128],
                avt[:, u * 128:(u + 1) * 128]),
                reads=[avk], writes=["aoT"])

    aunits = [(t, hh) for t in range(NT) for hh in range(NH)]
    pend_a = []
    for i, (t, hh) in enumerate(aunits):
        pi = i % 2
        pi4 = i % 4
        att_sc(t, hh, pi)
        att_sm(t, hh, pi, pi4)
        if pend_a:
            att_sm2(*pend_a[-1])
        if len(pend_a) >= 2:
            att_trav(*pend_a.pop(0))
        pend_a.append((t, hh, pi, pi4))
    att_sm2(*pend_a[-1])
    for pa in pend_a:
        att_trav(*pa)

    def ph3_wo(t, pr):
        for b_ in range(2):
            for cc in range(NCH):
                p.add("tensor", lambda e, b_=b_, cc=cc, t=t: e.matmul(
                    qps[b_][:, 0:512], aoT[:, cc * T_OWN + t * 128: cc * T_OWN + (t + 1) * 128],
                    woT[:, cc * C + b_ * 512: cc * C + b_ * 512 + 512],
                    start=(cc == 0), stop=(cc == NCH - 1)),
                    reads=["aoT", "woT"], writes=[("ps_s", b_)])

    def ph3_ln(t, pr):
        o = pr * C
        xrv = xres[:, o:o + C]
        lnv = lnr[:, o:o + C]
        lsq = lnsq[:, 0:C]
        xpv = xpb[:, o:o + C]
        ls_ = lsum[:, pr:pr + 1]
        lm_ = lmean[:, pr:pr + 1]
        lv_ = lvar[:, pr:pr + 1]
        l2_ = lm2[:, pr:pr + 1]
        lr_ = lrstd[:, pr:pr + 1]
        p.dma("sync", xrv, xin[HALO + t * 128: HALO + (t + 1) * 128, :], f"xr{pr}",
              writes=[("xres", pr)])
        ls2 = lm2[:, pr:pr + 1]
        for b_ in range(2):
            acc = ls_ if b_ == 0 else ls2
            p.add("vector", lambda e, b_=b_, xrv=xrv, lnv=lnv, acc=acc: e.scalar_tensor_tensor(
                lnv[:, b_ * 512:(b_ + 1) * 512], xrv[:, b_ * 512:(b_ + 1) * 512], 1.0,
                qps[b_][:, 0:512], ALU.mult, ALU.add, accum_out=acc),
                reads=[("xres", pr), ("ps_s", b_)], writes=[("lnr", pr), ("lsum", pr)])
        p.add("vector", lambda e, ls_=ls_, ls2=ls2: e.tensor_add(ls_, ls_, ls2),
              reads=[("lsum", pr)], writes=[("lsum", pr)])
        p.add("vector", lambda e, ls_=ls_, lm_=lm_: e.tensor_scalar_mul(lm_, ls_, 1.0 / C),
              reads=[("lsum", pr)], writes=[("lmean", pr)])
        p.add("scalar", lambda e, lnv=lnv, lsq=lsq, lv_=lv_: e.activation(
            lsq, lnv, AF.Square, accum_out=lv_),
            reads=[("lnr", pr)], writes=[("lnsq", pr), ("lvar", pr)])
        p.add("vector", lambda e, lm_=lm_, l2_=l2_: e.tensor_mul(l2_, lm_, lm_),
              reads=[("lmean", pr)], writes=[("lm2", pr)])
        p.add("vector", lambda e, lv_=lv_: e.tensor_scalar(lv_, lv_, 1.0 / C, None, ALU.mult),
              reads=[("lvar", pr)], writes=[("lvar", pr)])
        p.add("vector", lambda e, lv_=lv_, l2_=l2_: e.tensor_sub(lv_, lv_, l2_),
              reads=[("lvar", pr), ("lm2", pr)], writes=[("lvar", pr)])
        p.add("scalar", lambda e, lv_=lv_, lr_=lr_: e.activation(lr_, lv_, AF.Sqrt, bias=eps5[:]),
              reads=[("lvar", pr), "eps5"], writes=[("lrstd", pr)])
        p.add("vector", lambda e, lr_=lr_: e.reciprocal(lr_, lr_),
              reads=[("lrstd", pr)], writes=[("lrstd", pr)])
        p.add("vector", lambda e, lnv=lnv, xpv=xpv, lm_=lm_, lr_=lr_: e.tensor_scalar(
            xpv, lnv, lm_, lr_, ALU.subtract, ALU.mult),
            reads=[("lnr", pr), ("lmean", pr), ("lrstd", pr)], writes=[("xpb", pr)])
        p.dma("sync", xprime[t * 128:(t + 1) * 128, :], xpv, "xpst",
              reads=[("xpb", pr)], writes=["xprime_dram"])
        def xs_store(e, t=t, xpv=xpv):
            return e.dma_start(out=xs[ds(hof[1] + t * 128, 128), :], in_=xpv)
        p.add("gpsimd", xs_store, reads=[("xpb", pr)], writes=["xs_dram"], dma=f"xsst{pr}")

    def ph3_rt(t, pr):
        o = pr * C
        xpv = xpb[:, o:o + C]
        xTs = xpTs[:, 0:C]
        bk_, bkk = (ptrb, "ps_trb") if pr == 0 else (ptrb2, "ps_trb2")
        for cc in range(NCH):
            p.add("tensor", lambda e, cc=cc, xpv=xpv, bk_=bk_: e.transpose(
                bk_[:, cc * 128:(cc + 1) * 128], xpv[:, cc * 128:(cc + 1) * 128], id_b[:]),
                reads=[("xpb", pr), "id_b"], writes=[bkk])
        p.add("scalar", lambda e, xTs=xTs, bk_=bk_: e.copy(xTs[:, 0:C], bk_[:]),
              reads=[bkk], writes=[("xpTs", pr)])
        for cc in range(NCH):
            p.add("tensor", lambda e, cc=cc, xTs=xTs: e.matmul(
                lgps, xTs[:, cc * 128:(cc + 1) * 128], wg_sb[:, cc * E:(cc + 1) * E],
                start=(cc == 0), stop=(cc == NCH - 1)),
                reads=[("xpTs", pr), "wg_sb"], writes=["ps_av"])
        lgv = lgall[:, t * E:(t + 1) * E]
        p.add("vector", lambda e, lgv=lgv: e.tensor_copy(lgv, lgps), reads=["ps_av"],
              writes=[("lgall", t)])
        p.add("vector", lambda e, lgv=lgv: e.reduce_max(m8[:], lgv, axis=AX.X),
              reads=[("lgall", t)], writes=["m8"])
        p.add("vector", lambda e, lgv=lgv: e.tensor_scalar(pr_t[:], lgv, m8[:], None,
                                                           ALU.is_ge),
              reads=[("lgall", t), "m8"], writes=["pr_t"])
        p.add("vector", lambda e: e.tensor_tensor(pr_t[:], pr_t[:], iotadesc[:, 0:E], ALU.mult),
              reads=["pr_t", "iotadesc"], writes=["pr_t"])
        p.add("vector", lambda e: e.reduce_max(pmax[:], pr_t[:], axis=AX.X),
              reads=["pr_t"], writes=["pmax"])
        p.add("vector", lambda e, t=t: e.tensor_scalar(
            mask_tm[:, t * E:(t + 1) * E], pr_t[:], pmax[:], None, ALU.is_ge),
            reads=["pr_t", "pmax"], writes=["mask_tm"])
        p.add("tensor", lambda e, t=t, bk_=bk_: e.transpose(
            bk_[0:8, 0:128], mask_tm[:, t * E:(t + 1) * E], id_b[:]),
            reads=["mask_tm", "id_b"], writes=[bkk])
        p.add("vector", lambda e, bk_=bk_: e.tensor_copy(mTst[:], bk_[0:8, 0:128]),
              reads=[bkk], writes=["mTst"])
        def ms_store(e, t=t):
            return e.dma_start(out=msh[:, ds(hof[1] + t * 128, 128)], in_=mTst[:])
        p.add("gpsimd", ms_store, reads=["mTst"], writes=["ms_dram"], dma="msst")

    prev3 = None
    for t in range(NT):
        pr = t % 2
        ph3_wo(t, pr)
        if prev3 is not None:
            ph3_rt(*prev3)
        ph3_ln(t, pr)
        prev3 = (t, pr)
    ph3_rt(*prev3)

    p.add("sync", lambda e: e.nop(),
          reads=["xs_dram", "ms_dram", "xprime_dram"])

    p.barrier()

    attn_outer.close()
    rtB = ExitStack()
    sb_ = rtB.enter_context
    iota512 = sb_(nc.sbuf_tensor("iota512", [128, 512], F32))
    cumtm = sb_(nc.sbuf_tensor("cumtm", [128, 32 * E], F32))
    mycum = sb_(nc.sbuf_tensor("mycum", [128, 32], F32))
    cmpb = sb_(nc.sbuf_tensor("cmpb", [128, 2 * 512], BF16))
    cntf = sb_(nc.sbuf_tensor("cntf", [1, 512], F32))
    nslot = sb_(nc.sbuf_tensor("nslot", [128, 1], F32))
    tmp256 = sb_(nc.sbuf_tensor("tmp256", [128, 256], F32))
    rt_scope = ExitStack()
    sr_ = rt_scope.enter_context
    ms_sb = sr_(nc.sbuf_tensor("ms_sb", [8, T], BF16))
    ones8 = sr_(nc.sbuf_tensor("ones8", [8, T], BF16))
    cums8 = sr_(nc.sbuf_tensor("cums8", [8, T], F32))
    lgm = sr_(nc.sbuf_tensor("lgm", [128, NT], F32))
    lgex = sr_(nc.sbuf_tensor("lgex", [128, NT * E], F32))
    p_t = sr_(nc.sbuf_tensor("p_t", [128, NT], F32))
    keep_t = sr_(nc.sbuf_tensor("keep_t", [128, NT], F32))
    en_t = sr_(nc.sbuf_tensor("en_t", [128, NT], F32))
    gidxf = sr_(nc.sbuf_tensor("gidxf", [128, NT], F32))
    pB = sr_(nc.sbuf_tensor("pB", [128, NT], F32))
    tmp128 = sr_(nc.sbuf_tensor("tmp128", [128, 128], F32))


    p.dma("sync", ms_sb[:], msh, "msld", writes=["ms_sb"])
    p.add("vector", lambda e: e.reduce_max(
        lgm[:], lgall[:].rearrange("p (t e) -> p t e", e=E), axis=AX.X),
        reads=[("lgall", t) for t in range(NT)] if False else ["lgall_all"], writes=["lgm"])
    p.add("vector", lambda e: e.tensor_tensor(
        lgex[:].rearrange("p (t e) -> p t e", e=E),
        lgall[:].rearrange("p (t e) -> p t e", e=E),
        lgm[:].unsqueeze(2).broadcast_to([128, NT, E]), ALU.subtract),
        reads=["lgall_all", "lgm"], writes=["lgex"])
    p.add("scalar", lambda e: e.activation(lgex[:], lgex[:], AF.Exp),
          reads=["lgex"], writes=["lgex"])
    p.add("vector", lambda e: e.reduce_sum(
        gate_t[:], lgex[:].rearrange("p (t e) -> p t e", e=E), axis=AX.X),
        reads=["lgex"], writes=["gate_t"])
    p.add("vector", lambda e: e.reciprocal(gate_t[:], gate_t[:]),
          reads=["gate_t"], writes=["gate_t"])
    p.dma("sync", iota512[:], i512_in, "l_i512", writes=["iota512"])
    p.add("vector", lambda e: e.memset(ones8[:], 1.0), writes=["ones8"])
    p.add("vector", lambda e: e.tensor_tensor_scan(cums8[:], ones8[:], ms_sb[:], 0.0,
                                                   ALU.mult, ALU.add),
          reads=["ones8", "ms_sb"], writes=["cums8"])
    for ch in range(32):
        cb, cbk = (ps_av, "ps_av") if ch % 2 == 0 else (ps_x, "ps_x")
        p.add("tensor", lambda e, ch=ch, cb=cb: e.transpose(
            cb[:, 0:8], cums8[:, ch * 128:(ch + 1) * 128], id_f[0:8, 0:8]),
            reads=["cums8", "id_f"], writes=[cbk])
        p.add("vector", lambda e, ch=ch, cb=cb: e.tensor_copy(
            cumtm[:, ch * E:(ch + 1) * E], cb[:, 0:8]),
            reads=[cbk], writes=["cumtm"])

    m3 = lambda ap: ap.rearrange("p (t e) -> p t e", e=E)
    p.add("vector", lambda e: e.tensor_tensor(tmp128[:], cumtm[:, 0:128], mask_tm[:], ALU.mult),
          reads=["cumtm", "mask_tm"], writes=["tmp128"])
    p.add("vector", lambda e: e.reduce_sum(p_t[:], m3(tmp128[:]), axis=AX.X),
          reads=["tmp128"], writes=["p_t"])
    p.add("vector", lambda e: e.tensor_tensor(tmp128[:], cumtm[:, 128:256], mask_tm[:], ALU.mult),
          reads=["cumtm", "mask_tm"], writes=["tmp128"])
    p.add("vector", lambda e: e.reduce_sum(pB[:], m3(tmp128[:]), axis=AX.X),
          reads=["tmp128"], writes=["pB"])
    p.add("vector", lambda e: e.tensor_scalar(p_t[:], p_t[:], hsel[:, 0:1], None, ALU.mult),
          reads=["p_t", "hsel"], writes=["p_t"])
    p.add("vector", lambda e: e.scalar_tensor_tensor(p_t[:], pB[:], hsel[:, 1:2], p_t[:],
                                                     ALU.mult, ALU.add),
          reads=["pB", "hsel", "p_t"], writes=["p_t"])
    p.add("vector", lambda e: e.tensor_tensor(tmp128[:], iotadesc[:], mask_tm[:], ALU.mult),
          reads=["iotadesc", "mask_tm"], writes=["tmp128"])
    p.add("vector", lambda e: e.reduce_sum(en_t[:], m3(tmp128[:]), axis=AX.X),
          reads=["tmp128"], writes=["en_t"])
    p.add("vector", lambda e: e.tensor_scalar_mul(en_t[:], en_t[:], -1.0),
          reads=["en_t"], writes=["en_t"])
    p.add("vector", lambda e: e.tensor_scalar_add(p_t[:], p_t[:], -1.0),
          reads=["p_t"], writes=["p_t"])
    p.add("vector", lambda e: e.tensor_scalar(keep_t[:], p_t[:], -1.0, 511.5,
                                              ALU.mult, ALU.add),
          reads=["p_t"], writes=["keep_t"])
    p.add("vector", lambda e: e.tensor_scalar(keep_t[:], keep_t[:], 0.0, None, ALU.is_ge),
          reads=["keep_t"], writes=["keep_t"])
    p.add("vector", lambda e: e.tensor_tensor(scale_t[:], gate_t[:], keep_t[:], ALU.mult),
          reads=["gate_t", "keep_t"], writes=["scale_t"])
    p.add("vector", lambda e: e.tensor_scalar(gidxf[:], en_t[:], 512.0, 4096.0,
                                              ALU.mult, ALU.add),
          reads=["en_t"], writes=["gidxf"])
    p.add("vector", lambda e: e.tensor_add(gidxf[:], gidxf[:], p_t[:]),
          reads=["gidxf", "p_t"], writes=["gidxf"])
    p.add("vector", lambda e: e.tensor_copy(gidx[:], gidxf[:]),
          reads=["gidxf"], writes=["gidx"])

    rt_scope.close()
    ffn_scope = ExitStack()
    sf_ = ffn_scope.enter_context
    w1sb = sf_(nc.sbuf_tensor("w1sb", [128, 2 * NCH * 2048], BF16))
    w2sb = sf_(nc.sbuf_tensor("w2sb", [128, 32 * C], BF16))
    hT = sf_(nc.sbuf_tensor("hT", [128, 32 * 512], BF16))
    slin = sf_(nc.sbuf_tensor("slin", [128, 2 * C], BF16))
    slinT = sf_(nc.sbuf_tensor("slinT", [128, 2 * NCH * 512], BF16))
    orow = sf_(nc.sbuf_tensor("orow", [128, C], BF16))
    for eng in ("vector", "scalar", "gpsimd", "sync", "tensor"):
        p.fence(eng, ["ms_sb", "ones8", "cums8", "lgm", "lgex",
                      ("wstg", 0), ("wstg", 1)])

    def dispatch(j):
        selj = sel4[:, j * E:(j + 1) * E].unsqueeze(1).broadcast_to([128, 32, E])
        p.add("vector", lambda e, selj=selj: e.tensor_tensor(
            tmp256[:].rearrange("p (c e) -> p c e", e=E),
            cumtm[:].rearrange("p (c e) -> p c e", e=E), selj, ALU.mult),
            reads=["cumtm", "sel4"], writes=["tmp256"])
        p.add("vector", lambda e: e.reduce_sum(
            mycum[:], tmp256[:].rearrange("p (c e) -> p c e", e=E), axis=AX.X),
            reads=["tmp256"], writes=["mycum"])
        bk = j % 2
        for ch in range(32):
            csl = ch % 2
            p.add("vector", lambda e, ch=ch, csl=csl: e.tensor_scalar(
                cmpb[:, csl * 512:(csl + 1) * 512], iota512[:], mycum[:, ch:ch + 1],
                None, ALU.is_ge),
                reads=["iota512", "mycum"], writes=[("cmpb", csl)])
            p.add("tensor", lambda e, ch=ch, bk=bk, csl=csl: e.matmul(
                qps[bk][0:1, 0:512], ones128b[:], cmpb[:, csl * 512:(csl + 1) * 512],
                start=(ch == 0), stop=(ch == 31)),
                reads=["ones128b", ("cmpb", csl)], writes=[("ps_s", bk)])
        p.add("vector", lambda e, bk=bk: e.tensor_copy(cntf[:], qps[bk][0:1, 0:512]),
              reads=[("ps_s", bk)], writes=["cntf"])
        for q in range(4):
            p.add("tensor", lambda e, q=q, bk=bk: e.transpose(
                qps[bk][:, 520:521], cntf[0:1, q * 128:(q + 1) * 128], id_f[0:1, 0:1]),
                reads=["cntf", "id_f"], writes=[("ps_s", bk)])
            p.add("vector", lambda e, bk=bk: e.tensor_copy(nslot[:], qps[bk][:, 520:521]),
                  reads=[("ps_s", bk)], writes=["nslot"])
            p.add("vector", lambda e, j=j, q=q: e.tensor_copy(
                didx[:, j * 4 + q: j * 4 + q + 1], nslot[:]),
                reads=["nslot"], writes=["didx"])
        for q in range(4):
            ssl = q % 2
            sv = slin[:, ssl * C:(ssl + 1) * C]
            p.add("vector", lambda e, sv=sv: e.memset(sv, 0.0), writes=[("slin", ssl)])
            p.add("gpsimd", lambda e, j=j, q=q, sv=sv: e.indirect_dma_start(
                out=sv, out_offset=None, in_=xs,
                in_offset=IndirectOffsetOnAxis(didx[:, j * 4 + q: j * 4 + q + 1], 0),
                bounds_check=T - 1, oob_is_err=False),
                reads=["didx", ("slin", ssl), "xs_dram"], writes=[("slin", ssl)],
                dma=f"disp{ssl}")
            db, dbk = (ptrb, "ps_trb") if q % 2 == 0 else (ptrb2, "ps_trb2")
            for cc in range(NCH):
                p.add("tensor", lambda e, cc=cc, db=db, sv=sv: e.transpose(
                    db[:, cc * 128:(cc + 1) * 128],
                    sv[:, cc * 128: (cc + 1) * 128], id_b[:]),
                    reads=[("slin", ssl), "id_b"], writes=[dbk])
            dst = slinT[:, (j % 2) * NCH * 512:].rearrange(
                "p (c i) -> p c i", i=512)[:, 0:NCH, q * 128:(q + 1) * 128]
            p.add("scalar", lambda e, dst=dst, db=db: e.copy(
                dst, db[:].rearrange("p (c i) -> p c i", i=128)),
                reads=[dbk], writes=[("slinT", j % 2)])

    if debug:
        p.dma("scalar", dbg_ms, ms_sb[:], "dbg", reads=["ms_sb"])
        p.add("vector", lambda e: e.tensor_copy(nslot[:], gidx[:, 0:1]),
              reads=["gidx"], writes=["nslot"])
        def dbg_rt_store(e):
            return e.dma_start(out=dbg_rt, in_=cumtm[:, 0:64])
        p.add("scalar", dbg_rt_store, reads=["cumtm"], dma="dbg")
        p.add("vector", lambda e: e.tensor_copy(gidxf[:], gidxf[:]), reads=["gidxf"])
        def dbg_di_store(e):
            return e.dma_start(out=dbg_di.bitcast(U32) if hasattr(dbg_di, "bitcast") else dbg_di,
                               in_=didx[:])
        p.add("scalar", dbg_di_store, reads=["didx"], dma="dbg")

    hps = [qps[0][:, 0:512], qps[1][:, 0:512]]

    def w1_load(j, hh):
        sl = (2 * j + hh) % 2
        for qq in range(4):
            p.add("gpsimd", lambda e, j=j, hh=hh, sl=sl, qq=qq: e.dma_start(
                out=w1sb[:, sl * 16384:(sl + 1) * 16384].rearrange(
                    "p (c h) -> p c h", h=2048)[:, :, qq * 512:(qq + 1) * 512],
                in_=w1[j, :, hh * 2048 + qq * 512: hh * 2048 + (qq + 1) * 512].rearrange(
                    "(c p) h -> p c h", p=128)),
                writes=[("w1sb", sl, qq)], dma=f"w1l{sl}q{qq}")

    def w2_load(j, qt):
        sl = (4 * j + qt) % 4
        p.add("gpsimd", lambda e, j=j, qt=qt, sl=sl: e.dma_start(
            out=w2sb[:, sl * 8192:(sl + 1) * 8192].rearrange("p (m c) -> p m c", c=C),
            in_=w2[j, qt * 1024:(qt + 1) * 1024, :].rearrange("(m p) c -> p m c", p=128)),
            writes=[("w2sb", sl)], dma=f"w2l{sl}")

    dispatch(0)
    for hh in range(2):
        w1_load(0, hh)
    for qt in range(4):
        w2_load(0, qt)

    for j in range(E_LOC):
        for hhalf in range(2):
            sl = (2 * j + hhalf) % 2
            for m in range(16):
                mg = hhalf * 16 + m
                bk = m % 2
                for cc in range(NCH):
                    p.add("tensor", lambda e, m=m, cc=cc, bk=bk, sl=sl, j=j: e.matmul(
                        hps[bk],
                        w1sb[:, sl * 16384 + cc * 2048 + m * 128:
                             sl * 16384 + cc * 2048 + (m + 1) * 128],
                        slinT[:, ((j % 2) * NCH + cc) * 512:((j % 2) * NCH + cc + 1) * 512],
                        start=(cc == 0), stop=(cc == NCH - 1)),
                        reads=[("w1sb", sl, m // 4), ("slinT", j % 2)],
                        writes=[("ps_s", bk)])
                p.add("scalar", lambda e, mg=mg, bk=bk: e.activation(
                    hT[:, mg * 512:(mg + 1) * 512], hps[bk], AF.Gelu_apprx_tanh),
                    reads=[("ps_s", bk)], writes=["hT"])
        if j + 1 < E_LOC:
            dispatch(j + 1)
            w1_load(j + 1, 0)
            w1_load(j + 1, 1)
        op2 = [ps_x[:], ps_av[:]]
        op2k = ["ps_x", "ps_av"]
        for q in range(4):
            for half in range(2):
                for m in range(32):
                    sl = (4 * j + m // 8) % 4
                    p.add("tensor", lambda e, q=q, half=half, m=m, sl=sl: e.matmul(
                        op2[half], hT[:, m * 512 + q * 128: m * 512 + (q + 1) * 128],
                        w2sb[:, sl * 8192 + (m % 8) * C + half * 512:
                             sl * 8192 + (m % 8) * C + half * 512 + 512],
                        start=(m == 0), stop=(m == 31)),
                        reads=["hT", ("w2sb", sl)], writes=[op2k[half]])
            for half in range(2):
                p.add("vector", lambda e, half=half: e.tensor_copy(
                    orow[:, half * 512:(half + 1) * 512], op2[half]),
                    reads=[op2k[half]], writes=["orow"])
            def so_store(e, j=j, q=q):
                return e.dma_start(out=so[ds(hof[0] + j * 512 + q * 128, 128), :],
                                   in_=orow[:])
            p.add("scalar", so_store, reads=["orow"], writes=["so_dram"], dma="sost")

        if j + 1 < E_LOC:
            for qt in range(4):
                w2_load(j + 1, qt)

    if debug:
        for r in range(16):
            p.dma("scalar", dbg_si[r * 128:(r + 1) * 128, :],
                  hT[:, 0:C] if False else slinT[:, 0:512], "dbg", reads=["slinT"])
            break
    p.add("sync", lambda e: e.nop(), reads=["so_dram"])

    p.barrier()

    ffn_scope.close()
    lnsq2 = sbp("lnsq2", [128, C], F32)
    compb = sbp("compb", [128, 16 * C], BF16)
    xres2 = sbp("xres2", [128, 16 * C], BF16)
    yrow = sbp("yrow", [128, 2 * C], F32)
    for eng in ("vector", "scalar", "gpsimd", "sync"):
        p.fence(eng, ["w1sb", "w2sb", "hT", ("slin", 0), ("slin", 1)])
    p.add("vector", lambda e: e.memset(compb[:], 0.0),
          writes=[("compb", t_) for t_ in range(NT)])
    for t in range(NT):
        cpv = compb[:, t * C:(t + 1) * C]
        xrv = xres2[:, t * C:(t + 1) * C]
        p.add("gpsimd", lambda e, t=t, cpv=cpv: e.indirect_dma_start(
            out=cpv, out_offset=None, in_=so,
            in_offset=IndirectOffsetOnAxis(gidx[:, t:t + 1], 0),
            bounds_check=T - 1, oob_is_err=False),
            reads=["gidx", "so_dram"], writes=[("compb", t)],
            dma=f"comb{t}")
        p.dma("sync", xrv, xprime[t * 128:(t + 1) * 128, :], f"xpl{t}",
              reads=["xprime_dram"], writes=[("xres2", t)])
    for t in range(NT):
        pr = t % 2
        cpv = compb[:, t * C:(t + 1) * C]
        xrv = xres2[:, t * C:(t + 1) * C]
        yv = yrow[:, pr * C:(pr + 1) * C]
        ls_ = lsum[:, pr:pr + 1]
        lm_ = lmean[:, pr:pr + 1]
        lv_ = lvar[:, pr:pr + 1]
        l2_ = lm2[:, pr:pr + 1]
        lr_ = lrstd[:, pr:pr + 1]
        p.add("vector", lambda e, t=t, cpv=cpv, xrv=xrv, yv=yv, ls_=ls_: e.scalar_tensor_tensor(
            yv, cpv, scale_t[:, t:t + 1], xrv, ALU.mult, ALU.add,
            accum_out=ls_),
            reads=[("compb", t), "scale_t", ("xres2", t)],
            writes=[("yrow", pr), ("lsum", pr)])
        p.add("vector", lambda e, ls_=ls_, lm_=lm_: e.tensor_scalar_mul(lm_, ls_, 1.0 / C),
              reads=[("lsum", pr)], writes=[("lmean", pr)])
        p.add("scalar", lambda e, yv=yv, lv_=lv_: e.activation(lnsq2[:], yv, AF.Square,
                                                               accum_out=lv_),
              reads=[("yrow", pr)], writes=["lnsq2", ("lvar", pr)])
        p.add("vector", lambda e, lm_=lm_, l2_=l2_: e.tensor_mul(l2_, lm_, lm_),
              reads=[("lmean", pr)], writes=[("lm2", pr)])
        p.add("vector", lambda e, lv_=lv_: e.tensor_scalar(lv_, lv_, 1.0 / C,
                                                           None, ALU.mult),
              reads=[("lvar", pr)], writes=[("lvar", pr)])
        p.add("vector", lambda e, lv_=lv_, l2_=l2_: e.tensor_sub(lv_, lv_, l2_),
              reads=[("lvar", pr), ("lm2", pr)], writes=[("lvar", pr)])
        p.add("scalar", lambda e, lv_=lv_, lr_=lr_: e.activation(lr_, lv_, AF.Sqrt,
                                                                 bias=eps5[:]),
              reads=[("lvar", pr), "eps5"], writes=[("lrstd", pr)])
        p.add("vector", lambda e, lr_=lr_: e.reciprocal(lr_, lr_),
              reads=[("lrstd", pr)], writes=[("lrstd", pr)])
        p.add("vector", lambda e, yv=yv, lm_=lm_, lr_=lr_: e.tensor_scalar(
            yv, yv, lm_, lr_, ALU.subtract, ALU.mult),
            reads=[("yrow", pr), ("lmean", pr), ("lrstd", pr)], writes=[("yrow", pr)])
        p.dma("sync", yout[t * 128:(t + 1) * 128, :], yv, f"yst{pr}", reads=[("yrow", pr)])

    p.emit()
    p.final_waits("sync", ["yst0", "yst1"] + (["dbg"] if debug else []))
    psA.close()
    stack.close()
    nc.finalize()
    return nc


def _perm(n):
    out = []
    for h0 in range(0, n, HD):
        out.extend(range(h0, h0 + HD, 2))
        out.extend(range(h0 + 1, h0 + HD, 2))
    return np.array(out)


def make_in_maps(inputs):
    x = np.asarray(inputs["L0_attn_x"])
    Wq = np.asarray(inputs["L0_attn_Wq"])
    Wk = np.asarray(inputs["L0_attn_Wk"])
    Wv = np.asarray(inputs["L0_attn_Wv"])
    Wo = np.asarray(inputs["L0_attn_Wo"])
    qg = np.asarray(inputs["L0_attn_q_gamma"])
    kg = np.asarray(inputs["L0_attn_k_gamma"])
    Wg = np.asarray(inputs["L0_ffn_Wg"])
    W1 = np.asarray(inputs["L0_ffn_W1"])
    W2 = np.asarray(inputs["L0_ffn_W2"])

    pq = _perm(C)
    pk = _perm(HD)
    Wq_p = np.ascontiguousarray(Wq[pq, :])
    Wk_p = np.ascontiguousarray(Wk[pk, :])
    gq_t = np.tile(qg[pk][None, :], (128, NH)).astype(np.float32)
    gk_t = np.tile(kg[pk][None, :], (128, 1)).astype(np.float32)
    inv = (1.0 / (1.0e6 ** (np.arange(0, HD, 2, dtype=np.float32) / HD))).astype(np.float32)
    ident = np.eye(128, dtype=np.float32)

    ins = []
    for c in range(8):
        b, h = c >> 1, c & 1
        lo = h * T_OWN - HALO
        xs_ = np.zeros((T_KV, C), dtype=np.float32)
        s0 = max(0, lo)
        xs_[s0 - lo:, :] = x[b, s0: lo + T_KV, :]
        pos = (lo + np.arange(T_KV)).astype(np.float32)
        freqs = np.outer(pos, inv).astype(np.float32)
        kb = np.zeros((1, T_KV), dtype=np.float32)
        if h == 0:
            kb[0, :HALO] = NEG
        sel = np.zeros((128, E_LOC * E), dtype=np.float32)
        for j in range(E_LOC):
            sel[:, j * E + (E_LOC * h + j)] = 1.0
        hs = np.zeros((128, 2), dtype=np.float32)
        hs[:, h] = 1.0
        ins.append({
            "x": xs_,
            "wq": Wq_p, "wk": Wk_p, "wv": np.ascontiguousarray(Wv),
            "wo": np.ascontiguousarray(Wo), "wg": np.ascontiguousarray(Wg),
            "w1": np.ascontiguousarray(W1[E_LOC * h: E_LOC * h + E_LOC]),
            "w2": np.ascontiguousarray(W2[E_LOC * h: E_LOC * h + E_LOC]),
            "gq": gq_t, "gk": gk_t,
            "cosb": np.cos(freqs).astype(np.float32),
            "sinb": np.sin(freqs).astype(np.float32),
            "kbias": kb, "sel4": sel, "hsel": hs, "idf": ident,
            "i512": np.tile(np.arange(512, dtype=np.float32)[None, :], (128, 1)),
        })
    return ins


def kernel(**inputs):
    ins = make_in_maps(inputs)
    nc = build()
    res = run_bass_kernel_spmd(nc, ins, core_ids=list(range(8)))
    y = np.stack([res.results[c]["y"] for c in range(8)]).reshape(B, T, C)
    return y


# version 31
# speedup vs baseline: 1.0433x; 1.0167x over previous
import numpy as np
import concourse.bass as bass
import concourse.bacc as bacc
import concourse.mybir as mybir
from concourse.bass import IndirectOffsetOnAxis, ds
from concourse.bass_utils import run_bass_kernel_spmd
from contextlib import ExitStack

F32 = mybir.dt.float32
BF16 = mybir.dt.bfloat16
I32 = mybir.dt.int32
U32 = mybir.dt.uint32
AF = mybir.ActivationFunctionType
ALU = mybir.AluOpType
AX = mybir.AxisListType

B, T, C = 4, 4096, 1024
NH, HD = 4, 256
T_OWN, HALO = 2048, 512
T_KV = T_OWN + HALO
NT = T_OWN // 128
NKV = T_KV // 128
NCH = C // 128
E, E_LOC, CAP = 8, 4, 512
H_FFN = 4096
NEG = -1.0e30
ENGS = ("sync", "vector", "scalar", "tensor", "gpsimd")


class Prog:
    def __init__(self, nc):
        self.nc = nc
        self.ops = []
        self.phases = []
        self._cur = []
        self.dma_counts = {}
        self._pending_fence = {}

    def add(self, engine, fn, reads=(), writes=(), dma=None):
        if dma is not None:
            dma = f"{dma}_{engine}"
        writes = tuple(writes) + tuple(self._pending_fence.pop(engine, ()))
        op = dict(engine=engine, fn=fn, reads=tuple(reads), writes=tuple(writes),
                  dma=dma, idx=len(self.ops))
        if dma is not None:
            self.dma_counts[dma] = self.dma_counts.get(dma, 0) + 1
            op["dma_count"] = self.dma_counts[dma]
        self.ops.append(op)
        self._cur.append(op)
        return op

    def dma(self, engine, out, in_, tag, reads=(), writes=()):
        self.add(engine, lambda e, o=out, i=in_: e.dma_start(out=o, in_=i),
                 reads=reads, writes=writes, dma=tag)

    def fence(self, engine, regions):
        self._pending_fence[engine] = tuple(self._pending_fence.get(engine, ())) + tuple(regions)

    def barrier(self):
        if self._cur:
            self.phases.append(("ops", self._cur))
            self._cur = []
        self.phases.append(("barrier", None))

    def emit(self):
        nc = self.nc
        if self._cur:
            self.phases.append(("ops", self._cur))
            self._cur = []
        last_writer, readers = {}, {}
        for op in self.ops:
            deps = set()
            for r in op["reads"]:
                w = last_writer.get(r)
                if w is not None:
                    deps.add(w["idx"])
                readers.setdefault(r, []).append(op)
            for wr in op["writes"]:
                w = last_writer.get(wr)
                if w is not None:
                    deps.add(w["idx"])
                for rd in readers.get(wr, ()):
                    deps.add(rd["idx"])
                readers[wr] = []
                last_writer[wr] = op
            deps.discard(op["idx"])
            pruned = []
            for d in deps:
                pd = self.ops[d]
                if (pd["dma"] is None and op["dma"] is None
                        and pd["engine"] == "tensor" and op["engine"] == "tensor"):
                    continue
                pruned.append(d)
            best = {}
            for d in pruned:
                pd = self.ops[d]
                key = ("d", pd["dma"]) if pd["dma"] is not None else ("e", pd["engine"])
                if key not in best or d > best[key]:
                    best[key] = d
            op["deps"] = sorted(best.values())
        need_inc = set()
        for op in self.ops:
            for d in op["deps"]:
                if self.ops[d]["dma"] is None:
                    need_inc.add(d)
        cnt = {e: 0 for e in ENGS}
        for op in self.ops:
            if op["dma"] is None:
                op["do_inc"] = op["idx"] in need_inc
                if op["do_inc"]:
                    cnt[op["engine"]] += 1
                op["inc_val"] = cnt[op["engine"]]
        esem = {e: nc.alloc_semaphore(f"es_{e}") for e in ENGS}
        dsem = {t: nc.alloc_semaphore(f"ds_{t}") for t in self.dma_counts}
        self.esem, self.dsem = esem, dsem
        for op in self.ops:
            waits = {}
            for d in op["deps"]:
                pd = self.ops[d]
                if pd["dma"] is not None:
                    sem, val = dsem[pd["dma"]], 16 * pd["dma_count"]
                else:
                    sem, val = esem[pd["engine"]], pd["inc_val"]
                k = id(sem)
                if k not in waits or waits[k][1] < val:
                    waits[k] = (sem, val)
            op["waits"] = list(waits.values())
        watermark = {}
        for kind, payload in self.phases:
            if kind == "barrier":
                nc.all_core_barrier()
                continue
            by_eng = {}
            for op in payload:
                by_eng.setdefault(op["engine"], []).append(op)
            with nc.Block() as blk:
                for ename, ops_ in by_eng.items():
                    def mk(ename, ops_):
                        def body(eng):
                            for op in ops_:
                                for sem, val in op["waits"]:
                                    if watermark.get((ename, id(sem)), 0) < val:
                                        eng.wait_ge(sem, val)
                                        watermark[(ename, id(sem))] = val
                                inst = op["fn"](eng)
                                if op["dma"] is not None:
                                    inst.then_inc(self.dsem[op["dma"]], 16)
                                elif op["do_inc"]:
                                    inst.then_inc(self.esem[ename], 1)
                        return body
                    getattr(blk, ename)(mk(ename, ops_))

    def final_waits(self, engine, tags):
        full = [t2 for t2 in self.dma_counts for t in tags if t2.startswith(t)]
        def fw(eng):
            for t in full:
                eng.wait_ge(self.dsem[t], 16 * self.dma_counts[t])
        with self.nc.Block() as blk:
            getattr(blk, engine)(fw)


def build(debug=False, num_devices=8):
    nc = bacc.Bacc("TRN2", target_bir_lowering=False, debug=False, num_devices=num_devices)

    xin = nc.dram_tensor("x", [T_KV, C], F32, kind="ExternalInput").ap()
    wq = nc.dram_tensor("wq", [C, C], F32, kind="ExternalInput").ap()
    wk = nc.dram_tensor("wk", [HD, C], F32, kind="ExternalInput").ap()
    wv = nc.dram_tensor("wv", [HD, C], F32, kind="ExternalInput").ap()
    wo = nc.dram_tensor("wo", [C, C], F32, kind="ExternalInput").ap()
    wgi = nc.dram_tensor("wg", [C, E], F32, kind="ExternalInput").ap()
    w1 = nc.dram_tensor("w1", [E_LOC, C, H_FFN], F32, kind="ExternalInput").ap()
    w2 = nc.dram_tensor("w2", [E_LOC, H_FFN, C], F32, kind="ExternalInput").ap()
    gq_in = nc.dram_tensor("gq", [128, C], F32, kind="ExternalInput").ap()
    gk_in = nc.dram_tensor("gk", [128, HD], F32, kind="ExternalInput").ap()
    cos_in = nc.dram_tensor("cosb", [T_KV, 128], F32, kind="ExternalInput").ap()
    sin_in = nc.dram_tensor("sinb", [T_KV, 128], F32, kind="ExternalInput").ap()
    kb_in = nc.dram_tensor("kbias", [1, T_KV], F32, kind="ExternalInput").ap()
    sel_in = nc.dram_tensor("sel4", [128, E_LOC * E], F32, kind="ExternalInput").ap()
    hsel_in = nc.dram_tensor("hsel", [128, 2], F32, kind="ExternalInput").ap()
    idf_in = nc.dram_tensor("idf", [128, 128], F32, kind="ExternalInput").ap()
    i512_in = nc.dram_tensor("i512", [128, 512], F32, kind="ExternalInput").ap()
    yout = nc.dram_tensor("y", [T_OWN, C], F32, kind="ExternalOutput").ap()

    xprime = nc.dram_tensor("xprime", [T_OWN, C], BF16,
                            kind=("ExternalOutput" if debug else "Internal")).ap()
    xs = nc.dram_tensor("xs_sh", [T, C], BF16, kind="Internal", addr_space="Shared").ap()
    msh = nc.dram_tensor("ms_sh", [E, T], BF16, kind="Internal", addr_space="Shared").ap()
    so = nc.dram_tensor("so_sh", [T, C], BF16, kind="Internal", addr_space="Shared").ap()
    if debug:
        dbg_ms = nc.dram_tensor("dbg_ms", [E, T], F32, kind="ExternalOutput").ap()
        dbg_rt = nc.dram_tensor("dbg_rt", [128, 64], F32, kind="ExternalOutput").ap()
        dbg_di = nc.dram_tensor("dbg_di", [128, 16], F32, kind="ExternalOutput").ap()
        dbg_si = nc.dram_tensor("dbg_si", [E_LOC * CAP, C], F32, kind="ExternalOutput").ap()

    p = Prog(nc)
    hof = {}

    def snap_hof(e):
        hof[0] = e.snap((e.partition_id() % 2) * T_OWN, min_val=0, max_val=T_OWN)

    def snap_hofg(e):
        hof[1] = e.snap((e.partition_id() % 2) * T_OWN, min_val=0, max_val=T_OWN)
        return e.nop()
    stack = ExitStack()
    sbp = lambda n, s, d: nc.alloc_sbuf_tensor(n, s, d)
    sbs = lambda n, s, d: stack.enter_context(nc.sbuf_tensor(n, s, d))

    id_f = sbp("id_f", [128, 128], F32)
    id_b = sbp("id_b", [128, 128], BF16)
    ones1b = sbp("ones1b", [1, 128], BF16)
    ones128b = sbp("ones128b", [128, 1], BF16)
    iotadesc = sbp("iotadesc", [128, 128], F32)
    sel4 = sbp("sel4_sb", [128, E_LOC * E], F32)
    hsel = sbp("hsel_sb", [128, 2], F32)
    eps6 = sbp("eps6", [128, 1], F32)
    eps5 = sbp("eps5", [128, 1], F32)
    mask_tm = sbp("mask_tm", [128, NT * E], BF16)
    lgall = sbp("lgall", [128, NT * E], F32)
    gate_t = sbp("gate_t", [128, NT], F32)
    scale_t = sbp("scale_t", [128, NT], F32)
    gidx = sbp("gidx", [128, NT], U32)
    didx = sbp("didx", [128, 16], U32)
    lsum = sbp("lsum", [128, 2], F32)
    lmean = sbp("lmean", [128, 2], F32)
    lvar = sbp("lvar", [128, 2], F32)
    lm2 = sbp("lm2", [128, 2], F32)
    lrstd = sbp("lrstd", [128, 2], F32)
    tmpa = sbp("tmpa", [128, E], F32)
    tmpb2 = sbp("tmpb2", [128, E], F32)

    ptrb = nc.alloc_psum_tensor("ps_trb", [128, 1024], BF16)
    ptrb2 = nc.alloc_psum_tensor("ps_trb2", [128, 1024], BF16)

    p.add("scalar", snap_hof)
    p.add("gpsimd", snap_hofg)
    p.dma("sync", id_f[:], idf_in, "l_idf", writes=["id_f"])
    p.dma("gpsimd", id_b[:], idf_in, "l_idb", writes=["id_b"])
    p.dma("sync", sel4[:], sel_in, "l_sel", writes=["sel4"])
    p.dma("sync", hsel[:], hsel_in, "l_hsel", writes=["hsel"])
    for tile, val in ((ones1b, 1.0), (ones128b, 1.0),
                      (eps6, 1e-6), (eps5, 1e-5)):
        p.add("vector", lambda e, tl=tile, v=val: e.memset(tl[:], v),
              writes=[tile.name])
    p.add("gpsimd", lambda e: e.iota(iotadesci[:], pattern=[[0, 16], [-1, 8]], base=8,
                                     channel_multiplier=0), writes=["iotadesci"])
    p.add("vector", lambda e: e.tensor_copy(iotadesc[:], iotadesci[:]),
          reads=["iotadesci"], writes=["iotadesc"])

    attn_outer = ExitStack()
    so_ = attn_outer.enter_context
    kbias_b = so_(nc.sbuf_tensor("kbias_b", [1, T_KV], BF16))
    umask = so_(nc.sbuf_tensor("umask", [128, 128], F32))
    lmask = so_(nc.sbuf_tensor("lmask", [128, 128], F32))
    mTst = so_(nc.sbuf_tensor("mTst", [8, 128], BF16))
    iotadesci = so_(nc.sbuf_tensor("iotadesci", [128, 128], I32))
    gq_sb = so_(nc.sbuf_tensor("gq_sb", [128, C], BF16))
    gk_sb = so_(nc.sbuf_tensor("gk_sb", [128, HD], BF16))
    cosb = so_(nc.sbuf_tensor("cosb_sb", [128, T_KV], BF16))
    sinb = so_(nc.sbuf_tensor("sinb_sb", [128, T_KV], BF16))
    qT = so_(nc.sbuf_tensor("qT", [128, NH * 2 * T_OWN], BF16))
    kT = so_(nc.sbuf_tensor("kT", [128, 2 * T_KV], BF16))
    vsb = so_(nc.sbuf_tensor("vsb", [128, NKV * HD], BF16))
    wstg = so_(nc.sbuf_tensor("wstg", [128, 2 * C], BF16))
    rowmax = so_(nc.sbuf_tensor("rowmax", [128, 2], F32))
    negmax = so_(nc.sbuf_tensor("negmax", [128, 2], F32))
    denom = so_(nc.sbuf_tensor("denom", [128, 2], F32))
    rden = so_(nc.sbuf_tensor("rden", [128, 2], F32))
    xres = so_(nc.sbuf_tensor("xres", [128, 2 * C], F32))
    lnr = so_(nc.sbuf_tensor("lnr", [128, 2 * C], F32))
    xpb = so_(nc.sbuf_tensor("xpb", [128, 2 * C], BF16))
    xpTs = so_(nc.sbuf_tensor("xpTs", [128, C], BF16))
    lnsq = so_(nc.sbuf_tensor("lnsq", [128, C], F32))
    m8 = so_(nc.sbuf_tensor("m8", [128, 1], F32))
    dnm = so_(nc.sbuf_tensor("dnm", [128, 1], F32))
    pr_t = so_(nc.sbuf_tensor("pr_t", [128, E], F32))
    pmax = so_(nc.sbuf_tensor("pmax", [128, 1], F32))

    psA = ExitStack()
    qps = [psA.enter_context(nc.psum_tensor(f"ps_s{i}", [128, 640], F32)) for i in range(2)]
    ps_av = psA.enter_context(nc.psum_tensor("ps_av", [128, 512], F32))
    ps_x = psA.enter_context(nc.psum_tensor("ps_x", [128, 512], F32))
    lgps = ps_av[:, 128:136]

    p.dma("gpsimd", gq_sb[:], gq_in, "l_gq", writes=["gq_sb"])
    p.dma("gpsimd", gk_sb[:], gk_in, "l_gk", writes=["gk_sb"])
    p.dma("gpsimd", kbias_b[:], kb_in, "l_kb", writes=["kbias_b"])
    for tile, val in ((umask, 0.0), (lmask, 0.0)):
        p.add("vector", lambda e, tl=tile, v=val: e.memset(tl[:], v), writes=[tile.name])
    p.add("gpsimd", lambda e: e.affine_select(umask[:], umask[:], pattern=[[1, 128]],
                                              compare_op=ALU.is_ge, fill=NEG, base=0,
                                              channel_multiplier=-1),
          reads=["umask"], writes=["umask"])
    p.add("gpsimd", lambda e: e.affine_select(lmask[:], lmask[:], pattern=[[-1, 128]],
                                              compare_op=ALU.is_ge, fill=NEG, base=0,
                                              channel_multiplier=1),
          reads=["lmask"], writes=["lmask"])
    p.dma("gpsimd", cosb[:].rearrange("p (t i) -> p t i", i=128),
          cos_in.rearrange("(t p) i -> p t i", p=128), "l_cos", writes=["cosb"])
    p.dma("gpsimd", sinb[:].rearrange("p (t i) -> p t i", i=128),
          sin_in.rearrange("(t p) i -> p t i", p=128), "l_sin", writes=["sinb"])

    attn_inner = ExitStack()
    si_ = attn_inner.enter_context
    wqT = si_(nc.sbuf_tensor("wqT", [128, NCH * C], BF16))
    wkT = si_(nc.sbuf_tensor("wkT", [128, NCH * HD], BF16))
    wvT = si_(nc.sbuf_tensor("wvT", [128, NCH * HD], BF16))
    xT = si_(nc.sbuf_tensor("xT", [128, NCH * 1024], BF16))
    qf = si_(nc.sbuf_tensor("qf", [128, 2 * C], F32))
    qsq = si_(nc.sbuf_tensor("qsq", [128, HD], F32))
    qrt = si_(nc.sbuf_tensor("qrt", [128, C], BF16))
    qb = si_(nc.sbuf_tensor("qb", [128, 4 * C], BF16))
    ms4 = si_(nc.sbuf_tensor("ms4", [128, 2 * NH], F32))
    rstd4 = si_(nc.sbuf_tensor("rstd4", [128, 2 * NH], F32))


    def load_transposed(wsrc, wdst, dname, nrow_chunks, ncol_chunks, nout):
        for rc in range(nrow_chunks):
            sl = rc % 2
            bk_, bkk = (ptrb, "ps_trb") if rc % 2 == 0 else (ptrb2, "ps_trb2")
            p.dma("gpsimd", wstg[:, sl * C: sl * C + C], wsrc[rc * 128:(rc + 1) * 128, :],
                  f"wld{sl}", writes=[("wstg", sl)])
            for cc in range(ncol_chunks):
                p.add("tensor", lambda e, cc=cc, sl=sl, bk_=bk_: e.transpose(
                    bk_[:, cc * 128:(cc + 1) * 128],
                    wstg[:, sl * C + cc * 128: sl * C + (cc + 1) * 128], id_b[:]),
                    reads=[("wstg", sl), "id_b"], writes=[bkk])
            dst = wdst[:].rearrange("p (c i) -> p c i", i=nout)[:, :, rc * 128:(rc + 1) * 128]
            p.add("vector", lambda e, dst=dst, ncol_chunks=ncol_chunks, bk_=bk_: e.tensor_copy(
                dst, bk_[:, 0:ncol_chunks * 128].rearrange("p (c i) -> p c i", i=128)),
                reads=[bkk], writes=[dname])

    load_transposed(wq, wqT, "wqT", NCH, NCH, C)
    load_transposed(wk, wkT, "wkT", HD // 128, NCH, HD)
    load_transposed(wv, wvT, "wvT", HD // 128, NCH, HD)

    xbf2 = si_(nc.sbuf_tensor("xbf2", [128, 2 * C], BF16))

    def dma_x(t):
        sl = t % 2
        p.dma("gpsimd", xbf2[:, sl * C: sl * C + C], xin[t * 128:(t + 1) * 128, :], f"xl{sl}",
              writes=[("xbf2", sl)])

    def trx(t):
        sl = t % 2
        xsl = t % 8
        for cc in range(NCH):
            p.add("tensor", lambda e, cc=cc, sl=sl: e.transpose(
                ptrb2[:, cc * 128:(cc + 1) * 128],
                xbf2[:, sl * C + cc * 128: sl * C + (cc + 1) * 128], id_b[:]),
                reads=[("xbf2", sl), "id_b"], writes=["ps_trb2"])
        dst = xT[:].rearrange("p (c i) -> p c i", i=1024)[:, :, xsl * 128:(xsl + 1) * 128]
        p.add("scalar", lambda e, dst=dst: e.copy(dst, ptrb2[:].rearrange("p (c i) -> p c i", i=128)),
              reads=["ps_trb2"], writes=[("xT", xsl)])

    def qkv_mm(t, kv, par):
        if kv:
            wname, wT_, nout = "wkT", wkT, HD
            banks, keys = [ps_av], ["ps_av"]
        else:
            wname, wT_, nout = "wqT", wqT, C
            banks, keys = [qps[0], qps[1]], [("ps_s", 0), ("ps_s", 1)]
        nb = (nout + 511) // 512
        xsl = (t if kv else t + 4) % 8
        for b_ in range(nb):
            n0 = b_ * 512
            nw = min(512, nout - n0)
            for cc in range(NCH):
                p.add("tensor", lambda e, bk=banks[b_], cc=cc, n0=n0, nw=nw, xsl=xsl, wT_=wT_, nout=nout: e.matmul(
                    bk[:, 0:nw],
                    xT[:, cc * 1024 + xsl * 128: cc * 1024 + (xsl + 1) * 128],
                    wT_[:, cc * nout + n0: cc * nout + n0 + nw],
                    start=(cc == 0), stop=(cc == NCH - 1)),
                    reads=[("xT", xsl), wname], writes=[keys[b_]])
        for b_ in range(nb):
            n0 = b_ * 512
            nw = min(512, nout - n0)
            p.add("scalar", lambda e, bk=banks[b_], n0=n0, nw=nw, par=par: e.copy(
                qf[:, par * C + n0: par * C + n0 + nw], bk[:, 0:nw]),
                reads=[keys[b_]], writes=[("qf", par)])

    def v_mm(t):
        xsl = t % 8
        for cc in range(NCH):
            p.add("tensor", lambda e, cc=cc, xsl=xsl: e.matmul(
                ps_x[:, 0:HD], xT[:, cc * 1024 + xsl * 128: cc * 1024 + (xsl + 1) * 128],
                wvT[:, cc * HD:(cc + 1) * HD], start=(cc == 0), stop=(cc == NCH - 1)),
                reads=[("xT", xsl), "wvT"], writes=["ps_x"])
        p.add("scalar", lambda e, t=t: e.copy(vsb[:, t * HD:(t + 1) * HD], ps_x[:, 0:HD]),
              reads=["ps_x"], writes=["vsb"])

    def qkv_nr(t, kv, par, par4):
        nout = HD if kv else C
        nh_loc = 1 if kv else NH
        tok0 = t * 128 + (0 if kv else HALO)
        o = par * C
        qfv = qf[:, o:o + nout]
        qbv = qb[:, par4 * C: par4 * C + nout]
        msv = ms4[:, par * NH: par * NH + nh_loc]
        rsv = rstd4[:, par * NH: par * NH + nh_loc]
        for hh in range(nh_loc):
            p.add("scalar", lambda e, hh=hh: e.activation(
                qsq[:, 0:HD], qfv[:, hh * HD:(hh + 1) * HD], AF.Square,
                accum_out=msv[:, hh:hh + 1]),
                reads=[("qf", par)], writes=["qsq_dump", ("ms4", par)])
        p.add("scalar", lambda e: e.activation(rsv, msv, AF.Sqrt, scale=1.0 / HD, bias=eps6[:]),
              reads=[("ms4", par), "eps6"], writes=[("rstd4", par)])
        p.add("vector", lambda e: e.reciprocal(rsv, rsv),
              reads=[("rstd4", par)], writes=[("rstd4", par)])
        for hh in range(nh_loc):
            gam = gk_sb[:, 0:HD] if kv else gq_sb[:, hh * HD:(hh + 1) * HD]
            p.add("vector", lambda e, hh=hh, gam=gam: e.scalar_tensor_tensor(
                qbv[:, hh * HD:(hh + 1) * HD], qfv[:, hh * HD:(hh + 1) * HD],
                rsv[:, hh:hh + 1], gam, ALU.mult, ALU.mult),
                reads=[("qf", par), ("rstd4", par), "gk_sb" if kv else "gq_sb"],
                writes=[("qb", par4)])
        ct = cosb[:, tok0: tok0 + 128]
        st_ = sinb[:, tok0: tok0 + 128]
        qv = qbv.rearrange("p (h two i) -> p h two i", two=2, i=128)
        Av = qv[:, :, 0, :]
        Bv = qv[:, :, 1, :]
        ctb = ct.unsqueeze(1).broadcast_to([128, nh_loc, 128])
        stb = st_.unsqueeze(1).broadcast_to([128, nh_loc, 128])
        u3 = qrt[:, 0:nh_loc * 128].rearrange("p (h i) -> p h i", i=128)
        v3 = qrt[:, 512:512 + nh_loc * 128].rearrange("p (h i) -> p h i", i=128)
        p.add("vector", lambda e: e.tensor_tensor(u3, Av, stb, ALU.mult),
              reads=[("qb", par4), "sinb"], writes=["qrt"])
        p.add("vector", lambda e: e.tensor_tensor(Av, Av, ctb, ALU.mult),
              reads=[("qb", par4), "cosb"], writes=[("qb", par4)])
        p.add("vector", lambda e: e.tensor_tensor(v3, Bv, stb, ALU.mult),
              reads=[("qb", par4), "sinb"], writes=["qrt"])
        p.add("vector", lambda e: e.tensor_tensor(Bv, Bv, ctb, ALU.mult),
              reads=[("qb", par4), "cosb"], writes=[("qb", par4)])
        p.add("vector", lambda e: e.tensor_tensor(Av, Av, v3, ALU.subtract),
              reads=[("qb", par4), "qrt"], writes=[("qb", par4)])
        p.add("vector", lambda e: e.tensor_tensor(Bv, Bv, u3, ALU.add),
              reads=[("qb", par4), "qrt"], writes=[("qb", par4)])

    def qkv_tr(t, kv, par4):
        o = par4 * C
        n = 2 if kv else 8
        for u in range(n):
            p.add("tensor", lambda e, u=u: e.transpose(
                ptrb[:, u * 128:(u + 1) * 128], qb[:, o + u * 128: o + (u + 1) * 128],
                id_b[:]),
                reads=[("qb", par4), "id_b"], writes=["ps_trb"])
        if kv:
            dst = kT[:].rearrange("p (u i) -> p u i", i=T_KV)[:, :, t * 128:(t + 1) * 128]
            p.add("scalar", lambda e, dst=dst: e.copy(
                dst, ptrb[:, 0:256].rearrange("p (u i) -> p u i", i=128)),
                reads=["ps_trb"], writes=["kT"])
        else:
            dst = qT[:].rearrange("p (u i) -> p u i", i=T_OWN)[:, :, t * 128:(t + 1) * 128]
            p.add("scalar", lambda e, dst=dst: e.copy(
                dst, ptrb[:].rearrange("p (u i) -> p u i", i=128)),
                reads=["ps_trb"], writes=["qT"])

    units = []
    for t in range(5):
        units.append((t, True))
    for t in range(5, NKV):
        units.append((t - 5, False))
        units.append((t, True))
    units.append((15, False))
    dma_x(0)
    dma_x(1)
    trx(0)
    pend = []
    for i, (t, kv) in enumerate(units):
        par = i % 2
        par4 = i % 4
        qkv_mm(t, kv, par)
        if kv:
            v_mm(t)
        if len(pend) >= 2:
            qkv_tr(*pend.pop(0))
        nxt = units[i + 1] if i + 1 < len(units) else None
        if nxt is not None and nxt[1]:
            tn = nxt[0]
            if tn + 1 < NKV:
                dma_x(tn + 1)
            trx(tn)
        qkv_nr(t, kv, par, par4)
        pend.append((t, kv, par4))
    for pp in pend:
        qkv_tr(*pp)

    attn_inner.close()
    aoT = so_(nc.sbuf_tensor("aoT", [128, NCH * T_OWN], BF16))
    probs_f = so_(nc.sbuf_tensor("probs_f", [128, 2 * 640], F32))
    probsN = so_(nc.sbuf_tensor("probsN", [128, 4 * 640], BF16))
    probsTT = so_(nc.sbuf_tensor("probsTT", [128, 2 * 640], BF16))

    woT = so_(nc.sbuf_tensor("woT", [128, NCH * C], BF16))
    wg_sb = so_(nc.sbuf_tensor("wg_sb", [128, NCH * E], BF16))
    FREED = ["xT", "wqT", "wkT", "wvT", ("xbf2", 0), ("xbf2", 1), "qsq_dump", "qrt"] + [
        (n, par) for n in ("qf", "ms4", "rstd4") for par in (0, 1)] + [
        ("qb", s) for s in range(4)]
    for eng in ("scalar", "vector", "gpsimd", "tensor"):
        p.fence(eng, FREED)

    p.dma("gpsimd", wg_sb[:].rearrange("p (c e) -> p c e", e=E),
          wgi.rearrange("(c p) e -> p c e", p=128), "l_wg", writes=["wg_sb"])

    def wo_dma(rc):
        sl = rc % 2
        p.dma("gpsimd", wstg[:, sl * C: sl * C + C], wo[rc * 128:(rc + 1) * 128, :],
              f"wld{sl}", writes=[("wstg", sl)])

    def wo_tr(rc):
        sl = rc % 2
        bk_, bkk = (ptrb, "ps_trb") if rc % 2 == 0 else (ptrb2, "ps_trb2")
        for cc in range(NCH):
            p.add("tensor", lambda e, cc=cc, sl=sl, bk_=bk_: e.transpose(
                bk_[:, cc * 128:(cc + 1) * 128],
                wstg[:, sl * C + cc * 128: sl * C + (cc + 1) * 128], id_b[:]),
                reads=[("wstg", sl), "id_b"], writes=[bkk])
        dst = woT[:].rearrange("p (c i) -> p c i", i=C)[:, :, rc * 128:(rc + 1) * 128]
        p.add("vector", lambda e, dst=dst, bk_=bk_: e.tensor_copy(
            dst, bk_[:, 0:NCH * 128].rearrange("p (c i) -> p c i", i=128)),
            reads=[bkk], writes=["woT"])

    def att_sc(t, hh, pi):
        sp = qps[pi]
        spk = ("ps_s", pi)
        k0 = t * 128
        for u in range(2):
            p.add("tensor", lambda e, u=u, k0=k0, t=t, hh=hh, sp=sp: e.matmul(
                sp[:, 0:512],
                qT[:, (hh * 2 + u) * T_OWN + t * 128:(hh * 2 + u) * T_OWN + (t + 1) * 128],
                kT[:, u * T_KV + k0: u * T_KV + k0 + 512],
                start=(u == 0), stop=False),
                reads=["qT", "kT"], writes=[spk])
            p.add("tensor", lambda e, u=u, k0=k0, t=t, hh=hh, sp=sp: e.matmul(
                sp[:, 512:640],
                qT[:, (hh * 2 + u) * T_OWN + t * 128:(hh * 2 + u) * T_OWN + (t + 1) * 128],
                kT[:, u * T_KV + k0 + 512: u * T_KV + k0 + 640],
                start=(u == 0), stop=False),
                reads=["qT", "kT"], writes=[spk])
        p.add("tensor", lambda e, k0=k0, sp=sp: e.matmul(
            sp[:, 0:512], ones1b[:], kbias_b[:, k0:k0 + 512], start=False, stop=True),
            reads=["ones1b", "kbias_b"], writes=[spk])
        p.add("tensor", lambda e, k0=k0, sp=sp: e.matmul(
            sp[:, 512:640], ones1b[:], kbias_b[:, k0 + 512:k0 + 640],
            start=False, stop=True),
            reads=["ones1b", "kbias_b"], writes=[spk])

    def att_sm(t, hh, pi, pi4):
        sp = qps[pi]
        spk = ("ps_s", pi)
        pf = probs_f[:, pi * 640:(pi + 1) * 640]
        pn = probsN[:, pi4 * 640:(pi4 + 1) * 640]
        rmx = rowmax[:, pi:pi + 1]
        nmx = negmax[:, pi:pi + 1]
        dnm_ = denom[:, pi:pi + 1]
        rdn = rden[:, pi:pi + 1]
        p.add("vector", lambda e, sp=sp: e.tensor_add(sp[:, 0:128], sp[:, 0:128], umask[:]),
              reads=[spk, "umask"], writes=[spk])
        p.add("vector", lambda e, sp=sp: e.tensor_add(sp[:, 512:640], sp[:, 512:640], lmask[:]),
              reads=[spk, "lmask"], writes=[spk])
        p.add("vector", lambda e, sp=sp, rmx=rmx: e.reduce_max(rmx, sp[:], axis=AX.X),
              reads=[spk], writes=[("rowmax", pi)])
        p.add("vector", lambda e, rmx=rmx, nmx=nmx: e.tensor_scalar_mul(nmx, rmx, -0.0625),
              reads=[("rowmax", pi)], writes=[("negmax", pi)])
        p.add("scalar", lambda e, sp=sp, pf=pf, nmx=nmx: e.activation(
            pf, sp[:], AF.Exp, scale=0.0625, bias=nmx),
            reads=[spk, ("negmax", pi)], writes=[("probs_f", pi)])

    def att_sm2(t, hh, pi, pi4):
        pf = probs_f[:, pi * 640:(pi + 1) * 640]
        pn = probsN[:, pi4 * 640:(pi4 + 1) * 640]
        dnm_ = denom[:, pi:pi + 1]
        rdn = rden[:, pi:pi + 1]
        p.add("vector", lambda e, pf=pf, dnm_=dnm_: e.reduce_sum(dnm_, pf, axis=AX.X),
              reads=[("probs_f", pi)], writes=[("denom", pi)])
        p.add("vector", lambda e, dnm_=dnm_, rdn=rdn: e.reciprocal(rdn, dnm_),
              reads=[("denom", pi)], writes=[("rden", pi)])
        p.add("vector", lambda e, pf=pf, pn=pn, rdn=rdn: e.tensor_scalar(
            pn, pf, rdn, None, ALU.mult),
            reads=[("probs_f", pi), ("rden", pi)], writes=[("probsN", pi4)])

    def att_trav(t, hh, pi, pi4):
        pn = probsN[:, pi4 * 640:(pi4 + 1) * 640]
        ptt = probsTT[:, pi * 640:(pi + 1) * 640]
        avt = ps_av if pi == 0 else ps_x
        avk = "ps_av" if pi == 0 else "ps_x"
        bk_, bkk = (ptrb, "ps_trb") if pi == 0 else (ptrb2, "ps_trb2")
        for kt in range(5):
            p.add("tensor", lambda e, kt=kt, pn=pn, bk_=bk_: e.transpose(
                bk_[:, kt * 128:(kt + 1) * 128], pn[:, kt * 128:(kt + 1) * 128], id_b[:]),
                reads=[("probsN", pi4), "id_b"], writes=[bkk])
        p.add("scalar", lambda e, ptt=ptt, bk_=bk_: e.copy(ptt[:, 0:640], bk_[:, 0:640]),
              reads=[bkk], writes=[("probsTT", pi)])
        for u in range(2):
            for kt in range(5):
                p.add("tensor", lambda e, u=u, kt=kt, t=t, ptt=ptt, avt=avt: e.matmul(
                    avt[:, u * 128:(u + 1) * 128],
                    vsb[:, (t + kt) * HD + u * 128: (t + kt) * HD + (u + 1) * 128],
                    ptt[:, kt * 128:(kt + 1) * 128],
                    start=(u == 0 and kt == 0), stop=(u == 1 and kt == 4)),
                    reads=["vsb", ("probsTT", pi)], writes=[avk])
        for u in range(2):
            p.add("scalar", lambda e, t=t, hh=hh, u=u, avt=avt: e.copy(
                aoT[:, (hh * 2 + u) * T_OWN + t * 128:
                    (hh * 2 + u) * T_OWN + (t + 1) * 128],
                avt[:, u * 128:(u + 1) * 128]),
                reads=[avk], writes=["aoT"])

    aunits = [(t, hh) for t in range(NT) for hh in range(NH)]
    pend_a = []
    for i, (t, hh) in enumerate(aunits):
        pi = i % 2
        pi4 = i % 4
        att_sc(t, hh, pi)
        if i == 0:
            wo_dma(0)
            wo_dma(1)
        elif i <= NCH:
            wo_tr(i - 1)
            if i + 1 <= NCH - 1:
                wo_dma(i + 1)
        att_sm(t, hh, pi, pi4)
        if pend_a:
            att_sm2(*pend_a[-1])
        if len(pend_a) >= 2:
            att_trav(*pend_a.pop(0))
        pend_a.append((t, hh, pi, pi4))
    att_sm2(*pend_a[-1])
    for pa in pend_a:
        att_trav(*pa)

    def ph3_wo(t, pr):
        for b_ in range(2):
            for cc in range(NCH):
                p.add("tensor", lambda e, b_=b_, cc=cc, t=t: e.matmul(
                    qps[b_][:, 0:512], aoT[:, cc * T_OWN + t * 128: cc * T_OWN + (t + 1) * 128],
                    woT[:, cc * C + b_ * 512: cc * C + b_ * 512 + 512],
                    start=(cc == 0), stop=(cc == NCH - 1)),
                    reads=["aoT", "woT"], writes=[("ps_s", b_)])

    def ph3_ln(t, pr):
        o = pr * C
        xrv = xres[:, o:o + C]
        lnv = lnr[:, o:o + C]
        lsq = lnsq[:, 0:C]
        xpv = xpb[:, o:o + C]
        ls_ = lsum[:, pr:pr + 1]
        lm_ = lmean[:, pr:pr + 1]
        lv_ = lvar[:, pr:pr + 1]
        l2_ = lm2[:, pr:pr + 1]
        lr_ = lrstd[:, pr:pr + 1]
        p.dma("sync", xrv, xin[HALO + t * 128: HALO + (t + 1) * 128, :], f"xr{pr}",
              writes=[("xres", pr)])
        ls2 = lm2[:, pr:pr + 1]
        for b_ in range(2):
            acc = ls_ if b_ == 0 else ls2
            p.add("vector", lambda e, b_=b_, xrv=xrv, lnv=lnv, acc=acc: e.scalar_tensor_tensor(
                lnv[:, b_ * 512:(b_ + 1) * 512], xrv[:, b_ * 512:(b_ + 1) * 512], 1.0,
                qps[b_][:, 0:512], ALU.mult, ALU.add, accum_out=acc),
                reads=[("xres", pr), ("ps_s", b_)], writes=[("lnr", pr), ("lsum", pr)])
        p.add("vector", lambda e, ls_=ls_, ls2=ls2: e.tensor_add(ls_, ls_, ls2),
              reads=[("lsum", pr)], writes=[("lsum", pr)])
        p.add("vector", lambda e, ls_=ls_, lm_=lm_: e.tensor_scalar_mul(lm_, ls_, 1.0 / C),
              reads=[("lsum", pr)], writes=[("lmean", pr)])
        p.add("scalar", lambda e, lnv=lnv, lsq=lsq, lv_=lv_: e.activation(
            lsq, lnv, AF.Square, accum_out=lv_),
            reads=[("lnr", pr)], writes=[("lnsq", pr), ("lvar", pr)])
        p.add("vector", lambda e, lm_=lm_, l2_=l2_: e.tensor_mul(l2_, lm_, lm_),
              reads=[("lmean", pr)], writes=[("lm2", pr)])
        p.add("vector", lambda e, lv_=lv_: e.tensor_scalar(lv_, lv_, 1.0 / C, None, ALU.mult),
              reads=[("lvar", pr)], writes=[("lvar", pr)])
        p.add("vector", lambda e, lv_=lv_, l2_=l2_: e.tensor_sub(lv_, lv_, l2_),
              reads=[("lvar", pr), ("lm2", pr)], writes=[("lvar", pr)])
        p.add("scalar", lambda e, lv_=lv_, lr_=lr_: e.activation(lr_, lv_, AF.Sqrt, bias=eps5[:]),
              reads=[("lvar", pr), "eps5"], writes=[("lrstd", pr)])
        p.add("vector", lambda e, lr_=lr_: e.reciprocal(lr_, lr_),
              reads=[("lrstd", pr)], writes=[("lrstd", pr)])
        p.add("vector", lambda e, lnv=lnv, xpv=xpv, lm_=lm_, lr_=lr_: e.tensor_scalar(
            xpv, lnv, lm_, lr_, ALU.subtract, ALU.mult),
            reads=[("lnr", pr), ("lmean", pr), ("lrstd", pr)], writes=[("xpb", pr)])
        p.dma("sync", xprime[t * 128:(t + 1) * 128, :], xpv, "xpst",
              reads=[("xpb", pr)], writes=["xprime_dram"])
        def xs_store(e, t=t, xpv=xpv):
            return e.dma_start(out=xs[ds(hof[1] + t * 128, 128), :], in_=xpv)
        p.add("gpsimd", xs_store, reads=[("xpb", pr)], writes=["xs_dram"], dma=f"xsst{pr}")

    def ph3_rt(t, pr):
        o = pr * C
        xpv = xpb[:, o:o + C]
        xTs = xpTs[:, 0:C]
        bk_, bkk = (ptrb, "ps_trb") if pr == 0 else (ptrb2, "ps_trb2")
        for cc in range(NCH):
            p.add("tensor", lambda e, cc=cc, xpv=xpv, bk_=bk_: e.transpose(
                bk_[:, cc * 128:(cc + 1) * 128], xpv[:, cc * 128:(cc + 1) * 128], id_b[:]),
                reads=[("xpb", pr), "id_b"], writes=[bkk])
        p.add("scalar", lambda e, xTs=xTs, bk_=bk_: e.copy(xTs[:, 0:C], bk_[:]),
              reads=[bkk], writes=[("xpTs", pr)])
        for cc in range(NCH):
            p.add("tensor", lambda e, cc=cc, xTs=xTs: e.matmul(
                lgps, xTs[:, cc * 128:(cc + 1) * 128], wg_sb[:, cc * E:(cc + 1) * E],
                start=(cc == 0), stop=(cc == NCH - 1)),
                reads=[("xpTs", pr), "wg_sb"], writes=["ps_av"])
        lgv = lgall[:, t * E:(t + 1) * E]
        p.add("vector", lambda e, lgv=lgv: e.tensor_copy(lgv, lgps), reads=["ps_av"],
              writes=[("lgall", t)])
        p.add("vector", lambda e, lgv=lgv: e.reduce_max(m8[:], lgv, axis=AX.X),
              reads=[("lgall", t)], writes=["m8"])
        p.add("vector", lambda e, lgv=lgv: e.tensor_scalar(pr_t[:], lgv, m8[:], None,
                                                           ALU.is_ge),
              reads=[("lgall", t), "m8"], writes=["pr_t"])
        p.add("vector", lambda e: e.tensor_tensor(pr_t[:], pr_t[:], iotadesc[:, 0:E], ALU.mult),
              reads=["pr_t", "iotadesc"], writes=["pr_t"])
        p.add("vector", lambda e: e.reduce_max(pmax[:], pr_t[:], axis=AX.X),
              reads=["pr_t"], writes=["pmax"])
        p.add("vector", lambda e, t=t: e.tensor_scalar(
            mask_tm[:, t * E:(t + 1) * E], pr_t[:], pmax[:], None, ALU.is_ge),
            reads=["pr_t", "pmax"], writes=["mask_tm"])
        p.add("tensor", lambda e, t=t, bk_=bk_: e.transpose(
            bk_[0:8, 0:128], mask_tm[:, t * E:(t + 1) * E], id_b[:]),
            reads=["mask_tm", "id_b"], writes=[bkk])
        p.add("vector", lambda e, bk_=bk_: e.tensor_copy(mTst[:], bk_[0:8, 0:128]),
              reads=[bkk], writes=["mTst"])
        def ms_store(e, t=t):
            return e.dma_start(out=msh[:, ds(hof[1] + t * 128, 128)], in_=mTst[:])
        p.add("gpsimd", ms_store, reads=["mTst"], writes=["ms_dram"], dma="msst")

    prev3 = None
    for t in range(NT):
        pr = t % 2
        ph3_wo(t, pr)
        if prev3 is not None:
            ph3_rt(*prev3)
        ph3_ln(t, pr)
        prev3 = (t, pr)
    ph3_rt(*prev3)

    p.add("sync", lambda e: e.nop(),
          reads=["xs_dram", "ms_dram", "xprime_dram"])

    p.barrier()

    attn_outer.close()
    rtB = ExitStack()
    sb_ = rtB.enter_context
    iota512 = sb_(nc.sbuf_tensor("iota512", [128, 512], F32))
    cumtm = sb_(nc.sbuf_tensor("cumtm", [128, 32 * E], F32))
    mycum = sb_(nc.sbuf_tensor("mycum", [128, 32], F32))
    cmpb = sb_(nc.sbuf_tensor("cmpb", [128, 2 * 512], BF16))
    cntf = sb_(nc.sbuf_tensor("cntf", [1, 512], F32))
    nslot = sb_(nc.sbuf_tensor("nslot", [128, 1], F32))
    tmp256 = sb_(nc.sbuf_tensor("tmp256", [128, 256], F32))
    rt_scope = ExitStack()
    sr_ = rt_scope.enter_context
    ms_sb = sr_(nc.sbuf_tensor("ms_sb", [8, T], BF16))
    ones8 = sr_(nc.sbuf_tensor("ones8", [8, T], BF16))
    cums8 = sr_(nc.sbuf_tensor("cums8", [8, T], F32))
    lgm = sr_(nc.sbuf_tensor("lgm", [128, NT], F32))
    lgex = sr_(nc.sbuf_tensor("lgex", [128, NT * E], F32))
    p_t = sr_(nc.sbuf_tensor("p_t", [128, NT], F32))
    keep_t = sr_(nc.sbuf_tensor("keep_t", [128, NT], F32))
    en_t = sr_(nc.sbuf_tensor("en_t", [128, NT], F32))
    gidxf = sr_(nc.sbuf_tensor("gidxf", [128, NT], F32))
    pB = sr_(nc.sbuf_tensor("pB", [128, NT], F32))
    tmp128 = sr_(nc.sbuf_tensor("tmp128", [128, 128], F32))


    p.dma("sync", ms_sb[:], msh, "msld", writes=["ms_sb"])
    p.add("vector", lambda e: e.reduce_max(
        lgm[:], lgall[:].rearrange("p (t e) -> p t e", e=E), axis=AX.X),
        reads=[("lgall", t) for t in range(NT)] if False else ["lgall_all"], writes=["lgm"])
    p.add("vector", lambda e: e.tensor_tensor(
        lgex[:].rearrange("p (t e) -> p t e", e=E),
        lgall[:].rearrange("p (t e) -> p t e", e=E),
        lgm[:].unsqueeze(2).broadcast_to([128, NT, E]), ALU.subtract),
        reads=["lgall_all", "lgm"], writes=["lgex"])
    p.add("scalar", lambda e: e.activation(lgex[:], lgex[:], AF.Exp),
          reads=["lgex"], writes=["lgex"])
    p.add("vector", lambda e: e.reduce_sum(
        gate_t[:], lgex[:].rearrange("p (t e) -> p t e", e=E), axis=AX.X),
        reads=["lgex"], writes=["gate_t"])
    p.add("vector", lambda e: e.reciprocal(gate_t[:], gate_t[:]),
          reads=["gate_t"], writes=["gate_t"])
    p.dma("sync", iota512[:], i512_in, "l_i512", writes=["iota512"])
    p.add("vector", lambda e: e.memset(ones8[:], 1.0), writes=["ones8"])
    p.add("vector", lambda e: e.tensor_tensor_scan(cums8[:], ones8[:], ms_sb[:], 0.0,
                                                   ALU.mult, ALU.add),
          reads=["ones8", "ms_sb"], writes=["cums8"])
    for ch in range(32):
        cb, cbk = (ps_av, "ps_av") if ch % 2 == 0 else (ps_x, "ps_x")
        p.add("tensor", lambda e, ch=ch, cb=cb: e.transpose(
            cb[:, 0:8], cums8[:, ch * 128:(ch + 1) * 128], id_f[0:8, 0:8]),
            reads=["cums8", "id_f"], writes=[cbk])
        p.add("vector", lambda e, ch=ch, cb=cb: e.tensor_copy(
            cumtm[:, ch * E:(ch + 1) * E], cb[:, 0:8]),
            reads=[cbk], writes=["cumtm"])

    m3 = lambda ap: ap.rearrange("p (t e) -> p t e", e=E)
    p.add("vector", lambda e: e.tensor_tensor(tmp128[:], cumtm[:, 0:128], mask_tm[:], ALU.mult),
          reads=["cumtm", "mask_tm"], writes=["tmp128"])
    p.add("vector", lambda e: e.reduce_sum(p_t[:], m3(tmp128[:]), axis=AX.X),
          reads=["tmp128"], writes=["p_t"])
    p.add("vector", lambda e: e.tensor_tensor(tmp128[:], cumtm[:, 128:256], mask_tm[:], ALU.mult),
          reads=["cumtm", "mask_tm"], writes=["tmp128"])
    p.add("vector", lambda e: e.reduce_sum(pB[:], m3(tmp128[:]), axis=AX.X),
          reads=["tmp128"], writes=["pB"])
    p.add("vector", lambda e: e.tensor_scalar(p_t[:], p_t[:], hsel[:, 0:1], None, ALU.mult),
          reads=["p_t", "hsel"], writes=["p_t"])
    p.add("vector", lambda e: e.scalar_tensor_tensor(p_t[:], pB[:], hsel[:, 1:2], p_t[:],
                                                     ALU.mult, ALU.add),
          reads=["pB", "hsel", "p_t"], writes=["p_t"])
    p.add("vector", lambda e: e.tensor_tensor(tmp128[:], iotadesc[:], mask_tm[:], ALU.mult),
          reads=["iotadesc", "mask_tm"], writes=["tmp128"])
    p.add("vector", lambda e: e.reduce_sum(en_t[:], m3(tmp128[:]), axis=AX.X),
          reads=["tmp128"], writes=["en_t"])
    p.add("vector", lambda e: e.tensor_scalar_mul(en_t[:], en_t[:], -1.0),
          reads=["en_t"], writes=["en_t"])
    p.add("vector", lambda e: e.tensor_scalar_add(p_t[:], p_t[:], -1.0),
          reads=["p_t"], writes=["p_t"])
    p.add("vector", lambda e: e.tensor_scalar(keep_t[:], p_t[:], -1.0, 511.5,
                                              ALU.mult, ALU.add),
          reads=["p_t"], writes=["keep_t"])
    p.add("vector", lambda e: e.tensor_scalar(keep_t[:], keep_t[:], 0.0, None, ALU.is_ge),
          reads=["keep_t"], writes=["keep_t"])
    p.add("vector", lambda e: e.tensor_tensor(scale_t[:], gate_t[:], keep_t[:], ALU.mult),
          reads=["gate_t", "keep_t"], writes=["scale_t"])
    p.add("vector", lambda e: e.tensor_scalar(gidxf[:], en_t[:], 512.0, 4096.0,
                                              ALU.mult, ALU.add),
          reads=["en_t"], writes=["gidxf"])
    p.add("vector", lambda e: e.tensor_add(gidxf[:], gidxf[:], p_t[:]),
          reads=["gidxf", "p_t"], writes=["gidxf"])
    p.add("vector", lambda e: e.tensor_copy(gidx[:], gidxf[:]),
          reads=["gidxf"], writes=["gidx"])

    rt_scope.close()
    ffn_scope = ExitStack()
    sf_ = ffn_scope.enter_context
    w1sb = sf_(nc.sbuf_tensor("w1sb", [128, 2 * NCH * 2048], BF16))
    w2sb = sf_(nc.sbuf_tensor("w2sb", [128, 32 * C], BF16))
    hT = sf_(nc.sbuf_tensor("hT", [128, 32 * 512], BF16))
    slin = sf_(nc.sbuf_tensor("slin", [128, 2 * C], BF16))
    slinT = sf_(nc.sbuf_tensor("slinT", [128, 2 * NCH * 512], BF16))
    orow = sf_(nc.sbuf_tensor("orow", [128, C], BF16))
    for eng in ("vector", "scalar", "gpsimd", "sync", "tensor"):
        p.fence(eng, ["ms_sb", "ones8", "cums8", "lgm", "lgex",
                      ("wstg", 0), ("wstg", 1)])

    def dispatch(j):
        selj = sel4[:, j * E:(j + 1) * E].unsqueeze(1).broadcast_to([128, 32, E])
        p.add("vector", lambda e, selj=selj: e.tensor_tensor(
            tmp256[:].rearrange("p (c e) -> p c e", e=E),
            cumtm[:].rearrange("p (c e) -> p c e", e=E), selj, ALU.mult),
            reads=["cumtm", "sel4"], writes=["tmp256"])
        p.add("vector", lambda e: e.reduce_sum(
            mycum[:], tmp256[:].rearrange("p (c e) -> p c e", e=E), axis=AX.X),
            reads=["tmp256"], writes=["mycum"])
        bk = j % 2
        for ch in range(32):
            csl = ch % 2
            p.add("vector", lambda e, ch=ch, csl=csl: e.tensor_scalar(
                cmpb[:, csl * 512:(csl + 1) * 512], iota512[:], mycum[:, ch:ch + 1],
                None, ALU.is_ge),
                reads=["iota512", "mycum"], writes=[("cmpb", csl)])
            p.add("tensor", lambda e, ch=ch, bk=bk, csl=csl: e.matmul(
                qps[bk][0:1, 0:512], ones128b[:], cmpb[:, csl * 512:(csl + 1) * 512],
                start=(ch == 0), stop=(ch == 31)),
                reads=["ones128b", ("cmpb", csl)], writes=[("ps_s", bk)])
        p.add("vector", lambda e, bk=bk: e.tensor_copy(cntf[:], qps[bk][0:1, 0:512]),
              reads=[("ps_s", bk)], writes=["cntf"])
        for q in range(4):
            p.add("tensor", lambda e, q=q, bk=bk: e.transpose(
                qps[bk][:, 520:521], cntf[0:1, q * 128:(q + 1) * 128], id_f[0:1, 0:1]),
                reads=["cntf", "id_f"], writes=[("ps_s", bk)])
            p.add("vector", lambda e, bk=bk: e.tensor_copy(nslot[:], qps[bk][:, 520:521]),
                  reads=[("ps_s", bk)], writes=["nslot"])
            p.add("vector", lambda e, j=j, q=q: e.tensor_copy(
                didx[:, j * 4 + q: j * 4 + q + 1], nslot[:]),
                reads=["nslot"], writes=["didx"])
        for q in range(4):
            ssl = q % 2
            sv = slin[:, ssl * C:(ssl + 1) * C]
            p.add("vector", lambda e, sv=sv: e.memset(sv, 0.0), writes=[("slin", ssl)])
            p.add("gpsimd", lambda e, j=j, q=q, sv=sv: e.indirect_dma_start(
                out=sv, out_offset=None, in_=xs,
                in_offset=IndirectOffsetOnAxis(didx[:, j * 4 + q: j * 4 + q + 1], 0),
                bounds_check=T - 1, oob_is_err=False),
                reads=["didx", ("slin", ssl), "xs_dram"], writes=[("slin", ssl)],
                dma=f"disp{ssl}")
            db, dbk = (ptrb, "ps_trb") if q % 2 == 0 else (ptrb2, "ps_trb2")
            for cc in range(NCH):
                p.add("tensor", lambda e, cc=cc, db=db, sv=sv: e.transpose(
                    db[:, cc * 128:(cc + 1) * 128],
                    sv[:, cc * 128: (cc + 1) * 128], id_b[:]),
                    reads=[("slin", ssl), "id_b"], writes=[dbk])
            dst = slinT[:, (j % 2) * NCH * 512:].rearrange(
                "p (c i) -> p c i", i=512)[:, 0:NCH, q * 128:(q + 1) * 128]
            p.add("scalar", lambda e, dst=dst, db=db: e.copy(
                dst, db[:].rearrange("p (c i) -> p c i", i=128)),
                reads=[dbk], writes=[("slinT", j % 2)])

    if debug:
        p.dma("scalar", dbg_ms, ms_sb[:], "dbg", reads=["ms_sb"])
        p.add("vector", lambda e: e.tensor_copy(nslot[:], gidx[:, 0:1]),
              reads=["gidx"], writes=["nslot"])
        def dbg_rt_store(e):
            return e.dma_start(out=dbg_rt, in_=cumtm[:, 0:64])
        p.add("scalar", dbg_rt_store, reads=["cumtm"], dma="dbg")
        p.add("vector", lambda e: e.tensor_copy(gidxf[:], gidxf[:]), reads=["gidxf"])
        def dbg_di_store(e):
            return e.dma_start(out=dbg_di.bitcast(U32) if hasattr(dbg_di, "bitcast") else dbg_di,
                               in_=didx[:])
        p.add("scalar", dbg_di_store, reads=["didx"], dma="dbg")

    hps = [qps[0][:, 0:512], qps[1][:, 0:512]]

    def w1_load(j, hh):
        sl = (2 * j + hh) % 2
        for qq in range(4):
            p.add("gpsimd", lambda e, j=j, hh=hh, sl=sl, qq=qq: e.dma_start(
                out=w1sb[:, sl * 16384:(sl + 1) * 16384].rearrange(
                    "p (c h) -> p c h", h=2048)[:, :, qq * 512:(qq + 1) * 512],
                in_=w1[j, :, hh * 2048 + qq * 512: hh * 2048 + (qq + 1) * 512].rearrange(
                    "(c p) h -> p c h", p=128)),
                writes=[("w1sb", sl, qq)], dma=f"w1l{sl}q{qq}")

    def w2_load(j, qt):
        sl = (4 * j + qt) % 4
        p.add("gpsimd", lambda e, j=j, qt=qt, sl=sl: e.dma_start(
            out=w2sb[:, sl * 8192:(sl + 1) * 8192].rearrange("p (m c) -> p m c", c=C),
            in_=w2[j, qt * 1024:(qt + 1) * 1024, :].rearrange("(m p) c -> p m c", p=128)),
            writes=[("w2sb", sl)], dma=f"w2l{sl}")

    dispatch(0)
    for hh in range(2):
        w1_load(0, hh)
    for qt in range(4):
        w2_load(0, qt)

    for j in range(E_LOC):
        for hhalf in range(2):
            sl = (2 * j + hhalf) % 2
            for m in range(16):
                mg = hhalf * 16 + m
                bk = m % 2
                for cc in range(NCH):
                    p.add("tensor", lambda e, m=m, cc=cc, bk=bk, sl=sl, j=j: e.matmul(
                        hps[bk],
                        w1sb[:, sl * 16384 + cc * 2048 + m * 128:
                             sl * 16384 + cc * 2048 + (m + 1) * 128],
                        slinT[:, ((j % 2) * NCH + cc) * 512:((j % 2) * NCH + cc + 1) * 512],
                        start=(cc == 0), stop=(cc == NCH - 1)),
                        reads=[("w1sb", sl, m // 4), ("slinT", j % 2)],
                        writes=[("ps_s", bk)])
                p.add("scalar", lambda e, mg=mg, bk=bk: e.activation(
                    hT[:, mg * 512:(mg + 1) * 512], hps[bk], AF.Gelu_apprx_tanh),
                    reads=[("ps_s", bk)], writes=["hT"])
        if j + 1 < E_LOC:
            dispatch(j + 1)
            w1_load(j + 1, 0)
            w1_load(j + 1, 1)
        op2 = [ps_x[:], ps_av[:]]
        op2k = ["ps_x", "ps_av"]
        for q in range(4):
            for half in range(2):
                for m in range(32):
                    sl = (4 * j + m // 8) % 4
                    p.add("tensor", lambda e, q=q, half=half, m=m, sl=sl: e.matmul(
                        op2[half], hT[:, m * 512 + q * 128: m * 512 + (q + 1) * 128],
                        w2sb[:, sl * 8192 + (m % 8) * C + half * 512:
                             sl * 8192 + (m % 8) * C + half * 512 + 512],
                        start=(m == 0), stop=(m == 31)),
                        reads=["hT", ("w2sb", sl)], writes=[op2k[half]])
            for half in range(2):
                p.add("vector", lambda e, half=half: e.tensor_copy(
                    orow[:, half * 512:(half + 1) * 512], op2[half]),
                    reads=[op2k[half]], writes=["orow"])
            def so_store(e, j=j, q=q):
                return e.dma_start(out=so[ds(hof[0] + j * 512 + q * 128, 128), :],
                                   in_=orow[:])
            p.add("scalar", so_store, reads=["orow"], writes=["so_dram"], dma="sost")

        if j + 1 < E_LOC:
            for qt in range(4):
                w2_load(j + 1, qt)

    if debug:
        for r in range(16):
            p.dma("scalar", dbg_si[r * 128:(r + 1) * 128, :],
                  hT[:, 0:C] if False else slinT[:, 0:512], "dbg", reads=["slinT"])
            break
    p.add("sync", lambda e: e.nop(), reads=["so_dram"])

    p.barrier()

    ffn_scope.close()
    lnsq2 = sbp("lnsq2", [128, C], F32)
    compb = sbp("compb", [128, 16 * C], BF16)
    xres2 = sbp("xres2", [128, 16 * C], BF16)
    yrow = sbp("yrow", [128, 2 * C], F32)
    for eng in ("vector", "scalar", "gpsimd", "sync"):
        p.fence(eng, ["w1sb", "w2sb", "hT", ("slin", 0), ("slin", 1)])
    p.add("vector", lambda e: e.memset(compb[:], 0.0),
          writes=[("compb", t_) for t_ in range(NT)])
    for t in range(NT):
        cpv = compb[:, t * C:(t + 1) * C]
        xrv = xres2[:, t * C:(t + 1) * C]
        p.add("gpsimd", lambda e, t=t, cpv=cpv: e.indirect_dma_start(
            out=cpv, out_offset=None, in_=so,
            in_offset=IndirectOffsetOnAxis(gidx[:, t:t + 1], 0),
            bounds_check=T - 1, oob_is_err=False),
            reads=["gidx", "so_dram"], writes=[("compb", t)],
            dma=f"comb{t}")
        p.dma("sync", xrv, xprime[t * 128:(t + 1) * 128, :], f"xpl{t}",
              reads=["xprime_dram"], writes=[("xres2", t)])
    for t in range(NT):
        pr = t % 2
        cpv = compb[:, t * C:(t + 1) * C]
        xrv = xres2[:, t * C:(t + 1) * C]
        yv = yrow[:, pr * C:(pr + 1) * C]
        ls_ = lsum[:, pr:pr + 1]
        lm_ = lmean[:, pr:pr + 1]
        lv_ = lvar[:, pr:pr + 1]
        l2_ = lm2[:, pr:pr + 1]
        lr_ = lrstd[:, pr:pr + 1]
        p.add("vector", lambda e, t=t, cpv=cpv, xrv=xrv, yv=yv, ls_=ls_: e.scalar_tensor_tensor(
            yv, cpv, scale_t[:, t:t + 1], xrv, ALU.mult, ALU.add,
            accum_out=ls_),
            reads=[("compb", t), "scale_t", ("xres2", t)],
            writes=[("yrow", pr), ("lsum", pr)])
        p.add("vector", lambda e, ls_=ls_, lm_=lm_: e.tensor_scalar_mul(lm_, ls_, 1.0 / C),
              reads=[("lsum", pr)], writes=[("lmean", pr)])
        p.add("scalar", lambda e, yv=yv, lv_=lv_: e.activation(lnsq2[:], yv, AF.Square,
                                                               accum_out=lv_),
              reads=[("yrow", pr)], writes=["lnsq2", ("lvar", pr)])
        p.add("vector", lambda e, lm_=lm_, l2_=l2_: e.tensor_mul(l2_, lm_, lm_),
              reads=[("lmean", pr)], writes=[("lm2", pr)])
        p.add("vector", lambda e, lv_=lv_: e.tensor_scalar(lv_, lv_, 1.0 / C,
                                                           None, ALU.mult),
              reads=[("lvar", pr)], writes=[("lvar", pr)])
        p.add("vector", lambda e, lv_=lv_, l2_=l2_: e.tensor_sub(lv_, lv_, l2_),
              reads=[("lvar", pr), ("lm2", pr)], writes=[("lvar", pr)])
        p.add("scalar", lambda e, lv_=lv_, lr_=lr_: e.activation(lr_, lv_, AF.Sqrt,
                                                                 bias=eps5[:]),
              reads=[("lvar", pr), "eps5"], writes=[("lrstd", pr)])
        p.add("vector", lambda e, lr_=lr_: e.reciprocal(lr_, lr_),
              reads=[("lrstd", pr)], writes=[("lrstd", pr)])
        p.add("vector", lambda e, yv=yv, lm_=lm_, lr_=lr_: e.tensor_scalar(
            yv, yv, lm_, lr_, ALU.subtract, ALU.mult),
            reads=[("yrow", pr), ("lmean", pr), ("lrstd", pr)], writes=[("yrow", pr)])
        p.dma("sync", yout[t * 128:(t + 1) * 128, :], yv, f"yst{pr}", reads=[("yrow", pr)])

    p.emit()
    p.final_waits("sync", ["yst0", "yst1"] + (["dbg"] if debug else []))
    psA.close()
    stack.close()
    nc.finalize()
    return nc


def _perm(n):
    out = []
    for h0 in range(0, n, HD):
        out.extend(range(h0, h0 + HD, 2))
        out.extend(range(h0 + 1, h0 + HD, 2))
    return np.array(out)


def make_in_maps(inputs):
    x = np.asarray(inputs["L0_attn_x"])
    Wq = np.asarray(inputs["L0_attn_Wq"])
    Wk = np.asarray(inputs["L0_attn_Wk"])
    Wv = np.asarray(inputs["L0_attn_Wv"])
    Wo = np.asarray(inputs["L0_attn_Wo"])
    qg = np.asarray(inputs["L0_attn_q_gamma"])
    kg = np.asarray(inputs["L0_attn_k_gamma"])
    Wg = np.asarray(inputs["L0_ffn_Wg"])
    W1 = np.asarray(inputs["L0_ffn_W1"])
    W2 = np.asarray(inputs["L0_ffn_W2"])

    pq = _perm(C)
    pk = _perm(HD)
    Wq_p = np.ascontiguousarray(Wq[pq, :])
    Wk_p = np.ascontiguousarray(Wk[pk, :])
    gq_t = np.tile(qg[pk][None, :], (128, NH)).astype(np.float32)
    gk_t = np.tile(kg[pk][None, :], (128, 1)).astype(np.float32)
    inv = (1.0 / (1.0e6 ** (np.arange(0, HD, 2, dtype=np.float32) / HD))).astype(np.float32)
    ident = np.eye(128, dtype=np.float32)

    ins = []
    for c in range(8):
        b, h = c >> 1, c & 1
        lo = h * T_OWN - HALO
        xs_ = np.zeros((T_KV, C), dtype=np.float32)
        s0 = max(0, lo)
        xs_[s0 - lo:, :] = x[b, s0: lo + T_KV, :]
        pos = (lo + np.arange(T_KV)).astype(np.float32)
        freqs = np.outer(pos, inv).astype(np.float32)
        kb = np.zeros((1, T_KV), dtype=np.float32)
        if h == 0:
            kb[0, :HALO] = NEG
        sel = np.zeros((128, E_LOC * E), dtype=np.float32)
        for j in range(E_LOC):
            sel[:, j * E + (E_LOC * h + j)] = 1.0
        hs = np.zeros((128, 2), dtype=np.float32)
        hs[:, h] = 1.0
        ins.append({
            "x": xs_,
            "wq": Wq_p, "wk": Wk_p, "wv": np.ascontiguousarray(Wv),
            "wo": np.ascontiguousarray(Wo), "wg": np.ascontiguousarray(Wg),
            "w1": np.ascontiguousarray(W1[E_LOC * h: E_LOC * h + E_LOC]),
            "w2": np.ascontiguousarray(W2[E_LOC * h: E_LOC * h + E_LOC]),
            "gq": gq_t, "gk": gk_t,
            "cosb": np.cos(freqs).astype(np.float32),
            "sinb": np.sin(freqs).astype(np.float32),
            "kbias": kb, "sel4": sel, "hsel": hs, "idf": ident,
            "i512": np.tile(np.arange(512, dtype=np.float32)[None, :], (128, 1)),
        })
    return ins


def kernel(**inputs):
    ins = make_in_maps(inputs)
    nc = build()
    res = run_bass_kernel_spmd(nc, ins, core_ids=list(range(8)))
    y = np.stack([res.results[c]["y"] for c in range(8)]).reshape(B, T, C)
    return y


# version 32 (final)
# speedup vs baseline: 1.0446x; 1.0012x over previous
import numpy as np
import concourse.bass as bass
import concourse.bacc as bacc
import concourse.mybir as mybir
from concourse.bass import IndirectOffsetOnAxis, ds
from concourse.bass_utils import run_bass_kernel_spmd
from contextlib import ExitStack

F32 = mybir.dt.float32
BF16 = mybir.dt.bfloat16
I32 = mybir.dt.int32
U32 = mybir.dt.uint32
AF = mybir.ActivationFunctionType
ALU = mybir.AluOpType
AX = mybir.AxisListType

B, T, C = 4, 4096, 1024
NH, HD = 4, 256
T_OWN, HALO = 2048, 512
T_KV = T_OWN + HALO
NT = T_OWN // 128
NKV = T_KV // 128
NCH = C // 128
E, E_LOC, CAP = 8, 4, 512
H_FFN = 4096
NEG = -1.0e30
ENGS = ("sync", "vector", "scalar", "tensor", "gpsimd")


class Prog:
    def __init__(self, nc):
        self.nc = nc
        self.ops = []
        self.phases = []
        self._cur = []
        self.dma_counts = {}
        self._pending_fence = {}

    def add(self, engine, fn, reads=(), writes=(), dma=None):
        if dma is not None:
            dma = f"{dma}_{engine}"
        writes = tuple(writes) + tuple(self._pending_fence.pop(engine, ()))
        op = dict(engine=engine, fn=fn, reads=tuple(reads), writes=tuple(writes),
                  dma=dma, idx=len(self.ops))
        if dma is not None:
            self.dma_counts[dma] = self.dma_counts.get(dma, 0) + 1
            op["dma_count"] = self.dma_counts[dma]
        self.ops.append(op)
        self._cur.append(op)
        return op

    def dma(self, engine, out, in_, tag, reads=(), writes=()):
        self.add(engine, lambda e, o=out, i=in_: e.dma_start(out=o, in_=i),
                 reads=reads, writes=writes, dma=tag)

    def fence(self, engine, regions):
        self._pending_fence[engine] = tuple(self._pending_fence.get(engine, ())) + tuple(regions)

    def barrier(self):
        if self._cur:
            self.phases.append(("ops", self._cur))
            self._cur = []
        self.phases.append(("barrier", None))

    def emit(self):
        nc = self.nc
        if self._cur:
            self.phases.append(("ops", self._cur))
            self._cur = []
        last_writer, readers = {}, {}
        for op in self.ops:
            deps = set()
            for r in op["reads"]:
                w = last_writer.get(r)
                if w is not None:
                    deps.add(w["idx"])
                readers.setdefault(r, []).append(op)
            for wr in op["writes"]:
                w = last_writer.get(wr)
                if w is not None:
                    deps.add(w["idx"])
                for rd in readers.get(wr, ()):
                    deps.add(rd["idx"])
                readers[wr] = []
                last_writer[wr] = op
            deps.discard(op["idx"])
            pruned = []
            for d in deps:
                pd = self.ops[d]
                if (pd["dma"] is None and op["dma"] is None
                        and pd["engine"] == "tensor" and op["engine"] == "tensor"):
                    continue
                pruned.append(d)
            best = {}
            for d in pruned:
                pd = self.ops[d]
                key = ("d", pd["dma"]) if pd["dma"] is not None else ("e", pd["engine"])
                if key not in best or d > best[key]:
                    best[key] = d
            op["deps"] = sorted(best.values())
        need_inc = set()
        for op in self.ops:
            for d in op["deps"]:
                if self.ops[d]["dma"] is None:
                    need_inc.add(d)
        cnt = {e: 0 for e in ENGS}
        for op in self.ops:
            if op["dma"] is None:
                op["do_inc"] = op["idx"] in need_inc
                if op["do_inc"]:
                    cnt[op["engine"]] += 1
                op["inc_val"] = cnt[op["engine"]]
        esem = {e: nc.alloc_semaphore(f"es_{e}") for e in ENGS}
        dsem = {t: nc.alloc_semaphore(f"ds_{t}") for t in self.dma_counts}
        self.esem, self.dsem = esem, dsem
        for op in self.ops:
            waits = {}
            for d in op["deps"]:
                pd = self.ops[d]
                if pd["dma"] is not None:
                    sem, val = dsem[pd["dma"]], 16 * pd["dma_count"]
                else:
                    sem, val = esem[pd["engine"]], pd["inc_val"]
                k = id(sem)
                if k not in waits or waits[k][1] < val:
                    waits[k] = (sem, val)
            op["waits"] = list(waits.values())
        watermark = {}
        for kind, payload in self.phases:
            if kind == "barrier":
                nc.all_core_barrier()
                continue
            by_eng = {}
            for op in payload:
                by_eng.setdefault(op["engine"], []).append(op)
            with nc.Block() as blk:
                for ename, ops_ in by_eng.items():
                    def mk(ename, ops_):
                        def body(eng):
                            for op in ops_:
                                for sem, val in op["waits"]:
                                    if watermark.get((ename, id(sem)), 0) < val:
                                        eng.wait_ge(sem, val)
                                        watermark[(ename, id(sem))] = val
                                inst = op["fn"](eng)
                                if op["dma"] is not None:
                                    inst.then_inc(self.dsem[op["dma"]], 16)
                                elif op["do_inc"]:
                                    inst.then_inc(self.esem[ename], 1)
                        return body
                    getattr(blk, ename)(mk(ename, ops_))

    def final_waits(self, engine, tags):
        full = [t2 for t2 in self.dma_counts for t in tags if t2.startswith(t)]
        def fw(eng):
            for t in full:
                eng.wait_ge(self.dsem[t], 16 * self.dma_counts[t])
        with self.nc.Block() as blk:
            getattr(blk, engine)(fw)


def build(debug=False, num_devices=8):
    nc = bacc.Bacc("TRN2", target_bir_lowering=False, debug=False, num_devices=num_devices)

    xin = nc.dram_tensor("x", [T_KV, C], F32, kind="ExternalInput").ap()
    wq = nc.dram_tensor("wq", [C, C], F32, kind="ExternalInput").ap()
    wk = nc.dram_tensor("wk", [HD, C], F32, kind="ExternalInput").ap()
    wv = nc.dram_tensor("wv", [HD, C], F32, kind="ExternalInput").ap()
    wo = nc.dram_tensor("wo", [C, C], F32, kind="ExternalInput").ap()
    wgi = nc.dram_tensor("wg", [C, E], F32, kind="ExternalInput").ap()
    w1 = nc.dram_tensor("w1", [E_LOC, C, H_FFN], F32, kind="ExternalInput").ap()
    w2 = nc.dram_tensor("w2", [E_LOC, H_FFN, C], F32, kind="ExternalInput").ap()
    gq_in = nc.dram_tensor("gq", [128, C], F32, kind="ExternalInput").ap()
    gk_in = nc.dram_tensor("gk", [128, HD], F32, kind="ExternalInput").ap()
    cos_in = nc.dram_tensor("cosb", [T_KV, 128], F32, kind="ExternalInput").ap()
    sin_in = nc.dram_tensor("sinb", [T_KV, 128], F32, kind="ExternalInput").ap()
    kb_in = nc.dram_tensor("kbias", [1, T_KV], F32, kind="ExternalInput").ap()
    sel_in = nc.dram_tensor("sel4", [128, E_LOC * E], F32, kind="ExternalInput").ap()
    hsel_in = nc.dram_tensor("hsel", [128, 2], F32, kind="ExternalInput").ap()
    idf_in = nc.dram_tensor("idf", [128, 128], F32, kind="ExternalInput").ap()
    i512_in = nc.dram_tensor("i512", [128, 512], F32, kind="ExternalInput").ap()
    yout = nc.dram_tensor("y", [T_OWN, C], F32, kind="ExternalOutput").ap()

    xprime = nc.dram_tensor("xprime", [T_OWN, C], BF16,
                            kind=("ExternalOutput" if debug else "Internal")).ap()
    xs = nc.dram_tensor("xs_sh", [T, C], BF16, kind="Internal", addr_space="Shared").ap()
    msh = nc.dram_tensor("ms_sh", [E, T], BF16, kind="Internal", addr_space="Shared").ap()
    so = nc.dram_tensor("so_sh", [T, C], BF16, kind="Internal", addr_space="Shared").ap()
    if debug:
        dbg_ms = nc.dram_tensor("dbg_ms", [E, T], F32, kind="ExternalOutput").ap()
        dbg_rt = nc.dram_tensor("dbg_rt", [128, 64], F32, kind="ExternalOutput").ap()
        dbg_di = nc.dram_tensor("dbg_di", [128, 16], F32, kind="ExternalOutput").ap()
        dbg_si = nc.dram_tensor("dbg_si", [E_LOC * CAP, C], F32, kind="ExternalOutput").ap()

    p = Prog(nc)
    hof = {}

    def snap_hof(e):
        hof[0] = e.snap((e.partition_id() % 2) * T_OWN, min_val=0, max_val=T_OWN)

    def snap_hofg(e):
        hof[1] = e.snap((e.partition_id() % 2) * T_OWN, min_val=0, max_val=T_OWN)
        return e.nop()
    stack = ExitStack()
    sbp = lambda n, s, d: nc.alloc_sbuf_tensor(n, s, d)
    sbs = lambda n, s, d: stack.enter_context(nc.sbuf_tensor(n, s, d))

    id_f = sbp("id_f", [128, 128], F32)
    id_b = sbp("id_b", [128, 128], BF16)
    ones1b = sbp("ones1b", [1, 128], BF16)
    ones128b = sbp("ones128b", [128, 1], BF16)
    iotadesc = sbp("iotadesc", [128, 128], F32)
    sel4 = sbp("sel4_sb", [128, E_LOC * E], F32)
    hsel = sbp("hsel_sb", [128, 2], F32)
    eps6 = sbp("eps6", [128, 1], F32)
    eps5 = sbp("eps5", [128, 1], F32)
    mask_tm = sbp("mask_tm", [128, NT * E], BF16)
    lgall = sbp("lgall", [128, NT * E], F32)
    gate_t = sbp("gate_t", [128, NT], F32)
    scale_t = sbp("scale_t", [128, NT], F32)
    gidx = sbp("gidx", [128, NT], U32)
    didx = sbp("didx", [128, 16], U32)
    lsum = sbp("lsum", [128, 2], F32)
    lmean = sbp("lmean", [128, 2], F32)
    lvar = sbp("lvar", [128, 2], F32)
    lm2 = sbp("lm2", [128, 2], F32)
    lrstd = sbp("lrstd", [128, 2], F32)
    tmpa = sbp("tmpa", [128, E], F32)
    tmpb2 = sbp("tmpb2", [128, E], F32)

    ptrb = nc.alloc_psum_tensor("ps_trb", [128, 1024], BF16)
    ptrb2 = nc.alloc_psum_tensor("ps_trb2", [128, 1024], BF16)

    p.add("scalar", snap_hof)
    p.add("gpsimd", snap_hofg)
    p.dma("sync", id_f[:], idf_in, "l_idf", writes=["id_f"])
    p.dma("gpsimd", id_b[:], idf_in, "l_idb", writes=["id_b"])
    p.dma("sync", sel4[:], sel_in, "l_sel", writes=["sel4"])
    p.dma("sync", hsel[:], hsel_in, "l_hsel", writes=["hsel"])
    for tile, val in ((ones1b, 1.0), (ones128b, 1.0),
                      (eps6, 1e-6), (eps5, 1e-5)):
        p.add("vector", lambda e, tl=tile, v=val: e.memset(tl[:], v),
              writes=[tile.name])
    p.add("gpsimd", lambda e: e.iota(iotadesci[:], pattern=[[0, 16], [-1, 8]], base=8,
                                     channel_multiplier=0), writes=["iotadesci"])
    p.add("vector", lambda e: e.tensor_copy(iotadesc[:], iotadesci[:]),
          reads=["iotadesci"], writes=["iotadesc"])

    attn_outer = ExitStack()
    so_ = attn_outer.enter_context
    kbias_b = so_(nc.sbuf_tensor("kbias_b", [1, T_KV], BF16))
    umask = so_(nc.sbuf_tensor("umask", [128, 128], F32))
    lmask = so_(nc.sbuf_tensor("lmask", [128, 128], F32))
    mTst = so_(nc.sbuf_tensor("mTst", [8, 128], BF16))
    iotadesci = so_(nc.sbuf_tensor("iotadesci", [128, 128], I32))
    gq_sb = so_(nc.sbuf_tensor("gq_sb", [128, C], BF16))
    gk_sb = so_(nc.sbuf_tensor("gk_sb", [128, HD], BF16))
    cosb = so_(nc.sbuf_tensor("cosb_sb", [128, T_KV], BF16))
    sinb = so_(nc.sbuf_tensor("sinb_sb", [128, T_KV], BF16))
    qT = so_(nc.sbuf_tensor("qT", [128, NH * 2 * T_OWN], BF16))
    kT = so_(nc.sbuf_tensor("kT", [128, 2 * T_KV], BF16))
    vsb = so_(nc.sbuf_tensor("vsb", [128, NKV * HD], BF16))
    wstg = so_(nc.sbuf_tensor("wstg", [128, 2 * C], BF16))
    rowmax = so_(nc.sbuf_tensor("rowmax", [128, 2], F32))
    negmax = so_(nc.sbuf_tensor("negmax", [128, 2], F32))
    denom = so_(nc.sbuf_tensor("denom", [128, 2], F32))
    rden = so_(nc.sbuf_tensor("rden", [128, 2], F32))
    xres = so_(nc.sbuf_tensor("xres", [128, 2 * C], F32))
    lnr = so_(nc.sbuf_tensor("lnr", [128, 2 * C], F32))
    xpb = so_(nc.sbuf_tensor("xpb", [128, 2 * C], BF16))
    xpTs = so_(nc.sbuf_tensor("xpTs", [128, C], BF16))
    lnsq = so_(nc.sbuf_tensor("lnsq", [128, C], F32))
    m8 = so_(nc.sbuf_tensor("m8", [128, 1], F32))
    dnm = so_(nc.sbuf_tensor("dnm", [128, 1], F32))
    pr_t = so_(nc.sbuf_tensor("pr_t", [128, E], F32))
    pmax = so_(nc.sbuf_tensor("pmax", [128, 1], F32))

    psA = ExitStack()
    qps = [psA.enter_context(nc.psum_tensor(f"ps_s{i}", [128, 640], F32)) for i in range(2)]
    ps_av = psA.enter_context(nc.psum_tensor("ps_av", [128, 512], F32))
    ps_x = psA.enter_context(nc.psum_tensor("ps_x", [128, 512], F32))
    lgps = ps_av[:, 128:136]

    p.dma("gpsimd", gq_sb[:], gq_in, "l_gq", writes=["gq_sb"])
    p.dma("gpsimd", gk_sb[:], gk_in, "l_gk", writes=["gk_sb"])
    p.dma("gpsimd", kbias_b[:], kb_in, "l_kb", writes=["kbias_b"])
    for tile, val in ((umask, 0.0), (lmask, 0.0)):
        p.add("vector", lambda e, tl=tile, v=val: e.memset(tl[:], v), writes=[tile.name])
    p.add("gpsimd", lambda e: e.affine_select(umask[:], umask[:], pattern=[[1, 128]],
                                              compare_op=ALU.is_ge, fill=NEG, base=0,
                                              channel_multiplier=-1),
          reads=["umask"], writes=["umask"])
    p.add("gpsimd", lambda e: e.affine_select(lmask[:], lmask[:], pattern=[[-1, 128]],
                                              compare_op=ALU.is_ge, fill=NEG, base=0,
                                              channel_multiplier=1),
          reads=["lmask"], writes=["lmask"])
    p.dma("gpsimd", cosb[:].rearrange("p (t i) -> p t i", i=128),
          cos_in.rearrange("(t p) i -> p t i", p=128), "l_cos", writes=["cosb"])
    p.dma("gpsimd", sinb[:].rearrange("p (t i) -> p t i", i=128),
          sin_in.rearrange("(t p) i -> p t i", p=128), "l_sin", writes=["sinb"])

    attn_inner = ExitStack()
    si_ = attn_inner.enter_context
    wqT = si_(nc.sbuf_tensor("wqT", [128, NCH * C], BF16))
    wkT = si_(nc.sbuf_tensor("wkT", [128, NCH * HD], BF16))
    wvT = si_(nc.sbuf_tensor("wvT", [128, NCH * HD], BF16))
    xT = si_(nc.sbuf_tensor("xT", [128, NCH * 1024], BF16))
    qf = si_(nc.sbuf_tensor("qf", [128, 2 * C], F32))
    qsq = si_(nc.sbuf_tensor("qsq", [128, HD], F32))
    qrt = si_(nc.sbuf_tensor("qrt", [128, C], BF16))
    qb = si_(nc.sbuf_tensor("qb", [128, 4 * C], BF16))
    ms4 = si_(nc.sbuf_tensor("ms4", [128, 2 * NH], F32))
    rstd4 = si_(nc.sbuf_tensor("rstd4", [128, 2 * NH], F32))


    def load_transposed(wsrc, wdst, dname, nrow_chunks, ncol_chunks, nout):
        for rc in range(nrow_chunks):
            sl = rc % 2
            bk_, bkk = (ptrb, "ps_trb") if rc % 2 == 0 else (ptrb2, "ps_trb2")
            p.dma("gpsimd", wstg[:, sl * C: sl * C + C], wsrc[rc * 128:(rc + 1) * 128, :],
                  f"wld{sl}", writes=[("wstg", sl)])
            for cc in range(ncol_chunks):
                p.add("tensor", lambda e, cc=cc, sl=sl, bk_=bk_: e.transpose(
                    bk_[:, cc * 128:(cc + 1) * 128],
                    wstg[:, sl * C + cc * 128: sl * C + (cc + 1) * 128], id_b[:]),
                    reads=[("wstg", sl), "id_b"], writes=[bkk])
            dst = wdst[:].rearrange("p (c i) -> p c i", i=nout)[:, :, rc * 128:(rc + 1) * 128]
            p.add("vector", lambda e, dst=dst, ncol_chunks=ncol_chunks, bk_=bk_: e.tensor_copy(
                dst, bk_[:, 0:ncol_chunks * 128].rearrange("p (c i) -> p c i", i=128)),
                reads=[bkk], writes=[dname])

    load_transposed(wq, wqT, "wqT", NCH, NCH, C)
    load_transposed(wk, wkT, "wkT", HD // 128, NCH, HD)
    load_transposed(wv, wvT, "wvT", HD // 128, NCH, HD)

    xbf2 = si_(nc.sbuf_tensor("xbf2", [128, 2 * C], BF16))

    def dma_x(t):
        sl = t % 2
        p.dma("gpsimd", xbf2[:, sl * C: sl * C + C], xin[t * 128:(t + 1) * 128, :], f"xl{sl}",
              writes=[("xbf2", sl)])

    def trx(t):
        sl = t % 2
        xsl = t % 8
        for cc in range(NCH):
            p.add("tensor", lambda e, cc=cc, sl=sl: e.transpose(
                ptrb2[:, cc * 128:(cc + 1) * 128],
                xbf2[:, sl * C + cc * 128: sl * C + (cc + 1) * 128], id_b[:]),
                reads=[("xbf2", sl), "id_b"], writes=["ps_trb2"])
        dst = xT[:].rearrange("p (c i) -> p c i", i=1024)[:, :, xsl * 128:(xsl + 1) * 128]
        p.add("scalar", lambda e, dst=dst: e.copy(dst, ptrb2[:].rearrange("p (c i) -> p c i", i=128)),
              reads=["ps_trb2"], writes=[("xT", xsl)])

    def qkv_mm(t, kv, par):
        if kv:
            wname, wT_, nout = "wkT", wkT, HD
            banks, keys = [ps_av], ["ps_av"]
        else:
            wname, wT_, nout = "wqT", wqT, C
            banks, keys = [qps[0], qps[1]], [("ps_s", 0), ("ps_s", 1)]
        nb = (nout + 511) // 512
        xsl = (t if kv else t + 4) % 8
        for b_ in range(nb):
            n0 = b_ * 512
            nw = min(512, nout - n0)
            for cc in range(NCH):
                p.add("tensor", lambda e, bk=banks[b_], cc=cc, n0=n0, nw=nw, xsl=xsl, wT_=wT_, nout=nout: e.matmul(
                    bk[:, 0:nw],
                    xT[:, cc * 1024 + xsl * 128: cc * 1024 + (xsl + 1) * 128],
                    wT_[:, cc * nout + n0: cc * nout + n0 + nw],
                    start=(cc == 0), stop=(cc == NCH - 1)),
                    reads=[("xT", xsl), wname], writes=[keys[b_]])
        for b_ in range(nb):
            n0 = b_ * 512
            nw = min(512, nout - n0)
            p.add("scalar", lambda e, bk=banks[b_], n0=n0, nw=nw, par=par: e.copy(
                qf[:, par * C + n0: par * C + n0 + nw], bk[:, 0:nw]),
                reads=[keys[b_]], writes=[("qf", par)])

    def v_mm(t):
        xsl = t % 8
        for cc in range(NCH):
            p.add("tensor", lambda e, cc=cc, xsl=xsl: e.matmul(
                ps_x[:, 0:HD], xT[:, cc * 1024 + xsl * 128: cc * 1024 + (xsl + 1) * 128],
                wvT[:, cc * HD:(cc + 1) * HD], start=(cc == 0), stop=(cc == NCH - 1)),
                reads=[("xT", xsl), "wvT"], writes=["ps_x"])
        p.add("scalar", lambda e, t=t: e.copy(vsb[:, t * HD:(t + 1) * HD], ps_x[:, 0:HD]),
              reads=["ps_x"], writes=["vsb"])

    def qkv_nr(t, kv, par, par4):
        nout = HD if kv else C
        nh_loc = 1 if kv else NH
        tok0 = t * 128 + (0 if kv else HALO)
        o = par * C
        qfv = qf[:, o:o + nout]
        qbv = qb[:, par4 * C: par4 * C + nout]
        msv = ms4[:, par * NH: par * NH + nh_loc]
        rsv = rstd4[:, par * NH: par * NH + nh_loc]
        for hh in range(nh_loc):
            p.add("scalar", lambda e, hh=hh: e.activation(
                qsq[:, 0:HD], qfv[:, hh * HD:(hh + 1) * HD], AF.Square,
                accum_out=msv[:, hh:hh + 1]),
                reads=[("qf", par)], writes=["qsq_dump", ("ms4", par)])
        p.add("scalar", lambda e: e.activation(rsv, msv, AF.Sqrt, scale=1.0 / HD, bias=eps6[:]),
              reads=[("ms4", par), "eps6"], writes=[("rstd4", par)])
        p.add("vector", lambda e: e.reciprocal(rsv, rsv),
              reads=[("rstd4", par)], writes=[("rstd4", par)])
        for hh in range(nh_loc):
            gam = gk_sb[:, 0:HD] if kv else gq_sb[:, hh * HD:(hh + 1) * HD]
            p.add("vector", lambda e, hh=hh, gam=gam: e.scalar_tensor_tensor(
                qbv[:, hh * HD:(hh + 1) * HD], qfv[:, hh * HD:(hh + 1) * HD],
                rsv[:, hh:hh + 1], gam, ALU.mult, ALU.mult),
                reads=[("qf", par), ("rstd4", par), "gk_sb" if kv else "gq_sb"],
                writes=[("qb", par4)])
        ct = cosb[:, tok0: tok0 + 128]
        st_ = sinb[:, tok0: tok0 + 128]
        qv = qbv.rearrange("p (h two i) -> p h two i", two=2, i=128)
        Av = qv[:, :, 0, :]
        Bv = qv[:, :, 1, :]
        ctb = ct.unsqueeze(1).broadcast_to([128, nh_loc, 128])
        stb = st_.unsqueeze(1).broadcast_to([128, nh_loc, 128])
        u3 = qrt[:, 0:nh_loc * 128].rearrange("p (h i) -> p h i", i=128)
        v3 = qrt[:, 512:512 + nh_loc * 128].rearrange("p (h i) -> p h i", i=128)
        p.add("vector", lambda e: e.tensor_tensor(u3, Av, stb, ALU.mult),
              reads=[("qb", par4), "sinb"], writes=["qrt"])
        p.add("vector", lambda e: e.tensor_tensor(Av, Av, ctb, ALU.mult),
              reads=[("qb", par4), "cosb"], writes=[("qb", par4)])
        p.add("vector", lambda e: e.tensor_tensor(v3, Bv, stb, ALU.mult),
              reads=[("qb", par4), "sinb"], writes=["qrt"])
        p.add("vector", lambda e: e.tensor_tensor(Bv, Bv, ctb, ALU.mult),
              reads=[("qb", par4), "cosb"], writes=[("qb", par4)])
        p.add("vector", lambda e: e.tensor_tensor(Av, Av, v3, ALU.subtract),
              reads=[("qb", par4), "qrt"], writes=[("qb", par4)])
        p.add("vector", lambda e: e.tensor_tensor(Bv, Bv, u3, ALU.add),
              reads=[("qb", par4), "qrt"], writes=[("qb", par4)])

    def qkv_tr(t, kv, par4):
        o = par4 * C
        n = 2 if kv else 8
        for u in range(n):
            p.add("tensor", lambda e, u=u: e.transpose(
                ptrb[:, u * 128:(u + 1) * 128], qb[:, o + u * 128: o + (u + 1) * 128],
                id_b[:]),
                reads=[("qb", par4), "id_b"], writes=["ps_trb"])
        if kv:
            dst = kT[:].rearrange("p (u i) -> p u i", i=T_KV)[:, :, t * 128:(t + 1) * 128]
            p.add("scalar", lambda e, dst=dst: e.copy(
                dst, ptrb[:, 0:256].rearrange("p (u i) -> p u i", i=128)),
                reads=["ps_trb"], writes=["kT"])
        else:
            dst = qT[:].rearrange("p (u i) -> p u i", i=T_OWN)[:, :, t * 128:(t + 1) * 128]
            p.add("scalar", lambda e, dst=dst: e.copy(
                dst, ptrb[:].rearrange("p (u i) -> p u i", i=128)),
                reads=["ps_trb"], writes=["qT"])

    units = []
    for t in range(5):
        units.append((t, True))
    for t in range(5, NKV):
        units.append((t - 5, False))
        units.append((t, True))
    units.append((15, False))
    dma_x(0)
    dma_x(1)
    trx(0)
    pend = []
    for i, (t, kv) in enumerate(units):
        par = i % 2
        par4 = i % 4
        qkv_mm(t, kv, par)
        if kv:
            v_mm(t)
        if len(pend) >= 2:
            qkv_tr(*pend.pop(0))
        nxt = units[i + 1] if i + 1 < len(units) else None
        if nxt is not None and nxt[1]:
            tn = nxt[0]
            if tn + 1 < NKV:
                dma_x(tn + 1)
            trx(tn)
        qkv_nr(t, kv, par, par4)
        pend.append((t, kv, par4))
    for pp in pend:
        qkv_tr(*pp)

    attn_inner.close()
    aoT = so_(nc.sbuf_tensor("aoT", [128, NCH * T_OWN], BF16))
    probs_f = so_(nc.sbuf_tensor("probs_f", [128, 2 * 640], F32))
    probsN = so_(nc.sbuf_tensor("probsN", [128, 4 * 640], BF16))
    probsTT = so_(nc.sbuf_tensor("probsTT", [128, 2 * 640], BF16))

    woT = so_(nc.sbuf_tensor("woT", [128, NCH * C], BF16))
    wg_sb = so_(nc.sbuf_tensor("wg_sb", [128, NCH * E], BF16))
    FREED = ["xT", "wqT", "wkT", "wvT", ("xbf2", 0), ("xbf2", 1), "qsq_dump", "qrt"] + [
        (n, par) for n in ("qf", "ms4", "rstd4") for par in (0, 1)] + [
        ("qb", s) for s in range(4)]
    for eng in ("scalar", "vector", "gpsimd", "tensor"):
        p.fence(eng, FREED)

    p.dma("gpsimd", wg_sb[:].rearrange("p (c e) -> p c e", e=E),
          wgi.rearrange("(c p) e -> p c e", p=128), "l_wg", writes=["wg_sb"])

    def wo_dma(rc):
        sl = rc % 2
        p.dma("gpsimd", wstg[:, sl * C: sl * C + C], wo[rc * 128:(rc + 1) * 128, :],
              f"wld{sl}", writes=[("wstg", sl)])

    def wo_tr(rc):
        sl = rc % 2
        bk_, bkk = (ptrb, "ps_trb") if rc % 2 == 0 else (ptrb2, "ps_trb2")
        for cc in range(NCH):
            p.add("tensor", lambda e, cc=cc, sl=sl, bk_=bk_: e.transpose(
                bk_[:, cc * 128:(cc + 1) * 128],
                wstg[:, sl * C + cc * 128: sl * C + (cc + 1) * 128], id_b[:]),
                reads=[("wstg", sl), "id_b"], writes=[bkk])
        dst = woT[:].rearrange("p (c i) -> p c i", i=C)[:, :, rc * 128:(rc + 1) * 128]
        p.add("vector", lambda e, dst=dst, bk_=bk_: e.tensor_copy(
            dst, bk_[:, 0:NCH * 128].rearrange("p (c i) -> p c i", i=128)),
            reads=[bkk], writes=["woT"])

    def att_sc(t, hh, pi):
        sp = qps[pi]
        spk = ("ps_s", pi)
        k0 = t * 128
        for u in range(2):
            p.add("tensor", lambda e, u=u, k0=k0, t=t, hh=hh, sp=sp: e.matmul(
                sp[:, 0:512],
                qT[:, (hh * 2 + u) * T_OWN + t * 128:(hh * 2 + u) * T_OWN + (t + 1) * 128],
                kT[:, u * T_KV + k0: u * T_KV + k0 + 512],
                start=(u == 0), stop=False),
                reads=["qT", "kT"], writes=[spk])
            p.add("tensor", lambda e, u=u, k0=k0, t=t, hh=hh, sp=sp: e.matmul(
                sp[:, 512:640],
                qT[:, (hh * 2 + u) * T_OWN + t * 128:(hh * 2 + u) * T_OWN + (t + 1) * 128],
                kT[:, u * T_KV + k0 + 512: u * T_KV + k0 + 640],
                start=(u == 0), stop=False),
                reads=["qT", "kT"], writes=[spk])
        p.add("tensor", lambda e, k0=k0, sp=sp: e.matmul(
            sp[:, 0:512], ones1b[:], kbias_b[:, k0:k0 + 512], start=False, stop=True),
            reads=["ones1b", "kbias_b"], writes=[spk])
        p.add("tensor", lambda e, k0=k0, sp=sp: e.matmul(
            sp[:, 512:640], ones1b[:], kbias_b[:, k0 + 512:k0 + 640],
            start=False, stop=True),
            reads=["ones1b", "kbias_b"], writes=[spk])

    def att_sm(t, hh, pi, pi4):
        sp = qps[pi]
        spk = ("ps_s", pi)
        pf = probs_f[:, pi * 640:(pi + 1) * 640]
        pn = probsN[:, pi4 * 640:(pi4 + 1) * 640]
        rmx = rowmax[:, pi:pi + 1]
        nmx = negmax[:, pi:pi + 1]
        dnm_ = denom[:, pi:pi + 1]
        rdn = rden[:, pi:pi + 1]
        p.add("vector", lambda e, sp=sp: e.tensor_add(sp[:, 0:128], sp[:, 0:128], umask[:]),
              reads=[spk, "umask"], writes=[spk])
        p.add("vector", lambda e, sp=sp: e.tensor_add(sp[:, 512:640], sp[:, 512:640], lmask[:]),
              reads=[spk, "lmask"], writes=[spk])
        p.add("vector", lambda e, sp=sp, rmx=rmx: e.reduce_max(rmx, sp[:], axis=AX.X),
              reads=[spk], writes=[("rowmax", pi)])
        p.add("vector", lambda e, rmx=rmx, nmx=nmx: e.tensor_scalar_mul(nmx, rmx, -0.0625),
              reads=[("rowmax", pi)], writes=[("negmax", pi)])
        p.add("scalar", lambda e, sp=sp, pf=pf, nmx=nmx: e.activation(
            pf, sp[:], AF.Exp, scale=0.0625, bias=nmx),
            reads=[spk, ("negmax", pi)], writes=[("probs_f", pi)])

    def att_sm2(t, hh, pi, pi4):
        pf = probs_f[:, pi * 640:(pi + 1) * 640]
        pn = probsN[:, pi4 * 640:(pi4 + 1) * 640]
        dnm_ = denom[:, pi:pi + 1]
        rdn = rden[:, pi:pi + 1]
        p.add("vector", lambda e, pf=pf, dnm_=dnm_: e.reduce_sum(dnm_, pf, axis=AX.X),
              reads=[("probs_f", pi)], writes=[("denom", pi)])
        p.add("vector", lambda e, dnm_=dnm_, rdn=rdn: e.reciprocal(rdn, dnm_),
              reads=[("denom", pi)], writes=[("rden", pi)])
        p.add("vector", lambda e, pf=pf, pn=pn, rdn=rdn: e.tensor_scalar(
            pn, pf, rdn, None, ALU.mult),
            reads=[("probs_f", pi), ("rden", pi)], writes=[("probsN", pi4)])

    def att_trav(t, hh, pi, pi4):
        pn = probsN[:, pi4 * 640:(pi4 + 1) * 640]
        ptt = probsTT[:, pi * 640:(pi + 1) * 640]
        avt = ps_av if pi == 0 else ps_x
        avk = "ps_av" if pi == 0 else "ps_x"
        bk_, bkk = (ptrb, "ps_trb") if pi == 0 else (ptrb2, "ps_trb2")
        for kt in range(5):
            p.add("tensor", lambda e, kt=kt, pn=pn, bk_=bk_: e.transpose(
                bk_[:, kt * 128:(kt + 1) * 128], pn[:, kt * 128:(kt + 1) * 128], id_b[:]),
                reads=[("probsN", pi4), "id_b"], writes=[bkk])
        p.add("scalar", lambda e, ptt=ptt, bk_=bk_: e.copy(ptt[:, 0:640], bk_[:, 0:640]),
              reads=[bkk], writes=[("probsTT", pi)])
        for u in range(2):
            for kt in range(5):
                p.add("tensor", lambda e, u=u, kt=kt, t=t, ptt=ptt, avt=avt: e.matmul(
                    avt[:, u * 128:(u + 1) * 128],
                    vsb[:, (t + kt) * HD + u * 128: (t + kt) * HD + (u + 1) * 128],
                    ptt[:, kt * 128:(kt + 1) * 128],
                    start=(u == 0 and kt == 0), stop=(u == 1 and kt == 4)),
                    reads=["vsb", ("probsTT", pi)], writes=[avk])
        for u in range(2):
            p.add("scalar", lambda e, t=t, hh=hh, u=u, avt=avt: e.copy(
                aoT[:, (hh * 2 + u) * T_OWN + t * 128:
                    (hh * 2 + u) * T_OWN + (t + 1) * 128],
                avt[:, u * 128:(u + 1) * 128]),
                reads=[avk], writes=["aoT"])

    aunits = [(t, hh) for t in range(NT) for hh in range(NH)]
    pend_a = []
    for i, (t, hh) in enumerate(aunits):
        pi = i % 2
        pi4 = i % 4
        att_sc(t, hh, pi)
        if i == 0:
            wo_dma(0)
            wo_dma(1)
        elif i <= NCH:
            wo_tr(i - 1)
            if i + 1 <= NCH - 1:
                wo_dma(i + 1)
        att_sm(t, hh, pi, pi4)
        if pend_a:
            att_sm2(*pend_a[-1])
        if len(pend_a) >= 2:
            att_trav(*pend_a.pop(0))
        pend_a.append((t, hh, pi, pi4))
    att_sm2(*pend_a[-1])
    for pa in pend_a:
        att_trav(*pa)

    def ph3_wo(t, pr):
        for b_ in range(2):
            for cc in range(NCH):
                p.add("tensor", lambda e, b_=b_, cc=cc, t=t: e.matmul(
                    qps[b_][:, 0:512], aoT[:, cc * T_OWN + t * 128: cc * T_OWN + (t + 1) * 128],
                    woT[:, cc * C + b_ * 512: cc * C + b_ * 512 + 512],
                    start=(cc == 0), stop=(cc == NCH - 1)),
                    reads=["aoT", "woT"], writes=[("ps_s", b_)])

    def ph3_ln(t, pr):
        o = pr * C
        xrv = xres[:, o:o + C]
        lnv = lnr[:, o:o + C]
        lsq = lnsq[:, 0:C]
        xpv = xpb[:, o:o + C]
        ls_ = lsum[:, pr:pr + 1]
        lm_ = lmean[:, pr:pr + 1]
        lv_ = lvar[:, pr:pr + 1]
        l2_ = lm2[:, pr:pr + 1]
        lr_ = lrstd[:, pr:pr + 1]
        p.dma("sync", xrv, xin[HALO + t * 128: HALO + (t + 1) * 128, :], f"xr{pr}",
              writes=[("xres", pr)])
        ls2 = lm2[:, pr:pr + 1]
        for b_ in range(2):
            acc = ls_ if b_ == 0 else ls2
            p.add("vector", lambda e, b_=b_, xrv=xrv, lnv=lnv, acc=acc: e.scalar_tensor_tensor(
                lnv[:, b_ * 512:(b_ + 1) * 512], xrv[:, b_ * 512:(b_ + 1) * 512], 1.0,
                qps[b_][:, 0:512], ALU.mult, ALU.add, accum_out=acc),
                reads=[("xres", pr), ("ps_s", b_)], writes=[("lnr", pr), ("lsum", pr)])
        p.add("vector", lambda e, ls_=ls_, ls2=ls2: e.tensor_add(ls_, ls_, ls2),
              reads=[("lsum", pr)], writes=[("lsum", pr)])
        p.add("vector", lambda e, ls_=ls_, lm_=lm_: e.tensor_scalar_mul(lm_, ls_, 1.0 / C),
              reads=[("lsum", pr)], writes=[("lmean", pr)])
        p.add("scalar", lambda e, lnv=lnv, lsq=lsq, lv_=lv_: e.activation(
            lsq, lnv, AF.Square, accum_out=lv_),
            reads=[("lnr", pr)], writes=[("lnsq", pr), ("lvar", pr)])
        p.add("vector", lambda e, lm_=lm_, l2_=l2_: e.tensor_mul(l2_, lm_, lm_),
              reads=[("lmean", pr)], writes=[("lm2", pr)])
        p.add("vector", lambda e, lv_=lv_: e.tensor_scalar(lv_, lv_, 1.0 / C, None, ALU.mult),
              reads=[("lvar", pr)], writes=[("lvar", pr)])
        p.add("vector", lambda e, lv_=lv_, l2_=l2_: e.tensor_sub(lv_, lv_, l2_),
              reads=[("lvar", pr), ("lm2", pr)], writes=[("lvar", pr)])
        p.add("scalar", lambda e, lv_=lv_, lr_=lr_: e.activation(lr_, lv_, AF.Sqrt, bias=eps5[:]),
              reads=[("lvar", pr), "eps5"], writes=[("lrstd", pr)])
        p.add("vector", lambda e, lr_=lr_: e.reciprocal(lr_, lr_),
              reads=[("lrstd", pr)], writes=[("lrstd", pr)])
        p.add("vector", lambda e, lnv=lnv, xpv=xpv, lm_=lm_, lr_=lr_: e.tensor_scalar(
            xpv, lnv, lm_, lr_, ALU.subtract, ALU.mult),
            reads=[("lnr", pr), ("lmean", pr), ("lrstd", pr)], writes=[("xpb", pr)])
        p.dma("sync", xprime[t * 128:(t + 1) * 128, :], xpv, "xpst",
              reads=[("xpb", pr)], writes=["xprime_dram"])
        def xs_store(e, t=t, xpv=xpv):
            return e.dma_start(out=xs[ds(hof[1] + t * 128, 128), :], in_=xpv)
        p.add("gpsimd", xs_store, reads=[("xpb", pr)], writes=["xs_dram"], dma=f"xsst{pr}")

    def ph3_rt(t, pr):
        o = pr * C
        xpv = xpb[:, o:o + C]
        xTs = xpTs[:, 0:C]
        bk_, bkk = (ptrb, "ps_trb") if pr == 0 else (ptrb2, "ps_trb2")
        for cc in range(NCH):
            p.add("tensor", lambda e, cc=cc, xpv=xpv, bk_=bk_: e.transpose(
                bk_[:, cc * 128:(cc + 1) * 128], xpv[:, cc * 128:(cc + 1) * 128], id_b[:]),
                reads=[("xpb", pr), "id_b"], writes=[bkk])
        p.add("scalar", lambda e, xTs=xTs, bk_=bk_: e.copy(xTs[:, 0:C], bk_[:]),
              reads=[bkk], writes=[("xpTs", pr)])
        for cc in range(NCH):
            p.add("tensor", lambda e, cc=cc, xTs=xTs: e.matmul(
                lgps, xTs[:, cc * 128:(cc + 1) * 128], wg_sb[:, cc * E:(cc + 1) * E],
                start=(cc == 0), stop=(cc == NCH - 1)),
                reads=[("xpTs", pr), "wg_sb"], writes=["ps_av"])
        lgv = lgall[:, t * E:(t + 1) * E]
        p.add("vector", lambda e, lgv=lgv: e.tensor_copy(lgv, lgps), reads=["ps_av"],
              writes=[("lgall", t)])
        p.add("vector", lambda e, lgv=lgv: e.reduce_max(m8[:], lgv, axis=AX.X),
              reads=[("lgall", t)], writes=["m8"])
        p.add("vector", lambda e, lgv=lgv: e.tensor_scalar(pr_t[:], lgv, m8[:], None,
                                                           ALU.is_ge),
              reads=[("lgall", t), "m8"], writes=["pr_t"])
        p.add("vector", lambda e: e.tensor_tensor(pr_t[:], pr_t[:], iotadesc[:, 0:E], ALU.mult),
              reads=["pr_t", "iotadesc"], writes=["pr_t"])
        p.add("vector", lambda e: e.reduce_max(pmax[:], pr_t[:], axis=AX.X),
              reads=["pr_t"], writes=["pmax"])
        p.add("vector", lambda e, t=t: e.tensor_scalar(
            mask_tm[:, t * E:(t + 1) * E], pr_t[:], pmax[:], None, ALU.is_ge),
            reads=["pr_t", "pmax"], writes=["mask_tm"])
        p.add("tensor", lambda e, t=t, bk_=bk_: e.transpose(
            bk_[0:8, 0:128], mask_tm[:, t * E:(t + 1) * E], id_b[:]),
            reads=["mask_tm", "id_b"], writes=[bkk])
        p.add("vector", lambda e, bk_=bk_: e.tensor_copy(mTst[:], bk_[0:8, 0:128]),
              reads=[bkk], writes=["mTst"])
        def ms_store(e, t=t):
            return e.dma_start(out=msh[:, ds(hof[1] + t * 128, 128)], in_=mTst[:])
        p.add("gpsimd", ms_store, reads=["mTst"], writes=["ms_dram"], dma="msst")

    prev3 = None
    for t in range(NT):
        pr = t % 2
        ph3_wo(t, pr)
        if prev3 is not None:
            ph3_rt(*prev3)
        ph3_ln(t, pr)
        prev3 = (t, pr)
    ph3_rt(*prev3)

    p.add("sync", lambda e: e.nop(),
          reads=["xs_dram", "ms_dram", "xprime_dram"])

    p.barrier()

    attn_outer.close()
    rtB = ExitStack()
    sb_ = rtB.enter_context
    iota512 = sb_(nc.sbuf_tensor("iota512", [128, 512], F32))
    cumtm = sb_(nc.sbuf_tensor("cumtm", [128, 32 * E], F32))
    mycum = sb_(nc.sbuf_tensor("mycum", [128, 32], F32))
    cmpb = sb_(nc.sbuf_tensor("cmpb", [128, 2 * 512], BF16))
    cntf = sb_(nc.sbuf_tensor("cntf", [1, 512], F32))
    nslot = sb_(nc.sbuf_tensor("nslot", [128, 1], F32))
    tmp256 = sb_(nc.sbuf_tensor("tmp256", [128, 256], F32))
    rt_scope = ExitStack()
    sr_ = rt_scope.enter_context
    ms_sb = sr_(nc.sbuf_tensor("ms_sb", [8, T], BF16))
    ones8 = sr_(nc.sbuf_tensor("ones8", [8, T], BF16))
    cums8 = sr_(nc.sbuf_tensor("cums8", [8, T], F32))
    lgm = sr_(nc.sbuf_tensor("lgm", [128, NT], F32))
    lgex = sr_(nc.sbuf_tensor("lgex", [128, NT * E], F32))
    p_t = sr_(nc.sbuf_tensor("p_t", [128, NT], F32))
    keep_t = sr_(nc.sbuf_tensor("keep_t", [128, NT], F32))
    en_t = sr_(nc.sbuf_tensor("en_t", [128, NT], F32))
    gidxf = sr_(nc.sbuf_tensor("gidxf", [128, NT], F32))
    pB = sr_(nc.sbuf_tensor("pB", [128, NT], F32))
    tmp128 = sr_(nc.sbuf_tensor("tmp128", [128, 128], F32))


    p.dma("sync", ms_sb[:], msh, "msld", writes=["ms_sb"])
    p.add("vector", lambda e: e.reduce_max(
        lgm[:], lgall[:].rearrange("p (t e) -> p t e", e=E), axis=AX.X),
        reads=[("lgall", t) for t in range(NT)] if False else ["lgall_all"], writes=["lgm"])
    p.add("vector", lambda e: e.tensor_tensor(
        lgex[:].rearrange("p (t e) -> p t e", e=E),
        lgall[:].rearrange("p (t e) -> p t e", e=E),
        lgm[:].unsqueeze(2).broadcast_to([128, NT, E]), ALU.subtract),
        reads=["lgall_all", "lgm"], writes=["lgex"])
    p.add("scalar", lambda e: e.activation(lgex[:], lgex[:], AF.Exp),
          reads=["lgex"], writes=["lgex"])
    p.add("vector", lambda e: e.reduce_sum(
        gate_t[:], lgex[:].rearrange("p (t e) -> p t e", e=E), axis=AX.X),
        reads=["lgex"], writes=["gate_t"])
    p.add("vector", lambda e: e.reciprocal(gate_t[:], gate_t[:]),
          reads=["gate_t"], writes=["gate_t"])
    p.dma("sync", iota512[:], i512_in, "l_i512", writes=["iota512"])
    p.add("vector", lambda e: e.memset(ones8[:], 1.0), writes=["ones8"])
    p.add("vector", lambda e: e.tensor_tensor_scan(cums8[:], ones8[:], ms_sb[:], 0.0,
                                                   ALU.mult, ALU.add),
          reads=["ones8", "ms_sb"], writes=["cums8"])
    for ch in range(32):
        cb, cbk = (ps_av, "ps_av") if ch % 2 == 0 else (ps_x, "ps_x")
        p.add("tensor", lambda e, ch=ch, cb=cb: e.transpose(
            cb[:, 0:8], cums8[:, ch * 128:(ch + 1) * 128], id_f[0:8, 0:8]),
            reads=["cums8", "id_f"], writes=[cbk])
        p.add("vector", lambda e, ch=ch, cb=cb: e.tensor_copy(
            cumtm[:, ch * E:(ch + 1) * E], cb[:, 0:8]),
            reads=[cbk], writes=["cumtm"])

    m3 = lambda ap: ap.rearrange("p (t e) -> p t e", e=E)
    p.add("vector", lambda e: e.tensor_tensor(tmp128[:], cumtm[:, 0:128], mask_tm[:], ALU.mult),
          reads=["cumtm", "mask_tm"], writes=["tmp128"])
    p.add("vector", lambda e: e.reduce_sum(p_t[:], m3(tmp128[:]), axis=AX.X),
          reads=["tmp128"], writes=["p_t"])
    p.add("vector", lambda e: e.tensor_tensor(tmp128[:], cumtm[:, 128:256], mask_tm[:], ALU.mult),
          reads=["cumtm", "mask_tm"], writes=["tmp128"])
    p.add("vector", lambda e: e.reduce_sum(pB[:], m3(tmp128[:]), axis=AX.X),
          reads=["tmp128"], writes=["pB"])
    p.add("vector", lambda e: e.tensor_scalar(p_t[:], p_t[:], hsel[:, 0:1], None, ALU.mult),
          reads=["p_t", "hsel"], writes=["p_t"])
    p.add("vector", lambda e: e.scalar_tensor_tensor(p_t[:], pB[:], hsel[:, 1:2], p_t[:],
                                                     ALU.mult, ALU.add),
          reads=["pB", "hsel", "p_t"], writes=["p_t"])
    p.add("vector", lambda e: e.tensor_tensor(tmp128[:], iotadesc[:], mask_tm[:], ALU.mult),
          reads=["iotadesc", "mask_tm"], writes=["tmp128"])
    p.add("vector", lambda e: e.reduce_sum(en_t[:], m3(tmp128[:]), axis=AX.X),
          reads=["tmp128"], writes=["en_t"])
    p.add("vector", lambda e: e.tensor_scalar_mul(en_t[:], en_t[:], -1.0),
          reads=["en_t"], writes=["en_t"])
    p.add("vector", lambda e: e.tensor_scalar_add(p_t[:], p_t[:], -1.0),
          reads=["p_t"], writes=["p_t"])
    p.add("vector", lambda e: e.tensor_scalar(keep_t[:], p_t[:], -1.0, 511.5,
                                              ALU.mult, ALU.add),
          reads=["p_t"], writes=["keep_t"])
    p.add("vector", lambda e: e.tensor_scalar(keep_t[:], keep_t[:], 0.0, None, ALU.is_ge),
          reads=["keep_t"], writes=["keep_t"])
    p.add("vector", lambda e: e.tensor_tensor(scale_t[:], gate_t[:], keep_t[:], ALU.mult),
          reads=["gate_t", "keep_t"], writes=["scale_t"])
    p.add("vector", lambda e: e.tensor_scalar(gidxf[:], en_t[:], 512.0, 4096.0,
                                              ALU.mult, ALU.add),
          reads=["en_t"], writes=["gidxf"])
    p.add("vector", lambda e: e.tensor_add(gidxf[:], gidxf[:], p_t[:]),
          reads=["gidxf", "p_t"], writes=["gidxf"])
    p.add("vector", lambda e: e.tensor_copy(gidx[:], gidxf[:]),
          reads=["gidxf"], writes=["gidx"])

    rt_scope.close()
    ffn_scope = ExitStack()
    sf_ = ffn_scope.enter_context
    w1sb = sf_(nc.sbuf_tensor("w1sb", [128, 2 * NCH * 2048], BF16))
    w2sb = sf_(nc.sbuf_tensor("w2sb", [128, 32 * C], BF16))
    hT = sf_(nc.sbuf_tensor("hT", [128, 32 * 512], BF16))
    slin = sf_(nc.sbuf_tensor("slin", [128, 4 * C], BF16))
    slinT = sf_(nc.sbuf_tensor("slinT", [128, 2 * NCH * 512], BF16))
    orow = sf_(nc.sbuf_tensor("orow", [128, C], BF16))
    for eng in ("vector", "scalar", "gpsimd", "sync", "tensor"):
        p.fence(eng, ["ms_sb", "ones8", "cums8", "lgm", "lgex",
                      ("wstg", 0), ("wstg", 1)])

    def dispatch_ring(j, q):
        sv = slin[:, q * C:(q + 1) * C]
        p.add("gpsimd", lambda e, j=j, q=q, sv=sv: e.indirect_dma_start(
            out=sv, out_offset=None, in_=xs,
            in_offset=IndirectOffsetOnAxis(didx[:, j * 4 + q: j * 4 + q + 1], 0),
            bounds_check=T - 1, oob_is_err=False),
            reads=["didx", "xs_dram"], writes=[("slin", q)],
            dma=f"disp{q}")

    def dispatch_tr(j, q):
        sv = slin[:, q * C:(q + 1) * C]
        db, dbk = (ptrb, "ps_trb") if q % 2 == 0 else (ptrb2, "ps_trb2")
        for cc in range(NCH):
            p.add("tensor", lambda e, cc=cc, db=db, sv=sv: e.transpose(
                db[:, cc * 128:(cc + 1) * 128],
                sv[:, cc * 128: (cc + 1) * 128], id_b[:]),
                reads=[("slin", q), "id_b"], writes=[dbk])
        dst = slinT[:, (j % 2) * NCH * 512:].rearrange(
            "p (c i) -> p c i", i=512)[:, 0:NCH, q * 128:(q + 1) * 128]
        p.add("scalar", lambda e, dst=dst, db=db: e.copy(
            dst, db[:].rearrange("p (c i) -> p c i", i=128)),
            reads=[dbk], writes=[("slinT", j % 2)])

    def dispatch_counts(j):
        selj = sel4[:, j * E:(j + 1) * E].unsqueeze(1).broadcast_to([128, 32, E])
        p.add("vector", lambda e, selj=selj: e.tensor_tensor(
            tmp256[:].rearrange("p (c e) -> p c e", e=E),
            cumtm[:].rearrange("p (c e) -> p c e", e=E), selj, ALU.mult),
            reads=["cumtm", "sel4"], writes=["tmp256"])
        p.add("vector", lambda e: e.reduce_sum(
            mycum[:], tmp256[:].rearrange("p (c e) -> p c e", e=E), axis=AX.X),
            reads=["tmp256"], writes=["mycum"])
        bk = j % 2
        for ch in range(32):
            csl = ch % 2
            p.add("vector", lambda e, ch=ch, csl=csl: e.tensor_scalar(
                cmpb[:, csl * 512:(csl + 1) * 512], iota512[:], mycum[:, ch:ch + 1],
                None, ALU.is_ge),
                reads=["iota512", "mycum"], writes=[("cmpb", csl)])
            p.add("tensor", lambda e, ch=ch, bk=bk, csl=csl: e.matmul(
                qps[bk][0:1, 0:512], ones128b[:], cmpb[:, csl * 512:(csl + 1) * 512],
                start=(ch == 0), stop=(ch == 31)),
                reads=["ones128b", ("cmpb", csl)], writes=[("ps_s", bk)])
        p.add("vector", lambda e, bk=bk: e.tensor_copy(cntf[:], qps[bk][0:1, 0:512]),
              reads=[("ps_s", bk)], writes=["cntf"])
        for q in range(4):
            p.add("tensor", lambda e, q=q, bk=bk: e.transpose(
                qps[bk][:, 520:521], cntf[0:1, q * 128:(q + 1) * 128], id_f[0:1, 0:1]),
                reads=["cntf", "id_f"], writes=[("ps_s", bk)])
            p.add("vector", lambda e, bk=bk: e.tensor_copy(nslot[:], qps[bk][:, 520:521]),
                  reads=[("ps_s", bk)], writes=["nslot"])
            p.add("vector", lambda e, j=j, q=q: e.tensor_copy(
                didx[:, j * 4 + q: j * 4 + q + 1], nslot[:]),
                reads=["nslot"], writes=["didx"])

    def dispatch(j):
        dispatch_counts(j)
        for q in range(4):
            dispatch_ring(j, q)
            dispatch_tr(j, q)

    if debug:
        p.dma("scalar", dbg_ms, ms_sb[:], "dbg", reads=["ms_sb"])
        p.add("vector", lambda e: e.tensor_copy(nslot[:], gidx[:, 0:1]),
              reads=["gidx"], writes=["nslot"])
        def dbg_rt_store(e):
            return e.dma_start(out=dbg_rt, in_=cumtm[:, 0:64])
        p.add("scalar", dbg_rt_store, reads=["cumtm"], dma="dbg")
        p.add("vector", lambda e: e.tensor_copy(gidxf[:], gidxf[:]), reads=["gidxf"])
        def dbg_di_store(e):
            return e.dma_start(out=dbg_di.bitcast(U32) if hasattr(dbg_di, "bitcast") else dbg_di,
                               in_=didx[:])
        p.add("scalar", dbg_di_store, reads=["didx"], dma="dbg")

    hps = [qps[0][:, 0:512], qps[1][:, 0:512]]

    def w1_load(j, hh):
        sl = (2 * j + hh) % 2
        for qq in range(4):
            p.add("gpsimd", lambda e, j=j, hh=hh, sl=sl, qq=qq: e.dma_start(
                out=w1sb[:, sl * 16384:(sl + 1) * 16384].rearrange(
                    "p (c h) -> p c h", h=2048)[:, :, qq * 512:(qq + 1) * 512],
                in_=w1[j, :, hh * 2048 + qq * 512: hh * 2048 + (qq + 1) * 512].rearrange(
                    "(c p) h -> p c h", p=128)),
                writes=[("w1sb", sl, qq)], dma=f"w1l{sl}q{qq}")

    def w2_load(j, qt):
        sl = (4 * j + qt) % 4
        p.add("gpsimd", lambda e, j=j, qt=qt, sl=sl: e.dma_start(
            out=w2sb[:, sl * 8192:(sl + 1) * 8192].rearrange("p (m c) -> p m c", c=C),
            in_=w2[j, qt * 1024:(qt + 1) * 1024, :].rearrange("(m p) c -> p m c", p=128)),
            writes=[("w2sb", sl)], dma=f"w2l{sl}")

    p.add("vector", lambda e: e.memset(slin[:], 0.0),
          writes=[("slin", q_) for q_ in range(4)])
    dispatch(0)
    for hh in range(2):
        w1_load(0, hh)
    for qt in range(4):
        w2_load(0, qt)

    for j in range(E_LOC):
        for hhalf in range(2):
            sl = (2 * j + hhalf) % 2
            for m in range(16):
                mg = hhalf * 16 + m
                bk = m % 2
                for cc in range(NCH):
                    p.add("tensor", lambda e, m=m, cc=cc, bk=bk, sl=sl, j=j: e.matmul(
                        hps[bk],
                        w1sb[:, sl * 16384 + cc * 2048 + m * 128:
                             sl * 16384 + cc * 2048 + (m + 1) * 128],
                        slinT[:, ((j % 2) * NCH + cc) * 512:((j % 2) * NCH + cc + 1) * 512],
                        start=(cc == 0), stop=(cc == NCH - 1)),
                        reads=[("w1sb", sl, m // 4), ("slinT", j % 2)],
                        writes=[("ps_s", bk)])
                p.add("scalar", lambda e, mg=mg, bk=bk: e.activation(
                    hT[:, mg * 512:(mg + 1) * 512], hps[bk], AF.Gelu_apprx_tanh),
                    reads=[("ps_s", bk)], writes=["hT"])
        if j + 1 < E_LOC:
            dispatch_counts(j + 1)
            for q in range(4):
                dispatch_ring(j + 1, q)
            w1_load(j + 1, 0)
            w1_load(j + 1, 1)
        op2 = [ps_x[:], ps_av[:]]
        op2k = ["ps_x", "ps_av"]
        for q in range(4):
            for half in range(2):
                for m in range(32):
                    sl = (4 * j + m // 8) % 4
                    p.add("tensor", lambda e, q=q, half=half, m=m, sl=sl: e.matmul(
                        op2[half], hT[:, m * 512 + q * 128: m * 512 + (q + 1) * 128],
                        w2sb[:, sl * 8192 + (m % 8) * C + half * 512:
                             sl * 8192 + (m % 8) * C + half * 512 + 512],
                        start=(m == 0), stop=(m == 31)),
                        reads=["hT", ("w2sb", sl)], writes=[op2k[half]])
            for half in range(2):
                p.add("vector", lambda e, half=half: e.tensor_copy(
                    orow[:, half * 512:(half + 1) * 512], op2[half]),
                    reads=[op2k[half]], writes=["orow"])
            def so_store(e, j=j, q=q):
                return e.dma_start(out=so[ds(hof[0] + j * 512 + q * 128, 128), :],
                                   in_=orow[:])
            p.add("scalar", so_store, reads=["orow"], writes=["so_dram"], dma="sost")
            if j + 1 < E_LOC:
                dispatch_tr(j + 1, q)

        if j + 1 < E_LOC:
            for qt in range(4):
                w2_load(j + 1, qt)

    if debug:
        for r in range(16):
            p.dma("scalar", dbg_si[r * 128:(r + 1) * 128, :],
                  hT[:, 0:C] if False else slinT[:, 0:512], "dbg", reads=["slinT"])
            break
    p.add("sync", lambda e: e.nop(), reads=["so_dram"])

    p.barrier()

    ffn_scope.close()
    lnsq2 = sbp("lnsq2", [128, C], F32)
    compb = sbp("compb", [128, 16 * C], BF16)
    xres2 = sbp("xres2", [128, 16 * C], BF16)
    yrow = sbp("yrow", [128, 2 * C], F32)
    for eng in ("vector", "scalar", "gpsimd", "sync"):
        p.fence(eng, ["w1sb", "w2sb", "hT"] + [("slin", q_) for q_ in range(4)])
    p.add("vector", lambda e: e.memset(compb[:], 0.0),
          writes=[("compb", t_) for t_ in range(NT)])
    for t in range(NT):
        cpv = compb[:, t * C:(t + 1) * C]
        xrv = xres2[:, t * C:(t + 1) * C]
        p.add("gpsimd", lambda e, t=t, cpv=cpv: e.indirect_dma_start(
            out=cpv, out_offset=None, in_=so,
            in_offset=IndirectOffsetOnAxis(gidx[:, t:t + 1], 0),
            bounds_check=T - 1, oob_is_err=False),
            reads=["gidx", "so_dram"], writes=[("compb", t)],
            dma=f"comb{t}")
        p.dma("sync", xrv, xprime[t * 128:(t + 1) * 128, :], f"xpl{t}",
              reads=["xprime_dram"], writes=[("xres2", t)])
    for t in range(NT):
        pr = t % 2
        cpv = compb[:, t * C:(t + 1) * C]
        xrv = xres2[:, t * C:(t + 1) * C]
        yv = yrow[:, pr * C:(pr + 1) * C]
        ls_ = lsum[:, pr:pr + 1]
        lm_ = lmean[:, pr:pr + 1]
        lv_ = lvar[:, pr:pr + 1]
        l2_ = lm2[:, pr:pr + 1]
        lr_ = lrstd[:, pr:pr + 1]
        p.add("vector", lambda e, t=t, cpv=cpv, xrv=xrv, yv=yv, ls_=ls_: e.scalar_tensor_tensor(
            yv, cpv, scale_t[:, t:t + 1], xrv, ALU.mult, ALU.add,
            accum_out=ls_),
            reads=[("compb", t), "scale_t", ("xres2", t)],
            writes=[("yrow", pr), ("lsum", pr)])
        p.add("vector", lambda e, ls_=ls_, lm_=lm_: e.tensor_scalar_mul(lm_, ls_, 1.0 / C),
              reads=[("lsum", pr)], writes=[("lmean", pr)])
        p.add("scalar", lambda e, yv=yv, lv_=lv_: e.activation(lnsq2[:], yv, AF.Square,
                                                               accum_out=lv_),
              reads=[("yrow", pr)], writes=["lnsq2", ("lvar", pr)])
        p.add("vector", lambda e, lm_=lm_, l2_=l2_: e.tensor_mul(l2_, lm_, lm_),
              reads=[("lmean", pr)], writes=[("lm2", pr)])
        p.add("vector", lambda e, lv_=lv_: e.tensor_scalar(lv_, lv_, 1.0 / C,
                                                           None, ALU.mult),
              reads=[("lvar", pr)], writes=[("lvar", pr)])
        p.add("vector", lambda e, lv_=lv_, l2_=l2_: e.tensor_sub(lv_, lv_, l2_),
              reads=[("lvar", pr), ("lm2", pr)], writes=[("lvar", pr)])
        p.add("scalar", lambda e, lv_=lv_, lr_=lr_: e.activation(lr_, lv_, AF.Sqrt,
                                                                 bias=eps5[:]),
              reads=[("lvar", pr), "eps5"], writes=[("lrstd", pr)])
        p.add("vector", lambda e, lr_=lr_: e.reciprocal(lr_, lr_),
              reads=[("lrstd", pr)], writes=[("lrstd", pr)])
        p.add("vector", lambda e, yv=yv, lm_=lm_, lr_=lr_: e.tensor_scalar(
            yv, yv, lm_, lr_, ALU.subtract, ALU.mult),
            reads=[("yrow", pr), ("lmean", pr), ("lrstd", pr)], writes=[("yrow", pr)])
        p.dma("sync", yout[t * 128:(t + 1) * 128, :], yv, f"yst{pr}", reads=[("yrow", pr)])

    p.emit()
    p.final_waits("sync", ["yst0", "yst1"] + (["dbg"] if debug else []))
    psA.close()
    stack.close()
    nc.finalize()
    return nc


def _perm(n):
    out = []
    for h0 in range(0, n, HD):
        out.extend(range(h0, h0 + HD, 2))
        out.extend(range(h0 + 1, h0 + HD, 2))
    return np.array(out)


def make_in_maps(inputs):
    x = np.asarray(inputs["L0_attn_x"])
    Wq = np.asarray(inputs["L0_attn_Wq"])
    Wk = np.asarray(inputs["L0_attn_Wk"])
    Wv = np.asarray(inputs["L0_attn_Wv"])
    Wo = np.asarray(inputs["L0_attn_Wo"])
    qg = np.asarray(inputs["L0_attn_q_gamma"])
    kg = np.asarray(inputs["L0_attn_k_gamma"])
    Wg = np.asarray(inputs["L0_ffn_Wg"])
    W1 = np.asarray(inputs["L0_ffn_W1"])
    W2 = np.asarray(inputs["L0_ffn_W2"])

    pq = _perm(C)
    pk = _perm(HD)
    Wq_p = np.ascontiguousarray(Wq[pq, :])
    Wk_p = np.ascontiguousarray(Wk[pk, :])
    gq_t = np.tile(qg[pk][None, :], (128, NH)).astype(np.float32)
    gk_t = np.tile(kg[pk][None, :], (128, 1)).astype(np.float32)
    inv = (1.0 / (1.0e6 ** (np.arange(0, HD, 2, dtype=np.float32) / HD))).astype(np.float32)
    ident = np.eye(128, dtype=np.float32)

    ins = []
    for c in range(8):
        b, h = c >> 1, c & 1
        lo = h * T_OWN - HALO
        xs_ = np.zeros((T_KV, C), dtype=np.float32)
        s0 = max(0, lo)
        xs_[s0 - lo:, :] = x[b, s0: lo + T_KV, :]
        pos = (lo + np.arange(T_KV)).astype(np.float32)
        freqs = np.outer(pos, inv).astype(np.float32)
        kb = np.zeros((1, T_KV), dtype=np.float32)
        if h == 0:
            kb[0, :HALO] = NEG
        sel = np.zeros((128, E_LOC * E), dtype=np.float32)
        for j in range(E_LOC):
            sel[:, j * E + (E_LOC * h + j)] = 1.0
        hs = np.zeros((128, 2), dtype=np.float32)
        hs[:, h] = 1.0
        ins.append({
            "x": xs_,
            "wq": Wq_p, "wk": Wk_p, "wv": np.ascontiguousarray(Wv),
            "wo": np.ascontiguousarray(Wo), "wg": np.ascontiguousarray(Wg),
            "w1": np.ascontiguousarray(W1[E_LOC * h: E_LOC * h + E_LOC]),
            "w2": np.ascontiguousarray(W2[E_LOC * h: E_LOC * h + E_LOC]),
            "gq": gq_t, "gk": gk_t,
            "cosb": np.cos(freqs).astype(np.float32),
            "sinb": np.sin(freqs).astype(np.float32),
            "kbias": kb, "sel4": sel, "hsel": hs, "idf": ident,
            "i512": np.tile(np.arange(512, dtype=np.float32)[None, :], (128, 1)),
        })
    return ins


def kernel(**inputs):
    ins = make_in_maps(inputs)
    nc = build()
    res = run_bass_kernel_spmd(nc, ins, core_ids=list(range(8)))
    y = np.stack([res.results[c]["y"] for c in range(8)]).reshape(B, T, C)
    return y


# version 33
# speedup vs baseline: 1.0491x; 1.0043x over previous
import numpy as np
import concourse.bass as bass
import concourse.bacc as bacc
import concourse.mybir as mybir
from concourse.bass import IndirectOffsetOnAxis, ds
from concourse.bass_utils import run_bass_kernel_spmd
from contextlib import ExitStack

F32 = mybir.dt.float32
BF16 = mybir.dt.bfloat16
I32 = mybir.dt.int32
U32 = mybir.dt.uint32
AF = mybir.ActivationFunctionType
ALU = mybir.AluOpType
AX = mybir.AxisListType

B, T, C = 4, 4096, 1024
NH, HD = 4, 256
T_OWN, HALO = 2048, 512
T_KV = T_OWN + HALO
NT = T_OWN // 128
NKV = T_KV // 128
NCH = C // 128
E, E_LOC, CAP = 8, 4, 512
H_FFN = 4096
NEG = -1.0e30
ENGS = ("sync", "vector", "scalar", "tensor", "gpsimd")


class Prog:
    def __init__(self, nc):
        self.nc = nc
        self.ops = []
        self.phases = []
        self._cur = []
        self.dma_counts = {}
        self._pending_fence = {}

    def add(self, engine, fn, reads=(), writes=(), dma=None):
        if dma is not None:
            dma = f"{dma}_{engine}"
        writes = tuple(writes) + tuple(self._pending_fence.pop(engine, ()))
        op = dict(engine=engine, fn=fn, reads=tuple(reads), writes=tuple(writes),
                  dma=dma, idx=len(self.ops))
        if dma is not None:
            self.dma_counts[dma] = self.dma_counts.get(dma, 0) + 1
            op["dma_count"] = self.dma_counts[dma]
        self.ops.append(op)
        self._cur.append(op)
        return op

    def dma(self, engine, out, in_, tag, reads=(), writes=()):
        self.add(engine, lambda e, o=out, i=in_: e.dma_start(out=o, in_=i),
                 reads=reads, writes=writes, dma=tag)

    def fence(self, engine, regions):
        self._pending_fence[engine] = tuple(self._pending_fence.get(engine, ())) + tuple(regions)

    def barrier(self):
        if self._cur:
            self.phases.append(("ops", self._cur))
            self._cur = []
        self.phases.append(("barrier", None))

    def emit(self):
        nc = self.nc
        if self._cur:
            self.phases.append(("ops", self._cur))
            self._cur = []
        last_writer, readers = {}, {}
        for op in self.ops:
            deps = set()
            for r in op["reads"]:
                w = last_writer.get(r)
                if w is not None:
                    deps.add(w["idx"])
                readers.setdefault(r, []).append(op)
            for wr in op["writes"]:
                w = last_writer.get(wr)
                if w is not None:
                    deps.add(w["idx"])
                for rd in readers.get(wr, ()):
                    deps.add(rd["idx"])
                readers[wr] = []
                last_writer[wr] = op
            deps.discard(op["idx"])
            pruned = []
            for d in deps:
                pd = self.ops[d]
                if (pd["dma"] is None and op["dma"] is None
                        and pd["engine"] == "tensor" and op["engine"] == "tensor"):
                    continue
                pruned.append(d)
            best = {}
            for d in pruned:
                pd = self.ops[d]
                key = ("d", pd["dma"]) if pd["dma"] is not None else ("e", pd["engine"])
                if key not in best or d > best[key]:
                    best[key] = d
            op["deps"] = sorted(best.values())
        need_inc = set()
        for op in self.ops:
            for d in op["deps"]:
                if self.ops[d]["dma"] is None:
                    need_inc.add(d)
        cnt = {e: 0 for e in ENGS}
        for op in self.ops:
            if op["dma"] is None:
                op["do_inc"] = op["idx"] in need_inc
                if op["do_inc"]:
                    cnt[op["engine"]] += 1
                op["inc_val"] = cnt[op["engine"]]
        esem = {e: nc.alloc_semaphore(f"es_{e}") for e in ENGS}
        dsem = {t: nc.alloc_semaphore(f"ds_{t}") for t in self.dma_counts}
        self.esem, self.dsem = esem, dsem
        for op in self.ops:
            waits = {}
            for d in op["deps"]:
                pd = self.ops[d]
                if pd["dma"] is not None:
                    sem, val = dsem[pd["dma"]], 16 * pd["dma_count"]
                else:
                    sem, val = esem[pd["engine"]], pd["inc_val"]
                k = id(sem)
                if k not in waits or waits[k][1] < val:
                    waits[k] = (sem, val)
            op["waits"] = list(waits.values())
        watermark = {}
        for kind, payload in self.phases:
            if kind == "barrier":
                nc.all_core_barrier()
                continue
            by_eng = {}
            for op in payload:
                by_eng.setdefault(op["engine"], []).append(op)
            with nc.Block() as blk:
                for ename, ops_ in by_eng.items():
                    def mk(ename, ops_):
                        def body(eng):
                            for op in ops_:
                                for sem, val in op["waits"]:
                                    if watermark.get((ename, id(sem)), 0) < val:
                                        eng.wait_ge(sem, val)
                                        watermark[(ename, id(sem))] = val
                                inst = op["fn"](eng)
                                if op["dma"] is not None:
                                    inst.then_inc(self.dsem[op["dma"]], 16)
                                elif op["do_inc"]:
                                    inst.then_inc(self.esem[ename], 1)
                        return body
                    getattr(blk, ename)(mk(ename, ops_))

    def final_waits(self, engine, tags):
        full = [t2 for t2 in self.dma_counts for t in tags if t2.startswith(t)]
        def fw(eng):
            for t in full:
                eng.wait_ge(self.dsem[t], 16 * self.dma_counts[t])
        with self.nc.Block() as blk:
            getattr(blk, engine)(fw)


def build(debug=False, num_devices=8):
    nc = bacc.Bacc("TRN2", target_bir_lowering=False, debug=False, num_devices=num_devices)

    xin = nc.dram_tensor("x", [T_KV, C], F32, kind="ExternalInput").ap()
    wq = nc.dram_tensor("wq", [C, C], F32, kind="ExternalInput").ap()
    wk = nc.dram_tensor("wk", [HD, C], F32, kind="ExternalInput").ap()
    wv = nc.dram_tensor("wv", [HD, C], F32, kind="ExternalInput").ap()
    wo = nc.dram_tensor("wo", [C, C], F32, kind="ExternalInput").ap()
    wgi = nc.dram_tensor("wg", [C, E], F32, kind="ExternalInput").ap()
    w1 = nc.dram_tensor("w1", [E_LOC, C, H_FFN], F32, kind="ExternalInput").ap()
    w2 = nc.dram_tensor("w2", [E_LOC, H_FFN, C], F32, kind="ExternalInput").ap()
    gq_in = nc.dram_tensor("gq", [128, C], F32, kind="ExternalInput").ap()
    gk_in = nc.dram_tensor("gk", [128, HD], F32, kind="ExternalInput").ap()
    cos_in = nc.dram_tensor("cosb", [T_KV, 128], F32, kind="ExternalInput").ap()
    sin_in = nc.dram_tensor("sinb", [T_KV, 128], F32, kind="ExternalInput").ap()
    kb_in = nc.dram_tensor("kbias", [1, T_KV], F32, kind="ExternalInput").ap()
    sel_in = nc.dram_tensor("sel4", [128, E_LOC * E], F32, kind="ExternalInput").ap()
    hsel_in = nc.dram_tensor("hsel", [128, 2], F32, kind="ExternalInput").ap()
    idf_in = nc.dram_tensor("idf", [128, 128], F32, kind="ExternalInput").ap()
    i512_in = nc.dram_tensor("i512", [128, 512], F32, kind="ExternalInput").ap()
    yout = nc.dram_tensor("y", [T_OWN, C], F32, kind="ExternalOutput").ap()

    xprime = nc.dram_tensor("xprime", [T_OWN, C], BF16,
                            kind=("ExternalOutput" if debug else "Internal")).ap()
    xs = nc.dram_tensor("xs_sh", [T, C], BF16, kind="Internal", addr_space="Shared").ap()
    msh = nc.dram_tensor("ms_sh", [E, T], BF16, kind="Internal", addr_space="Shared").ap()
    so = nc.dram_tensor("so_sh", [T, C], BF16, kind="Internal", addr_space="Shared").ap()
    if debug:
        dbg_ms = nc.dram_tensor("dbg_ms", [E, T], F32, kind="ExternalOutput").ap()
        dbg_rt = nc.dram_tensor("dbg_rt", [128, 64], F32, kind="ExternalOutput").ap()
        dbg_di = nc.dram_tensor("dbg_di", [128, 16], F32, kind="ExternalOutput").ap()
        dbg_si = nc.dram_tensor("dbg_si", [E_LOC * CAP, C], F32, kind="ExternalOutput").ap()

    p = Prog(nc)
    hof = {}

    def snap_hof(e):
        hof[0] = e.snap((e.partition_id() % 2) * T_OWN, min_val=0, max_val=T_OWN)

    def snap_hofg(e):
        hof[1] = e.snap((e.partition_id() % 2) * T_OWN, min_val=0, max_val=T_OWN)
        return e.nop()
    stack = ExitStack()
    sbp = lambda n, s, d: nc.alloc_sbuf_tensor(n, s, d)
    sbs = lambda n, s, d: stack.enter_context(nc.sbuf_tensor(n, s, d))

    id_f = sbp("id_f", [128, 128], F32)
    id_b = sbp("id_b", [128, 128], BF16)
    ones1b = sbp("ones1b", [1, 128], BF16)
    ones128b = sbp("ones128b", [128, 1], BF16)
    iotadesc = sbp("iotadesc", [128, 128], F32)
    sel4 = sbp("sel4_sb", [128, E_LOC * E], F32)
    hsel = sbp("hsel_sb", [128, 2], F32)
    eps6 = sbp("eps6", [128, 1], F32)
    eps5 = sbp("eps5", [128, 1], F32)
    mask_tm = sbp("mask_tm", [128, NT * E], BF16)
    lgall = sbp("lgall", [128, NT * E], F32)
    gate_t = sbp("gate_t", [128, NT], F32)
    scale_t = sbp("scale_t", [128, NT], F32)
    gidx = sbp("gidx", [128, NT], U32)
    didx = sbp("didx", [128, 16], U32)
    lsum = sbp("lsum", [128, 2], F32)
    lmean = sbp("lmean", [128, 2], F32)
    lvar = sbp("lvar", [128, 2], F32)
    lm2 = sbp("lm2", [128, 2], F32)
    lrstd = sbp("lrstd", [128, 2], F32)
    tmpa = sbp("tmpa", [128, E], F32)
    tmpb2 = sbp("tmpb2", [128, E], F32)

    ptrb = nc.alloc_psum_tensor("ps_trb", [128, 1024], BF16)
    ptrb2 = nc.alloc_psum_tensor("ps_trb2", [128, 1024], BF16)

    p.add("scalar", snap_hof)
    p.add("gpsimd", snap_hofg)
    p.dma("sync", id_f[:], idf_in, "l_idf", writes=["id_f"])
    p.dma("gpsimd", id_b[:], idf_in, "l_idb", writes=["id_b"])
    p.dma("sync", sel4[:], sel_in, "l_sel", writes=["sel4"])
    p.dma("sync", hsel[:], hsel_in, "l_hsel", writes=["hsel"])
    for tile, val in ((ones1b, 1.0), (ones128b, 1.0),
                      (eps6, 1e-6), (eps5, 1e-5)):
        p.add("vector", lambda e, tl=tile, v=val: e.memset(tl[:], v),
              writes=[tile.name])
    p.add("gpsimd", lambda e: e.iota(iotadesci[:], pattern=[[0, 16], [-1, 8]], base=8,
                                     channel_multiplier=0), writes=["iotadesci"])
    p.add("vector", lambda e: e.tensor_copy(iotadesc[:], iotadesci[:]),
          reads=["iotadesci"], writes=["iotadesc"])

    attn_outer = ExitStack()
    so_ = attn_outer.enter_context
    kbias_b = so_(nc.sbuf_tensor("kbias_b", [1, T_KV], BF16))
    umask = so_(nc.sbuf_tensor("umask", [128, 128], F32))
    lmask = so_(nc.sbuf_tensor("lmask", [128, 128], F32))
    mTst = so_(nc.sbuf_tensor("mTst", [8, 128], BF16))
    iotadesci = so_(nc.sbuf_tensor("iotadesci", [128, 128], I32))
    gq_sb = so_(nc.sbuf_tensor("gq_sb", [128, C], BF16))
    gk_sb = so_(nc.sbuf_tensor("gk_sb", [128, HD], BF16))
    cosb = so_(nc.sbuf_tensor("cosb_sb", [128, T_KV], BF16))
    sinb = so_(nc.sbuf_tensor("sinb_sb", [128, T_KV], BF16))
    qT = so_(nc.sbuf_tensor("qT", [128, NH * 2 * T_OWN], BF16))
    kT = so_(nc.sbuf_tensor("kT", [128, 2 * T_KV], BF16))
    vsb = so_(nc.sbuf_tensor("vsb", [128, NKV * HD], BF16))
    wstg = so_(nc.sbuf_tensor("wstg", [128, 2 * C], BF16))
    rowmax = so_(nc.sbuf_tensor("rowmax", [128, 2], F32))
    negmax = so_(nc.sbuf_tensor("negmax", [128, 2], F32))
    denom = so_(nc.sbuf_tensor("denom", [128, 2], F32))
    rden = so_(nc.sbuf_tensor("rden", [128, 2], F32))
    xres = so_(nc.sbuf_tensor("xres", [128, 2 * C], F32))
    lnr = so_(nc.sbuf_tensor("lnr", [128, 2 * C], F32))
    xpb = so_(nc.sbuf_tensor("xpb", [128, 2 * C], BF16))
    xpTs = so_(nc.sbuf_tensor("xpTs", [128, C], BF16))
    lnsq = so_(nc.sbuf_tensor("lnsq", [128, C], F32))
    m8 = so_(nc.sbuf_tensor("m8", [128, 1], F32))
    dnm = so_(nc.sbuf_tensor("dnm", [128, 1], F32))
    pr_t = so_(nc.sbuf_tensor("pr_t", [128, E], F32))
    pmax = so_(nc.sbuf_tensor("pmax", [128, 1], F32))

    psA = ExitStack()
    qps = [psA.enter_context(nc.psum_tensor(f"ps_s{i}", [128, 640], F32)) for i in range(2)]
    ps_av = psA.enter_context(nc.psum_tensor("ps_av", [128, 512], F32))
    ps_x = psA.enter_context(nc.psum_tensor("ps_x", [128, 512], F32))
    lgps = ps_av[:, 128:136]

    p.dma("gpsimd", gq_sb[:], gq_in, "l_gq", writes=["gq_sb"])
    p.dma("gpsimd", gk_sb[:], gk_in, "l_gk", writes=["gk_sb"])
    p.dma("gpsimd", kbias_b[:], kb_in, "l_kb", writes=["kbias_b"])
    for tile, val in ((umask, 0.0), (lmask, 0.0)):
        p.add("vector", lambda e, tl=tile, v=val: e.memset(tl[:], v), writes=[tile.name])
    p.add("gpsimd", lambda e: e.affine_select(umask[:], umask[:], pattern=[[1, 128]],
                                              compare_op=ALU.is_ge, fill=NEG, base=0,
                                              channel_multiplier=-1),
          reads=["umask"], writes=["umask"])
    p.add("gpsimd", lambda e: e.affine_select(lmask[:], lmask[:], pattern=[[-1, 128]],
                                              compare_op=ALU.is_ge, fill=NEG, base=0,
                                              channel_multiplier=1),
          reads=["lmask"], writes=["lmask"])
    p.dma("gpsimd", cosb[:].rearrange("p (t i) -> p t i", i=128),
          cos_in.rearrange("(t p) i -> p t i", p=128), "l_cos", writes=["cosb"])
    p.dma("gpsimd", sinb[:].rearrange("p (t i) -> p t i", i=128),
          sin_in.rearrange("(t p) i -> p t i", p=128), "l_sin", writes=["sinb"])

    attn_inner = ExitStack()
    si_ = attn_inner.enter_context
    wqT = si_(nc.sbuf_tensor("wqT", [128, NCH * C], BF16))
    wkT = si_(nc.sbuf_tensor("wkT", [128, NCH * HD], BF16))
    wvT = si_(nc.sbuf_tensor("wvT", [128, NCH * HD], BF16))
    xT = si_(nc.sbuf_tensor("xT", [128, NCH * 1024], BF16))
    qf = si_(nc.sbuf_tensor("qf", [128, 2 * C], F32))
    qsq = si_(nc.sbuf_tensor("qsq", [128, HD], F32))
    qrt = si_(nc.sbuf_tensor("qrt", [128, C], BF16))
    qb = si_(nc.sbuf_tensor("qb", [128, 4 * C], BF16))
    ms4 = si_(nc.sbuf_tensor("ms4", [128, 2 * NH], F32))
    rstd4 = si_(nc.sbuf_tensor("rstd4", [128, 2 * NH], F32))


    def load_transposed(wsrc, wdst, dname, nrow_chunks, ncol_chunks, nout):
        for rc in range(nrow_chunks):
            sl = rc % 2
            bk_, bkk = (ptrb, "ps_trb") if rc % 2 == 0 else (ptrb2, "ps_trb2")
            p.dma("gpsimd", wstg[:, sl * C: sl * C + C], wsrc[rc * 128:(rc + 1) * 128, :],
                  f"wld{sl}", writes=[("wstg", sl)])
            for cc in range(ncol_chunks):
                p.add("tensor", lambda e, cc=cc, sl=sl, bk_=bk_: e.transpose(
                    bk_[:, cc * 128:(cc + 1) * 128],
                    wstg[:, sl * C + cc * 128: sl * C + (cc + 1) * 128], id_b[:]),
                    reads=[("wstg", sl), "id_b"], writes=[bkk])
            dst = wdst[:].rearrange("p (c i) -> p c i", i=nout)[:, :, rc * 128:(rc + 1) * 128]
            p.add("vector", lambda e, dst=dst, ncol_chunks=ncol_chunks, bk_=bk_: e.tensor_copy(
                dst, bk_[:, 0:ncol_chunks * 128].rearrange("p (c i) -> p c i", i=128)),
                reads=[bkk], writes=[dname])

    load_transposed(wq, wqT, "wqT", NCH, NCH, C)
    load_transposed(wk, wkT, "wkT", HD // 128, NCH, HD)
    load_transposed(wv, wvT, "wvT", HD // 128, NCH, HD)

    xbf2 = si_(nc.sbuf_tensor("xbf2", [128, 2 * C], BF16))

    def dma_x(t):
        sl = t % 2
        p.dma("gpsimd", xbf2[:, sl * C: sl * C + C], xin[t * 128:(t + 1) * 128, :], f"xl{sl}",
              writes=[("xbf2", sl)])

    def trx(t):
        sl = t % 2
        xsl = t % 8
        for cc in range(NCH):
            p.add("tensor", lambda e, cc=cc, sl=sl: e.transpose(
                ptrb2[:, cc * 128:(cc + 1) * 128],
                xbf2[:, sl * C + cc * 128: sl * C + (cc + 1) * 128], id_b[:]),
                reads=[("xbf2", sl), "id_b"], writes=["ps_trb2"])
        dst = xT[:].rearrange("p (c i) -> p c i", i=1024)[:, :, xsl * 128:(xsl + 1) * 128]
        p.add("scalar", lambda e, dst=dst: e.copy(dst, ptrb2[:].rearrange("p (c i) -> p c i", i=128)),
              reads=["ps_trb2"], writes=[("xT", xsl)])

    def qkv_mm(t, kv, par):
        if kv:
            wname, wT_, nout = "wkT", wkT, HD
            banks, keys = [ps_av], ["ps_av"]
        else:
            wname, wT_, nout = "wqT", wqT, C
            banks, keys = [qps[0], qps[1]], [("ps_s", 0), ("ps_s", 1)]
        nb = (nout + 511) // 512
        xsl = (t if kv else t + 4) % 8
        for b_ in range(nb):
            n0 = b_ * 512
            nw = min(512, nout - n0)
            for cc in range(NCH):
                p.add("tensor", lambda e, bk=banks[b_], cc=cc, n0=n0, nw=nw, xsl=xsl, wT_=wT_, nout=nout: e.matmul(
                    bk[:, 0:nw],
                    xT[:, cc * 1024 + xsl * 128: cc * 1024 + (xsl + 1) * 128],
                    wT_[:, cc * nout + n0: cc * nout + n0 + nw],
                    start=(cc == 0), stop=(cc == NCH - 1)),
                    reads=[("xT", xsl), wname], writes=[keys[b_]])
        for b_ in range(nb):
            n0 = b_ * 512
            nw = min(512, nout - n0)
            p.add("scalar", lambda e, bk=banks[b_], n0=n0, nw=nw, par=par: e.copy(
                qf[:, par * C + n0: par * C + n0 + nw], bk[:, 0:nw]),
                reads=[keys[b_]], writes=[("qf", par)])

    def v_mm(t):
        xsl = t % 8
        for cc in range(NCH):
            p.add("tensor", lambda e, cc=cc, xsl=xsl: e.matmul(
                ps_x[:, 0:HD], xT[:, cc * 1024 + xsl * 128: cc * 1024 + (xsl + 1) * 128],
                wvT[:, cc * HD:(cc + 1) * HD], start=(cc == 0), stop=(cc == NCH - 1)),
                reads=[("xT", xsl), "wvT"], writes=["ps_x"])
        p.add("scalar", lambda e, t=t: e.copy(vsb[:, t * HD:(t + 1) * HD], ps_x[:, 0:HD]),
              reads=["ps_x"], writes=["vsb"])

    def qkv_nr(t, kv, par, par4):
        nout = HD if kv else C
        nh_loc = 1 if kv else NH
        tok0 = t * 128 + (0 if kv else HALO)
        o = par * C
        qfv = qf[:, o:o + nout]
        qbv = qb[:, par4 * C: par4 * C + nout]
        msv = ms4[:, par * NH: par * NH + nh_loc]
        rsv = rstd4[:, par * NH: par * NH + nh_loc]
        for hh in range(nh_loc):
            p.add("scalar", lambda e, hh=hh: e.activation(
                qsq[:, 0:HD], qfv[:, hh * HD:(hh + 1) * HD], AF.Square,
                accum_out=msv[:, hh:hh + 1]),
                reads=[("qf", par)], writes=["qsq_dump", ("ms4", par)])
        p.add("scalar", lambda e: e.activation(rsv, msv, AF.Sqrt, scale=1.0 / HD, bias=eps6[:]),
              reads=[("ms4", par), "eps6"], writes=[("rstd4", par)])
        p.add("vector", lambda e: e.reciprocal(rsv, rsv),
              reads=[("rstd4", par)], writes=[("rstd4", par)])
        for hh in range(nh_loc):
            gam = gk_sb[:, 0:HD] if kv else gq_sb[:, hh * HD:(hh + 1) * HD]
            p.add("vector", lambda e, hh=hh, gam=gam: e.scalar_tensor_tensor(
                qbv[:, hh * HD:(hh + 1) * HD], qfv[:, hh * HD:(hh + 1) * HD],
                rsv[:, hh:hh + 1], gam, ALU.mult, ALU.mult),
                reads=[("qf", par), ("rstd4", par), "gk_sb" if kv else "gq_sb"],
                writes=[("qb", par4)])
        ct = cosb[:, tok0: tok0 + 128]
        st_ = sinb[:, tok0: tok0 + 128]
        qv = qbv.rearrange("p (h two i) -> p h two i", two=2, i=128)
        Av = qv[:, :, 0, :]
        Bv = qv[:, :, 1, :]
        ctb = ct.unsqueeze(1).broadcast_to([128, nh_loc, 128])
        stb = st_.unsqueeze(1).broadcast_to([128, nh_loc, 128])
        u3 = qrt[:, 0:nh_loc * 128].rearrange("p (h i) -> p h i", i=128)
        v3 = qrt[:, 512:512 + nh_loc * 128].rearrange("p (h i) -> p h i", i=128)
        p.add("vector", lambda e: e.tensor_tensor(u3, Av, stb, ALU.mult),
              reads=[("qb", par4), "sinb"], writes=["qrt"])
        p.add("vector", lambda e: e.tensor_tensor(Av, Av, ctb, ALU.mult),
              reads=[("qb", par4), "cosb"], writes=[("qb", par4)])
        p.add("vector", lambda e: e.tensor_tensor(v3, Bv, stb, ALU.mult),
              reads=[("qb", par4), "sinb"], writes=["qrt"])
        p.add("vector", lambda e: e.tensor_tensor(Bv, Bv, ctb, ALU.mult),
              reads=[("qb", par4), "cosb"], writes=[("qb", par4)])
        p.add("vector", lambda e: e.tensor_tensor(Av, Av, v3, ALU.subtract),
              reads=[("qb", par4), "qrt"], writes=[("qb", par4)])
        p.add("vector", lambda e: e.tensor_tensor(Bv, Bv, u3, ALU.add),
              reads=[("qb", par4), "qrt"], writes=[("qb", par4)])

    def qkv_tr(t, kv, par4):
        o = par4 * C
        n = 2 if kv else 8
        for u in range(n):
            p.add("tensor", lambda e, u=u: e.transpose(
                ptrb[:, u * 128:(u + 1) * 128], qb[:, o + u * 128: o + (u + 1) * 128],
                id_b[:]),
                reads=[("qb", par4), "id_b"], writes=["ps_trb"])
        if kv:
            dst = kT[:].rearrange("p (u i) -> p u i", i=T_KV)[:, :, t * 128:(t + 1) * 128]
            p.add("scalar", lambda e, dst=dst: e.copy(
                dst, ptrb[:, 0:256].rearrange("p (u i) -> p u i", i=128)),
                reads=["ps_trb"], writes=["kT"])
        else:
            dst = qT[:].rearrange("p (u i) -> p u i", i=T_OWN)[:, :, t * 128:(t + 1) * 128]
            p.add("scalar", lambda e, dst=dst: e.copy(
                dst, ptrb[:].rearrange("p (u i) -> p u i", i=128)),
                reads=["ps_trb"], writes=["qT"])

    units = []
    for t in range(5):
        units.append((t, True))
    for t in range(5, NKV):
        units.append((t - 5, False))
        units.append((t, True))
    units.append((15, False))
    dma_x(0)
    dma_x(1)
    trx(0)
    pend = []
    for i, (t, kv) in enumerate(units):
        par = i % 2
        par4 = i % 4
        qkv_mm(t, kv, par)
        if kv:
            v_mm(t)
        if len(pend) >= 2:
            qkv_tr(*pend.pop(0))
        nxt = units[i + 1] if i + 1 < len(units) else None
        if nxt is not None and nxt[1]:
            tn = nxt[0]
            if tn + 1 < NKV:
                dma_x(tn + 1)
            trx(tn)
        qkv_nr(t, kv, par, par4)
        pend.append((t, kv, par4))
    for pp in pend:
        qkv_tr(*pp)

    attn_inner.close()
    aoT = so_(nc.sbuf_tensor("aoT", [128, NCH * T_OWN], BF16))
    probs_f = so_(nc.sbuf_tensor("probs_f", [128, 2 * 640], F32))
    probsN = so_(nc.sbuf_tensor("probsN", [128, 4 * 640], BF16))
    probsTT = so_(nc.sbuf_tensor("probsTT", [128, 2 * 640], BF16))

    woT = so_(nc.sbuf_tensor("woT", [128, NCH * C], BF16))
    wg_sb = so_(nc.sbuf_tensor("wg_sb", [128, NCH * E], BF16))
    FREED = ["xT", "wqT", "wkT", "wvT", ("xbf2", 0), ("xbf2", 1), "qsq_dump", "qrt"] + [
        (n, par) for n in ("qf", "ms4", "rstd4") for par in (0, 1)] + [
        ("qb", s) for s in range(4)]
    for eng in ("scalar", "vector", "gpsimd", "tensor"):
        p.fence(eng, FREED)

    p.dma("gpsimd", wg_sb[:].rearrange("p (c e) -> p c e", e=E),
          wgi.rearrange("(c p) e -> p c e", p=128), "l_wg", writes=["wg_sb"])

    def wo_dma(rc):
        sl = rc % 2
        p.dma("gpsimd", wstg[:, sl * C: sl * C + C], wo[rc * 128:(rc + 1) * 128, :],
              f"wld{sl}", writes=[("wstg", sl)])

    def wo_tr(rc):
        sl = rc % 2
        bk_, bkk = (ptrb, "ps_trb") if rc % 2 == 0 else (ptrb2, "ps_trb2")
        for cc in range(NCH):
            p.add("tensor", lambda e, cc=cc, sl=sl, bk_=bk_: e.transpose(
                bk_[:, cc * 128:(cc + 1) * 128],
                wstg[:, sl * C + cc * 128: sl * C + (cc + 1) * 128], id_b[:]),
                reads=[("wstg", sl), "id_b"], writes=[bkk])
        dst = woT[:].rearrange("p (c i) -> p c i", i=C)[:, :, rc * 128:(rc + 1) * 128]
        p.add("vector", lambda e, dst=dst, bk_=bk_: e.tensor_copy(
            dst, bk_[:, 0:NCH * 128].rearrange("p (c i) -> p c i", i=128)),
            reads=[bkk], writes=["woT"])

    def att_sc(t, hh, pi):
        sp = qps[pi]
        spk = ("ps_s", pi)
        k0 = t * 128
        for u in range(2):
            p.add("tensor", lambda e, u=u, k0=k0, t=t, hh=hh, sp=sp: e.matmul(
                sp[:, 0:512],
                qT[:, (hh * 2 + u) * T_OWN + t * 128:(hh * 2 + u) * T_OWN + (t + 1) * 128],
                kT[:, u * T_KV + k0: u * T_KV + k0 + 512],
                start=(u == 0), stop=False),
                reads=["qT", "kT"], writes=[spk])
            p.add("tensor", lambda e, u=u, k0=k0, t=t, hh=hh, sp=sp: e.matmul(
                sp[:, 512:640],
                qT[:, (hh * 2 + u) * T_OWN + t * 128:(hh * 2 + u) * T_OWN + (t + 1) * 128],
                kT[:, u * T_KV + k0 + 512: u * T_KV + k0 + 640],
                start=(u == 0), stop=False),
                reads=["qT", "kT"], writes=[spk])
        p.add("tensor", lambda e, k0=k0, sp=sp: e.matmul(
            sp[:, 0:512], ones1b[:], kbias_b[:, k0:k0 + 512], start=False, stop=True),
            reads=["ones1b", "kbias_b"], writes=[spk])
        p.add("tensor", lambda e, k0=k0, sp=sp: e.matmul(
            sp[:, 512:640], ones1b[:], kbias_b[:, k0 + 512:k0 + 640],
            start=False, stop=True),
            reads=["ones1b", "kbias_b"], writes=[spk])

    def att_sm(t, hh, pi, pi4):
        sp = qps[pi]
        spk = ("ps_s", pi)
        pf = probs_f[:, pi * 640:(pi + 1) * 640]
        pn = probsN[:, pi4 * 640:(pi4 + 1) * 640]
        rmx = rowmax[:, pi:pi + 1]
        nmx = negmax[:, pi:pi + 1]
        dnm_ = denom[:, pi:pi + 1]
        rdn = rden[:, pi:pi + 1]
        p.add("vector", lambda e, sp=sp: e.tensor_add(sp[:, 0:128], sp[:, 0:128], umask[:]),
              reads=[spk, "umask"], writes=[spk])
        p.add("vector", lambda e, sp=sp: e.tensor_add(sp[:, 512:640], sp[:, 512:640], lmask[:]),
              reads=[spk, "lmask"], writes=[spk])
        p.add("vector", lambda e, sp=sp, rmx=rmx: e.reduce_max(rmx, sp[:], axis=AX.X),
              reads=[spk], writes=[("rowmax", pi)])
        p.add("vector", lambda e, rmx=rmx, nmx=nmx: e.tensor_scalar_mul(nmx, rmx, -0.0625),
              reads=[("rowmax", pi)], writes=[("negmax", pi)])
        p.add("scalar", lambda e, sp=sp, pf=pf, nmx=nmx: e.activation(
            pf, sp[:], AF.Exp, scale=0.0625, bias=nmx),
            reads=[spk, ("negmax", pi)], writes=[("probs_f", pi)])

    def att_sm2(t, hh, pi, pi4):
        pf = probs_f[:, pi * 640:(pi + 1) * 640]
        pn = probsN[:, pi4 * 640:(pi4 + 1) * 640]
        dnm_ = denom[:, pi:pi + 1]
        rdn = rden[:, pi:pi + 1]
        p.add("vector", lambda e, pf=pf, dnm_=dnm_: e.reduce_sum(dnm_, pf, axis=AX.X),
              reads=[("probs_f", pi)], writes=[("denom", pi)])
        p.add("vector", lambda e, dnm_=dnm_, rdn=rdn: e.reciprocal(rdn, dnm_),
              reads=[("denom", pi)], writes=[("rden", pi)])
        p.add("vector", lambda e, pf=pf, pn=pn, rdn=rdn: e.tensor_scalar(
            pn, pf, rdn, None, ALU.mult),
            reads=[("probs_f", pi), ("rden", pi)], writes=[("probsN", pi4)])

    def att_trav(t, hh, pi, pi4):
        pn = probsN[:, pi4 * 640:(pi4 + 1) * 640]
        ptt = probsTT[:, pi * 640:(pi + 1) * 640]
        avt = ps_av if pi == 0 else ps_x
        avk = "ps_av" if pi == 0 else "ps_x"
        bk_, bkk = (ptrb, "ps_trb") if pi == 0 else (ptrb2, "ps_trb2")
        for kt in range(5):
            p.add("tensor", lambda e, kt=kt, pn=pn, bk_=bk_: e.transpose(
                bk_[:, kt * 128:(kt + 1) * 128], pn[:, kt * 128:(kt + 1) * 128], id_b[:]),
                reads=[("probsN", pi4), "id_b"], writes=[bkk])
        p.add("scalar", lambda e, ptt=ptt, bk_=bk_: e.copy(ptt[:, 0:640], bk_[:, 0:640]),
              reads=[bkk], writes=[("probsTT", pi)])
        for u in range(2):
            for kt in range(5):
                p.add("tensor", lambda e, u=u, kt=kt, t=t, ptt=ptt, avt=avt: e.matmul(
                    avt[:, u * 128:(u + 1) * 128],
                    vsb[:, (t + kt) * HD + u * 128: (t + kt) * HD + (u + 1) * 128],
                    ptt[:, kt * 128:(kt + 1) * 128],
                    start=(u == 0 and kt == 0), stop=(u == 1 and kt == 4)),
                    reads=["vsb", ("probsTT", pi)], writes=[avk])
        for u in range(2):
            p.add("scalar", lambda e, t=t, hh=hh, u=u, avt=avt: e.copy(
                aoT[:, (hh * 2 + u) * T_OWN + t * 128:
                    (hh * 2 + u) * T_OWN + (t + 1) * 128],
                avt[:, u * 128:(u + 1) * 128]),
                reads=[avk], writes=["aoT"])

    aunits = [(t, hh) for t in range(NT) for hh in range(NH)]
    pend_a = []
    for i, (t, hh) in enumerate(aunits):
        pi = i % 2
        pi4 = i % 4
        att_sc(t, hh, pi)
        if i == 0:
            wo_dma(0)
            wo_dma(1)
        elif i <= NCH:
            wo_tr(i - 1)
            if i + 1 <= NCH - 1:
                wo_dma(i + 1)
        att_sm(t, hh, pi, pi4)
        if pend_a:
            att_sm2(*pend_a[-1])
        if len(pend_a) >= 3:
            att_trav(*pend_a.pop(0))
        pend_a.append((t, hh, pi, pi4))
    att_sm2(*pend_a[-1])
    for pa in pend_a:
        att_trav(*pa)

    def ph3_wo(t, pr):
        for b_ in range(2):
            for cc in range(NCH):
                p.add("tensor", lambda e, b_=b_, cc=cc, t=t: e.matmul(
                    qps[b_][:, 0:512], aoT[:, cc * T_OWN + t * 128: cc * T_OWN + (t + 1) * 128],
                    woT[:, cc * C + b_ * 512: cc * C + b_ * 512 + 512],
                    start=(cc == 0), stop=(cc == NCH - 1)),
                    reads=["aoT", "woT"], writes=[("ps_s", b_)])

    def ph3_ln(t, pr):
        o = pr * C
        xrv = xres[:, o:o + C]
        lnv = lnr[:, o:o + C]
        lsq = lnsq[:, 0:C]
        xpv = xpb[:, o:o + C]
        ls_ = lsum[:, pr:pr + 1]
        lm_ = lmean[:, pr:pr + 1]
        lv_ = lvar[:, pr:pr + 1]
        l2_ = lm2[:, pr:pr + 1]
        lr_ = lrstd[:, pr:pr + 1]
        p.dma("sync", xrv, xin[HALO + t * 128: HALO + (t + 1) * 128, :], f"xr{pr}",
              writes=[("xres", pr)])
        ls2 = lm2[:, pr:pr + 1]
        for b_ in range(2):
            acc = ls_ if b_ == 0 else ls2
            p.add("vector", lambda e, b_=b_, xrv=xrv, lnv=lnv, acc=acc: e.scalar_tensor_tensor(
                lnv[:, b_ * 512:(b_ + 1) * 512], xrv[:, b_ * 512:(b_ + 1) * 512], 1.0,
                qps[b_][:, 0:512], ALU.mult, ALU.add, accum_out=acc),
                reads=[("xres", pr), ("ps_s", b_)], writes=[("lnr", pr), ("lsum", pr)])
        p.add("vector", lambda e, ls_=ls_, ls2=ls2: e.tensor_add(ls_, ls_, ls2),
              reads=[("lsum", pr)], writes=[("lsum", pr)])
        p.add("vector", lambda e, ls_=ls_, lm_=lm_: e.tensor_scalar_mul(lm_, ls_, 1.0 / C),
              reads=[("lsum", pr)], writes=[("lmean", pr)])
        p.add("scalar", lambda e, lnv=lnv, lsq=lsq, lv_=lv_: e.activation(
            lsq, lnv, AF.Square, accum_out=lv_),
            reads=[("lnr", pr)], writes=[("lnsq", pr), ("lvar", pr)])
        p.add("vector", lambda e, lm_=lm_, l2_=l2_: e.tensor_mul(l2_, lm_, lm_),
              reads=[("lmean", pr)], writes=[("lm2", pr)])
        p.add("vector", lambda e, lv_=lv_: e.tensor_scalar(lv_, lv_, 1.0 / C, None, ALU.mult),
              reads=[("lvar", pr)], writes=[("lvar", pr)])
        p.add("vector", lambda e, lv_=lv_, l2_=l2_: e.tensor_sub(lv_, lv_, l2_),
              reads=[("lvar", pr), ("lm2", pr)], writes=[("lvar", pr)])
        p.add("scalar", lambda e, lv_=lv_, lr_=lr_: e.activation(lr_, lv_, AF.Sqrt, bias=eps5[:]),
              reads=[("lvar", pr), "eps5"], writes=[("lrstd", pr)])
        p.add("vector", lambda e, lr_=lr_: e.reciprocal(lr_, lr_),
              reads=[("lrstd", pr)], writes=[("lrstd", pr)])
        p.add("vector", lambda e, lnv=lnv, xpv=xpv, lm_=lm_, lr_=lr_: e.tensor_scalar(
            xpv, lnv, lm_, lr_, ALU.subtract, ALU.mult),
            reads=[("lnr", pr), ("lmean", pr), ("lrstd", pr)], writes=[("xpb", pr)])
        p.dma("sync", xprime[t * 128:(t + 1) * 128, :], xpv, "xpst",
              reads=[("xpb", pr)], writes=["xprime_dram"])
        def xs_store(e, t=t, xpv=xpv):
            return e.dma_start(out=xs[ds(hof[1] + t * 128, 128), :], in_=xpv)
        p.add("gpsimd", xs_store, reads=[("xpb", pr)], writes=["xs_dram"], dma=f"xsst{pr}")

    def ph3_rt(t, pr):
        o = pr * C
        xpv = xpb[:, o:o + C]
        xTs = xpTs[:, 0:C]
        bk_, bkk = (ptrb, "ps_trb") if pr == 0 else (ptrb2, "ps_trb2")
        for cc in range(NCH):
            p.add("tensor", lambda e, cc=cc, xpv=xpv, bk_=bk_: e.transpose(
                bk_[:, cc * 128:(cc + 1) * 128], xpv[:, cc * 128:(cc + 1) * 128], id_b[:]),
                reads=[("xpb", pr), "id_b"], writes=[bkk])
        p.add("scalar", lambda e, xTs=xTs, bk_=bk_: e.copy(xTs[:, 0:C], bk_[:]),
              reads=[bkk], writes=[("xpTs", pr)])
        for cc in range(NCH):
            p.add("tensor", lambda e, cc=cc, xTs=xTs: e.matmul(
                lgps, xTs[:, cc * 128:(cc + 1) * 128], wg_sb[:, cc * E:(cc + 1) * E],
                start=(cc == 0), stop=(cc == NCH - 1)),
                reads=[("xpTs", pr), "wg_sb"], writes=["ps_av"])
        lgv = lgall[:, t * E:(t + 1) * E]
        p.add("vector", lambda e, lgv=lgv: e.tensor_copy(lgv, lgps), reads=["ps_av"],
              writes=[("lgall", t)])
        p.add("vector", lambda e, lgv=lgv: e.reduce_max(m8[:], lgv, axis=AX.X),
              reads=[("lgall", t)], writes=["m8"])
        p.add("vector", lambda e, lgv=lgv: e.tensor_scalar(pr_t[:], lgv, m8[:], None,
                                                           ALU.is_ge),
              reads=[("lgall", t), "m8"], writes=["pr_t"])
        p.add("vector", lambda e: e.tensor_tensor(pr_t[:], pr_t[:], iotadesc[:, 0:E], ALU.mult),
              reads=["pr_t", "iotadesc"], writes=["pr_t"])
        p.add("vector", lambda e: e.reduce_max(pmax[:], pr_t[:], axis=AX.X),
              reads=["pr_t"], writes=["pmax"])
        p.add("vector", lambda e, t=t: e.tensor_scalar(
            mask_tm[:, t * E:(t + 1) * E], pr_t[:], pmax[:], None, ALU.is_ge),
            reads=["pr_t", "pmax"], writes=["mask_tm"])
        p.add("tensor", lambda e, t=t, bk_=bk_: e.transpose(
            bk_[0:8, 0:128], mask_tm[:, t * E:(t + 1) * E], id_b[:]),
            reads=["mask_tm", "id_b"], writes=[bkk])
        p.add("vector", lambda e, bk_=bk_: e.tensor_copy(mTst[:], bk_[0:8, 0:128]),
              reads=[bkk], writes=["mTst"])
        def ms_store(e, t=t):
            return e.dma_start(out=msh[:, ds(hof[1] + t * 128, 128)], in_=mTst[:])
        p.add("gpsimd", ms_store, reads=["mTst"], writes=["ms_dram"], dma="msst")

    prev3 = None
    for t in range(NT):
        pr = t % 2
        ph3_wo(t, pr)
        if prev3 is not None:
            ph3_rt(*prev3)
        ph3_ln(t, pr)
        prev3 = (t, pr)
    ph3_rt(*prev3)

    p.add("sync", lambda e: e.nop(),
          reads=["xs_dram", "ms_dram", "xprime_dram"])

    p.barrier()

    attn_outer.close()
    rtB = ExitStack()
    sb_ = rtB.enter_context
    iota512 = sb_(nc.sbuf_tensor("iota512", [128, 512], F32))
    cumtm = sb_(nc.sbuf_tensor("cumtm", [128, 32 * E], F32))
    mycum = sb_(nc.sbuf_tensor("mycum", [128, 32], F32))
    cmpb = sb_(nc.sbuf_tensor("cmpb", [128, 2 * 512], BF16))
    cntf = sb_(nc.sbuf_tensor("cntf", [1, 512], F32))
    nslot = sb_(nc.sbuf_tensor("nslot", [128, 1], F32))
    tmp256 = sb_(nc.sbuf_tensor("tmp256", [128, 256], F32))
    rt_scope = ExitStack()
    sr_ = rt_scope.enter_context
    ms_sb = sr_(nc.sbuf_tensor("ms_sb", [8, T], BF16))
    ones8 = sr_(nc.sbuf_tensor("ones8", [8, T], BF16))
    cums8 = sr_(nc.sbuf_tensor("cums8", [8, T], F32))
    lgm = sr_(nc.sbuf_tensor("lgm", [128, NT], F32))
    lgex = sr_(nc.sbuf_tensor("lgex", [128, NT * E], F32))
    p_t = sr_(nc.sbuf_tensor("p_t", [128, NT], F32))
    keep_t = sr_(nc.sbuf_tensor("keep_t", [128, NT], F32))
    en_t = sr_(nc.sbuf_tensor("en_t", [128, NT], F32))
    gidxf = sr_(nc.sbuf_tensor("gidxf", [128, NT], F32))
    pB = sr_(nc.sbuf_tensor("pB", [128, NT], F32))
    tmp128 = sr_(nc.sbuf_tensor("tmp128", [128, 128], F32))


    p.dma("sync", ms_sb[:], msh, "msld", writes=["ms_sb"])
    p.add("vector", lambda e: e.reduce_max(
        lgm[:], lgall[:].rearrange("p (t e) -> p t e", e=E), axis=AX.X),
        reads=[("lgall", t) for t in range(NT)] if False else ["lgall_all"], writes=["lgm"])
    p.add("vector", lambda e: e.tensor_tensor(
        lgex[:].rearrange("p (t e) -> p t e", e=E),
        lgall[:].rearrange("p (t e) -> p t e", e=E),
        lgm[:].unsqueeze(2).broadcast_to([128, NT, E]), ALU.subtract),
        reads=["lgall_all", "lgm"], writes=["lgex"])
    p.add("scalar", lambda e: e.activation(lgex[:], lgex[:], AF.Exp),
          reads=["lgex"], writes=["lgex"])
    p.add("vector", lambda e: e.reduce_sum(
        gate_t[:], lgex[:].rearrange("p (t e) -> p t e", e=E), axis=AX.X),
        reads=["lgex"], writes=["gate_t"])
    p.add("vector", lambda e: e.reciprocal(gate_t[:], gate_t[:]),
          reads=["gate_t"], writes=["gate_t"])
    p.dma("sync", iota512[:], i512_in, "l_i512", writes=["iota512"])
    p.add("vector", lambda e: e.memset(ones8[:], 1.0), writes=["ones8"])
    p.add("vector", lambda e: e.tensor_tensor_scan(cums8[:], ones8[:], ms_sb[:], 0.0,
                                                   ALU.mult, ALU.add),
          reads=["ones8", "ms_sb"], writes=["cums8"])
    for ch in range(32):
        cb, cbk = (ps_av, "ps_av") if ch % 2 == 0 else (ps_x, "ps_x")
        p.add("tensor", lambda e, ch=ch, cb=cb: e.transpose(
            cb[:, 0:8], cums8[:, ch * 128:(ch + 1) * 128], id_f[0:8, 0:8]),
            reads=["cums8", "id_f"], writes=[cbk])
        p.add("vector", lambda e, ch=ch, cb=cb: e.tensor_copy(
            cumtm[:, ch * E:(ch + 1) * E], cb[:, 0:8]),
            reads=[cbk], writes=["cumtm"])

    m3 = lambda ap: ap.rearrange("p (t e) -> p t e", e=E)
    p.add("vector", lambda e: e.tensor_tensor(tmp128[:], cumtm[:, 0:128], mask_tm[:], ALU.mult),
          reads=["cumtm", "mask_tm"], writes=["tmp128"])
    p.add("vector", lambda e: e.reduce_sum(p_t[:], m3(tmp128[:]), axis=AX.X),
          reads=["tmp128"], writes=["p_t"])
    p.add("vector", lambda e: e.tensor_tensor(tmp128[:], cumtm[:, 128:256], mask_tm[:], ALU.mult),
          reads=["cumtm", "mask_tm"], writes=["tmp128"])
    p.add("vector", lambda e: e.reduce_sum(pB[:], m3(tmp128[:]), axis=AX.X),
          reads=["tmp128"], writes=["pB"])
    p.add("vector", lambda e: e.tensor_scalar(p_t[:], p_t[:], hsel[:, 0:1], None, ALU.mult),
          reads=["p_t", "hsel"], writes=["p_t"])
    p.add("vector", lambda e: e.scalar_tensor_tensor(p_t[:], pB[:], hsel[:, 1:2], p_t[:],
                                                     ALU.mult, ALU.add),
          reads=["pB", "hsel", "p_t"], writes=["p_t"])
    p.add("vector", lambda e: e.tensor_tensor(tmp128[:], iotadesc[:], mask_tm[:], ALU.mult),
          reads=["iotadesc", "mask_tm"], writes=["tmp128"])
    p.add("vector", lambda e: e.reduce_sum(en_t[:], m3(tmp128[:]), axis=AX.X),
          reads=["tmp128"], writes=["en_t"])
    p.add("vector", lambda e: e.tensor_scalar_mul(en_t[:], en_t[:], -1.0),
          reads=["en_t"], writes=["en_t"])
    p.add("vector", lambda e: e.tensor_scalar_add(p_t[:], p_t[:], -1.0),
          reads=["p_t"], writes=["p_t"])
    p.add("vector", lambda e: e.tensor_scalar(keep_t[:], p_t[:], -1.0, 511.5,
                                              ALU.mult, ALU.add),
          reads=["p_t"], writes=["keep_t"])
    p.add("vector", lambda e: e.tensor_scalar(keep_t[:], keep_t[:], 0.0, None, ALU.is_ge),
          reads=["keep_t"], writes=["keep_t"])
    p.add("vector", lambda e: e.tensor_tensor(scale_t[:], gate_t[:], keep_t[:], ALU.mult),
          reads=["gate_t", "keep_t"], writes=["scale_t"])
    p.add("vector", lambda e: e.tensor_scalar(gidxf[:], en_t[:], 512.0, 4096.0,
                                              ALU.mult, ALU.add),
          reads=["en_t"], writes=["gidxf"])
    p.add("vector", lambda e: e.tensor_add(gidxf[:], gidxf[:], p_t[:]),
          reads=["gidxf", "p_t"], writes=["gidxf"])
    p.add("vector", lambda e: e.tensor_copy(gidx[:], gidxf[:]),
          reads=["gidxf"], writes=["gidx"])

    rt_scope.close()
    ffn_scope = ExitStack()
    sf_ = ffn_scope.enter_context
    w1sb = sf_(nc.sbuf_tensor("w1sb", [128, 2 * NCH * 2048], BF16))
    w2sb = sf_(nc.sbuf_tensor("w2sb", [128, 32 * C], BF16))
    hT = sf_(nc.sbuf_tensor("hT", [128, 32 * 512], BF16))
    slin = sf_(nc.sbuf_tensor("slin", [128, 4 * C], BF16))
    slinT = sf_(nc.sbuf_tensor("slinT", [128, 2 * NCH * 512], BF16))
    orow = sf_(nc.sbuf_tensor("orow", [128, C], BF16))
    for eng in ("vector", "scalar", "gpsimd", "sync", "tensor"):
        p.fence(eng, ["ms_sb", "ones8", "cums8", "lgm", "lgex",
                      ("wstg", 0), ("wstg", 1)])

    def dispatch_ring(j, q):
        sv = slin[:, q * C:(q + 1) * C]
        p.add("gpsimd", lambda e, j=j, q=q, sv=sv: e.indirect_dma_start(
            out=sv, out_offset=None, in_=xs,
            in_offset=IndirectOffsetOnAxis(didx[:, j * 4 + q: j * 4 + q + 1], 0),
            bounds_check=T - 1, oob_is_err=False),
            reads=["didx", "xs_dram"], writes=[("slin", q)],
            dma=f"disp{q}")

    def dispatch_tr(j, q):
        sv = slin[:, q * C:(q + 1) * C]
        db, dbk = (ptrb, "ps_trb") if q % 2 == 0 else (ptrb2, "ps_trb2")
        for cc in range(NCH):
            p.add("tensor", lambda e, cc=cc, db=db, sv=sv: e.transpose(
                db[:, cc * 128:(cc + 1) * 128],
                sv[:, cc * 128: (cc + 1) * 128], id_b[:]),
                reads=[("slin", q), "id_b"], writes=[dbk])
        dst = slinT[:, (j % 2) * NCH * 512:].rearrange(
            "p (c i) -> p c i", i=512)[:, 0:NCH, q * 128:(q + 1) * 128]
        p.add("scalar", lambda e, dst=dst, db=db: e.copy(
            dst, db[:].rearrange("p (c i) -> p c i", i=128)),
            reads=[dbk], writes=[("slinT", j % 2)])

    def dispatch_counts(j):
        selj = sel4[:, j * E:(j + 1) * E].unsqueeze(1).broadcast_to([128, 32, E])
        p.add("vector", lambda e, selj=selj: e.tensor_tensor(
            tmp256[:].rearrange("p (c e) -> p c e", e=E),
            cumtm[:].rearrange("p (c e) -> p c e", e=E), selj, ALU.mult),
            reads=["cumtm", "sel4"], writes=["tmp256"])
        p.add("vector", lambda e: e.reduce_sum(
            mycum[:], tmp256[:].rearrange("p (c e) -> p c e", e=E), axis=AX.X),
            reads=["tmp256"], writes=["mycum"])
        bk = j % 2
        for ch in range(32):
            csl = ch % 2
            p.add("vector", lambda e, ch=ch, csl=csl: e.tensor_scalar(
                cmpb[:, csl * 512:(csl + 1) * 512], iota512[:], mycum[:, ch:ch + 1],
                None, ALU.is_ge),
                reads=["iota512", "mycum"], writes=[("cmpb", csl)])
            p.add("tensor", lambda e, ch=ch, bk=bk, csl=csl: e.matmul(
                qps[bk][0:1, 0:512], ones128b[:], cmpb[:, csl * 512:(csl + 1) * 512],
                start=(ch == 0), stop=(ch == 31)),
                reads=["ones128b", ("cmpb", csl)], writes=[("ps_s", bk)])
        p.add("vector", lambda e, bk=bk: e.tensor_copy(cntf[:], qps[bk][0:1, 0:512]),
              reads=[("ps_s", bk)], writes=["cntf"])
        for q in range(4):
            p.add("tensor", lambda e, q=q, bk=bk: e.transpose(
                qps[bk][:, 520:521], cntf[0:1, q * 128:(q + 1) * 128], id_f[0:1, 0:1]),
                reads=["cntf", "id_f"], writes=[("ps_s", bk)])
            p.add("vector", lambda e, bk=bk: e.tensor_copy(nslot[:], qps[bk][:, 520:521]),
                  reads=[("ps_s", bk)], writes=["nslot"])
            p.add("vector", lambda e, j=j, q=q: e.tensor_copy(
                didx[:, j * 4 + q: j * 4 + q + 1], nslot[:]),
                reads=["nslot"], writes=["didx"])

    def dispatch(j):
        dispatch_counts(j)
        for q in range(4):
            dispatch_ring(j, q)
            dispatch_tr(j, q)

    if debug:
        p.dma("scalar", dbg_ms, ms_sb[:], "dbg", reads=["ms_sb"])
        p.add("vector", lambda e: e.tensor_copy(nslot[:], gidx[:, 0:1]),
              reads=["gidx"], writes=["nslot"])
        def dbg_rt_store(e):
            return e.dma_start(out=dbg_rt, in_=cumtm[:, 0:64])
        p.add("scalar", dbg_rt_store, reads=["cumtm"], dma="dbg")
        p.add("vector", lambda e: e.tensor_copy(gidxf[:], gidxf[:]), reads=["gidxf"])
        def dbg_di_store(e):
            return e.dma_start(out=dbg_di.bitcast(U32) if hasattr(dbg_di, "bitcast") else dbg_di,
                               in_=didx[:])
        p.add("scalar", dbg_di_store, reads=["didx"], dma="dbg")

    hps = [qps[0][:, 0:512], qps[1][:, 0:512]]

    def w1_load(j, hh):
        sl = (2 * j + hh) % 2
        for qq in range(4):
            p.add("gpsimd", lambda e, j=j, hh=hh, sl=sl, qq=qq: e.dma_start(
                out=w1sb[:, sl * 16384:(sl + 1) * 16384].rearrange(
                    "p (c h) -> p c h", h=2048)[:, :, qq * 512:(qq + 1) * 512],
                in_=w1[j, :, hh * 2048 + qq * 512: hh * 2048 + (qq + 1) * 512].rearrange(
                    "(c p) h -> p c h", p=128)),
                writes=[("w1sb", sl, qq)], dma=f"w1l{sl}q{qq}")

    def w2_load(j, qt):
        sl = (4 * j + qt) % 4
        p.add("gpsimd", lambda e, j=j, qt=qt, sl=sl: e.dma_start(
            out=w2sb[:, sl * 8192:(sl + 1) * 8192].rearrange("p (m c) -> p m c", c=C),
            in_=w2[j, qt * 1024:(qt + 1) * 1024, :].rearrange("(m p) c -> p m c", p=128)),
            writes=[("w2sb", sl)], dma=f"w2l{sl}")

    p.add("vector", lambda e: e.memset(slin[:], 0.0),
          writes=[("slin", q_) for q_ in range(4)])
    dispatch(0)
    for hh in range(2):
        w1_load(0, hh)
    for qt in range(4):
        w2_load(0, qt)

    for j in range(E_LOC):
        for hhalf in range(2):
            sl = (2 * j + hhalf) % 2
            for m in range(16):
                mg = hhalf * 16 + m
                bk = m % 2
                for cc in range(NCH):
                    p.add("tensor", lambda e, m=m, cc=cc, bk=bk, sl=sl, j=j: e.matmul(
                        hps[bk],
                        w1sb[:, sl * 16384 + cc * 2048 + m * 128:
                             sl * 16384 + cc * 2048 + (m + 1) * 128],
                        slinT[:, ((j % 2) * NCH + cc) * 512:((j % 2) * NCH + cc + 1) * 512],
                        start=(cc == 0), stop=(cc == NCH - 1)),
                        reads=[("w1sb", sl, m // 4), ("slinT", j % 2)],
                        writes=[("ps_s", bk)])
                p.add("scalar", lambda e, mg=mg, bk=bk: e.activation(
                    hT[:, mg * 512:(mg + 1) * 512], hps[bk], AF.Gelu_apprx_tanh),
                    reads=[("ps_s", bk)], writes=["hT"])
        if j + 1 < E_LOC:
            dispatch_counts(j + 1)
            for q in range(4):
                dispatch_ring(j + 1, q)
            w1_load(j + 1, 0)
            w1_load(j + 1, 1)
        op2 = [ps_x[:], ps_av[:]]
        op2k = ["ps_x", "ps_av"]
        for q in range(4):
            for half in range(2):
                for m in range(32):
                    sl = (4 * j + m // 8) % 4
                    p.add("tensor", lambda e, q=q, half=half, m=m, sl=sl: e.matmul(
                        op2[half], hT[:, m * 512 + q * 128: m * 512 + (q + 1) * 128],
                        w2sb[:, sl * 8192 + (m % 8) * C + half * 512:
                             sl * 8192 + (m % 8) * C + half * 512 + 512],
                        start=(m == 0), stop=(m == 31)),
                        reads=["hT", ("w2sb", sl)], writes=[op2k[half]])
            for half in range(2):
                p.add("vector", lambda e, half=half: e.tensor_copy(
                    orow[:, half * 512:(half + 1) * 512], op2[half]),
                    reads=[op2k[half]], writes=["orow"])
            def so_store(e, j=j, q=q):
                return e.dma_start(out=so[ds(hof[0] + j * 512 + q * 128, 128), :],
                                   in_=orow[:])
            p.add("scalar", so_store, reads=["orow"], writes=["so_dram"], dma="sost")
            if j + 1 < E_LOC:
                dispatch_tr(j + 1, q)

        if j + 1 < E_LOC:
            for qt in range(4):
                w2_load(j + 1, qt)

    if debug:
        for r in range(16):
            p.dma("scalar", dbg_si[r * 128:(r + 1) * 128, :],
                  hT[:, 0:C] if False else slinT[:, 0:512], "dbg", reads=["slinT"])
            break
    p.add("sync", lambda e: e.nop(), reads=["so_dram"])

    p.barrier()

    ffn_scope.close()
    lnsq2 = sbp("lnsq2", [128, C], F32)
    compb = sbp("compb", [128, 16 * C], BF16)
    xres2 = sbp("xres2", [128, 16 * C], BF16)
    yrow = sbp("yrow", [128, 2 * C], F32)
    for eng in ("vector", "scalar", "gpsimd", "sync"):
        p.fence(eng, ["w1sb", "w2sb", "hT"] + [("slin", q_) for q_ in range(4)])
    p.add("vector", lambda e: e.memset(compb[:], 0.0),
          writes=[("compb", t_) for t_ in range(NT)])
    for t in range(NT):
        cpv = compb[:, t * C:(t + 1) * C]
        xrv = xres2[:, t * C:(t + 1) * C]
        p.add("gpsimd", lambda e, t=t, cpv=cpv: e.indirect_dma_start(
            out=cpv, out_offset=None, in_=so,
            in_offset=IndirectOffsetOnAxis(gidx[:, t:t + 1], 0),
            bounds_check=T - 1, oob_is_err=False),
            reads=["gidx", "so_dram"], writes=[("compb", t)],
            dma=f"comb{t}")
        p.dma("sync", xrv, xprime[t * 128:(t + 1) * 128, :], f"xpl{t}",
              reads=["xprime_dram"], writes=[("xres2", t)])
    for t in range(NT):
        pr = t % 2
        cpv = compb[:, t * C:(t + 1) * C]
        xrv = xres2[:, t * C:(t + 1) * C]
        yv = yrow[:, pr * C:(pr + 1) * C]
        ls_ = lsum[:, pr:pr + 1]
        lm_ = lmean[:, pr:pr + 1]
        lv_ = lvar[:, pr:pr + 1]
        l2_ = lm2[:, pr:pr + 1]
        lr_ = lrstd[:, pr:pr + 1]
        p.add("vector", lambda e, t=t, cpv=cpv, xrv=xrv, yv=yv, ls_=ls_: e.scalar_tensor_tensor(
            yv, cpv, scale_t[:, t:t + 1], xrv, ALU.mult, ALU.add,
            accum_out=ls_),
            reads=[("compb", t), "scale_t", ("xres2", t)],
            writes=[("yrow", pr), ("lsum", pr)])
        p.add("vector", lambda e, ls_=ls_, lm_=lm_: e.tensor_scalar_mul(lm_, ls_, 1.0 / C),
              reads=[("lsum", pr)], writes=[("lmean", pr)])
        p.add("scalar", lambda e, yv=yv, lv_=lv_: e.activation(lnsq2[:], yv, AF.Square,
                                                               accum_out=lv_),
              reads=[("yrow", pr)], writes=["lnsq2", ("lvar", pr)])
        p.add("vector", lambda e, lm_=lm_, l2_=l2_: e.tensor_mul(l2_, lm_, lm_),
              reads=[("lmean", pr)], writes=[("lm2", pr)])
        p.add("vector", lambda e, lv_=lv_: e.tensor_scalar(lv_, lv_, 1.0 / C,
                                                           None, ALU.mult),
              reads=[("lvar", pr)], writes=[("lvar", pr)])
        p.add("vector", lambda e, lv_=lv_, l2_=l2_: e.tensor_sub(lv_, lv_, l2_),
              reads=[("lvar", pr), ("lm2", pr)], writes=[("lvar", pr)])
        p.add("scalar", lambda e, lv_=lv_, lr_=lr_: e.activation(lr_, lv_, AF.Sqrt,
                                                                 bias=eps5[:]),
              reads=[("lvar", pr), "eps5"], writes=[("lrstd", pr)])
        p.add("vector", lambda e, lr_=lr_: e.reciprocal(lr_, lr_),
              reads=[("lrstd", pr)], writes=[("lrstd", pr)])
        p.add("vector", lambda e, yv=yv, lm_=lm_, lr_=lr_: e.tensor_scalar(
            yv, yv, lm_, lr_, ALU.subtract, ALU.mult),
            reads=[("yrow", pr), ("lmean", pr), ("lrstd", pr)], writes=[("yrow", pr)])
        p.dma("sync", yout[t * 128:(t + 1) * 128, :], yv, f"yst{pr}", reads=[("yrow", pr)])

    p.emit()
    p.final_waits("sync", ["yst0", "yst1"] + (["dbg"] if debug else []))
    psA.close()
    stack.close()
    nc.finalize()
    return nc


def _perm(n):
    out = []
    for h0 in range(0, n, HD):
        out.extend(range(h0, h0 + HD, 2))
        out.extend(range(h0 + 1, h0 + HD, 2))
    return np.array(out)


def make_in_maps(inputs):
    x = np.asarray(inputs["L0_attn_x"])
    Wq = np.asarray(inputs["L0_attn_Wq"])
    Wk = np.asarray(inputs["L0_attn_Wk"])
    Wv = np.asarray(inputs["L0_attn_Wv"])
    Wo = np.asarray(inputs["L0_attn_Wo"])
    qg = np.asarray(inputs["L0_attn_q_gamma"])
    kg = np.asarray(inputs["L0_attn_k_gamma"])
    Wg = np.asarray(inputs["L0_ffn_Wg"])
    W1 = np.asarray(inputs["L0_ffn_W1"])
    W2 = np.asarray(inputs["L0_ffn_W2"])

    pq = _perm(C)
    pk = _perm(HD)
    Wq_p = np.ascontiguousarray(Wq[pq, :])
    Wk_p = np.ascontiguousarray(Wk[pk, :])
    gq_t = np.tile(qg[pk][None, :], (128, NH)).astype(np.float32)
    gk_t = np.tile(kg[pk][None, :], (128, 1)).astype(np.float32)
    inv = (1.0 / (1.0e6 ** (np.arange(0, HD, 2, dtype=np.float32) / HD))).astype(np.float32)
    ident = np.eye(128, dtype=np.float32)

    ins = []
    for c in range(8):
        b, h = c >> 1, c & 1
        lo = h * T_OWN - HALO
        xs_ = np.zeros((T_KV, C), dtype=np.float32)
        s0 = max(0, lo)
        xs_[s0 - lo:, :] = x[b, s0: lo + T_KV, :]
        pos = (lo + np.arange(T_KV)).astype(np.float32)
        freqs = np.outer(pos, inv).astype(np.float32)
        kb = np.zeros((1, T_KV), dtype=np.float32)
        if h == 0:
            kb[0, :HALO] = NEG
        sel = np.zeros((128, E_LOC * E), dtype=np.float32)
        for j in range(E_LOC):
            sel[:, j * E + (E_LOC * h + j)] = 1.0
        hs = np.zeros((128, 2), dtype=np.float32)
        hs[:, h] = 1.0
        ins.append({
            "x": xs_,
            "wq": Wq_p, "wk": Wk_p, "wv": np.ascontiguousarray(Wv),
            "wo": np.ascontiguousarray(Wo), "wg": np.ascontiguousarray(Wg),
            "w1": np.ascontiguousarray(W1[E_LOC * h: E_LOC * h + E_LOC]),
            "w2": np.ascontiguousarray(W2[E_LOC * h: E_LOC * h + E_LOC]),
            "gq": gq_t, "gk": gk_t,
            "cosb": np.cos(freqs).astype(np.float32),
            "sinb": np.sin(freqs).astype(np.float32),
            "kbias": kb, "sel4": sel, "hsel": hs, "idf": ident,
            "i512": np.tile(np.arange(512, dtype=np.float32)[None, :], (128, 1)),
        })
    return ins


def kernel(**inputs):
    ins = make_in_maps(inputs)
    nc = build()
    res = run_bass_kernel_spmd(nc, ins, core_ids=list(range(8)))
    y = np.stack([res.results[c]["y"] for c in range(8)]).reshape(B, T, C)
    return y
